# version 22
# baseline (speedup 1.0000x reference)
import math
import numpy as np
import ml_dtypes
from contextlib import ExitStack

import concourse.bass as bass
import concourse.mybir as mybir
from concourse.bass_utils import run_bass_kernel_spmd
from concourse.bass import IndirectOffsetOnAxis

F32 = mybir.dt.float32
BF16 = mybir.dt.bfloat16
I32 = mybir.dt.int32
AF = mybir.ActivationFunctionType
ALU = mybir.AluOpType
AX = mybir.AxisListType.X

NCORE = 8
S, D, H = 8192, 2048, 8192
T = S // NCORE
KC = D // 128
HC = H // 128
NH, HD = 16, 128
REG = 384
NSLOT = NCORE * REG
SC = NSLOT // 128
RANK_T = 1638
CSTAB = 16.0
QSCALE = 1.0 / math.sqrt(HD)
PADC = float((S - NSLOT) * math.exp(-CSTAB))
EPS = 1e-5
NBIS17 = 11

ENGS = ("sync", "gpsimd", "tensor", "scalar", "vector")


class Sem:
    def __init__(self, stack, nc, name):
        self.h = stack.enter_context(nc.semaphore(name))
        self.n = 0


class B:
    def __init__(self, nc, stack):
        self.nc = nc
        self.q = {e: [] for e in ENGS}
        self.stack = stack

    def sem(self, name):
        return Sem(self.stack, self.nc, name)

    def em(self, eng, fn):
        self.q[eng].append(fn)

    trace = []

    def dma(self, eng, sem, out, in_):
        self.trace.append(("dma", eng, sem.h.name, sem.n + 16))
        self.em(eng, lambda E, o=out, i=in_, s=sem.h: E.dma_start(out=o, in_=i).then_inc(s, 16))
        sem.n += 16
        return sem.n

    def dmac(self, eng, sem, fn):
        self.em(eng, lambda E, f=fn, s=sem.h: f(E).then_inc(s, 16))
        sem.n += 16
        return sem.n

    def wait(self, eng, sem, val=None):
        v = sem.n if val is None else val
        self.trace.append(("wait", eng, sem.h.name, v))
        self.em(eng, lambda E, s=sem.h, v=v: E.wait_ge(s, v))

    def op(self, eng, fn, inc=None, k=1):
        if inc is None:
            self.em(eng, fn)
        else:
            self.em(eng, lambda E, f=fn, s=inc.h, kk=k: f(E).then_inc(s, kk))
            inc.n += k

    def replay(self, block):
        qs = self.q

        @block.sync
        def _(e):
            for f in qs["sync"]:
                f(e)

        @block.gpsimd
        def _(e):
            for f in qs["gpsimd"]:
                f(e)

        @block.tensor
        def _(e):
            for f in qs["tensor"]:
                f(e)

        @block.scalar
        def _(e):
            for f in qs["scalar"]:
                f(e)

        @block.vector
        def _(e):
            for f in qs["vector"]:
                f(e)


def build(verbose=False):
    nc = bass.Bass("TRN2", target_bir_lowering=False)
    st = ExitStack()
    b = B(nc, st)

    def inp(name, shape, dt=F32):
        return nc.dram_tensor(name, shape, dt, kind="ExternalInput").ap()

    xT_h = inp("xT", [D, T])
    xrows_h = inp("xrows", [T, D])
    wq_h = inp("wq_t", [16, 128, 16 * 128])
    wo_h = inp("wo_t", [16, 128, 16 * 128])
    wkbh_h = inp("wk_bh", [128, 16 * 256])
    wvbh_h = inp("wv_bh", [128, 16 * 256])
    w1_h = inp("w1_t", [64, 128, 16 * 128])
    w3_h = inp("w3_t", [64, 128, 16 * 128])
    w2_h = inp("w2_slt", [128, HC * 256])
    wi_h = inp("wi_t", [128, 16])
    bq_h = inp("bq_t", [128, 16])
    bo_h = inp("bo_t", [128, 16])
    bkbh_h = inp("bk_bh", [128, 2])
    bvrow_h = inp("bv_row", [1, 256])
    bi_h = inp("bi", [1, 1])
    ones_c_h = inp("ones_c", [128, 1])
    ones_cb_h = inp("ones_cb", [128, 1], BF16)
    ones_r_h = inp("ones_r", [1, 128])
    ones_rb_h = inp("ones_rb", [1, 128], BF16)
    ones_sq_h = inp("ones_sq", [128, 128])
    ones_sqb_h = inp("ones_sqb", [128, 128], BF16)
    ident_h = inp("ident", [128, 128])
    h128_h = inp("h128", [128, 128], BF16)
    rind_h = inp("rind", [128, 8])
    iota384_h = inp("iota384", [1, REG])
    iota16_h = inp("iota16", [128, 16])

    y_t = nc.dram_tensor("y", [T, D], F32, kind="ExternalOutput")
    y_h = y_t.ap()

    imp_own_d = nc.dram_tensor("imp_own_d", [T, 1], F32)
    imp_all_d = nc.dram_tensor("imp_all_d", [S, 1], F32)
    lng_own_d = nc.dram_tensor("lng_own_d", [128 * 16, REG], BF16)
    lng_all_d = nc.dram_tensor("lng_all_d", [NCORE * 16 * 128, REG], BF16)
    kv_own_e_d = nc.dram_tensor("kv_own_e_d", [768, 1024], BF16)
    kv_own_o_d = nc.dram_tensor("kv_own_o_d", [768, 1024], BF16)
    kv_all_e_d = nc.dram_tensor("kv_all_e_d", [NCORE * 768, 1024], BF16)
    kv_all_o_d = nc.dram_tensor("kv_all_o_d", [NCORE * 768, 1024], BF16)
    gath_d = nc.dram_tensor("gath_d", [512, D], F32)
    w2t_own_d = nc.dram_tensor("w2t_own_d", [128 * HC, 256], BF16)
    w2t_all_d = nc.dram_tensor("w2t_all_d", [NCORE * 128 * HC, 256], BF16)
    q_d = nc.dram_tensor("q_d", [NH * 128, T], BF16)
    x1_d = nc.dram_tensor("x1_d", [KC * 128, T], F32)

    PSa = nc.alloc_psum_tensor("PSa", [128, 1024], F32)
    PSb = nc.alloc_psum_tensor("PSb", [128, 1024], F32)
    PSc = nc.alloc_psum_tensor("PSc", [128, 1024], F32)
    PSd = nc.alloc_psum_tensor("PSd", [128, 1024], F32)
    PS = [PSa, PSb, PSc, PSd]

    A = nc.alloc_sbuf_tensor("A", [128, KC * T], BF16)
    SqB = nc.alloc_sbuf_tensor("SqB", [128, 2048], F32)
    QSt = nc.alloc_sbuf_tensor("QSt", [128, 2 * T], BF16)
    XS = nc.alloc_sbuf_tensor("XS", [128, 2048], F32)
    Ident = nc.alloc_sbuf_tensor("Ident", [128, 128], F32)
    ws_scope = ExitStack()
    WsA = ws_scope.enter_context(nc.sbuf_tensor("WsA", [128, 2 * 2048], BF16))
    WsB = ws_scope.enter_context(nc.sbuf_tensor("WsB", [128, 2 * 2048], BF16))
    mid_scope = ExitStack()

    def mid(name, shape, dt=F32):
        return mid_scope.enter_context(nc.sbuf_tensor(name, shape, dt))

    Sq2 = mid("Sq2", [128, 1024])
    RowS = mid("RowS", [1, 2048])
    RstdB = mid("RstdB", [128, 1024])
    MrB = mid("MrB", [128, 1024])
    OnesSq = mid("OnesSq", [128, 128])
    OnesSqB = mid("OnesSqB", [128, 128], BF16)
    OnesR = mid("OnesR", [1, 128])
    OnesRb = mid("OnesRb", [1, 128], BF16)
    OnesC = mid("OnesC", [128, 1])
    OnesCb = mid("OnesCb", [128, 1], BF16)
    Rind = mid("Rind", [128, 8])
    Iota384 = mid("Iota384", [1, REG])
    BiasQ = mid("BiasQ", [128, 16])
    BiasQs = mid("BiasQs", [128, 16])
    BiasO = mid("BiasO", [128, 16])
    BiasKbh = mid("BiasKbh", [128, 2])
    BiasVr = mid("BiasVr", [1, 256])
    BvB = mid("BvB", [128, 256])
    BiT = mid("BiT", [1, 1])
    Wi = mid("Wi", [128, 16])
    Wib = mid("Wib", [128, 16], BF16)
    H128 = mid("H128", [128, 128], BF16)
    EpsCol = mid("EpsCol", [128, 1])
    NegC = mid("NegC", [128, 1])
    w2t_scope = ExitStack()
    W2T = w2t_scope.enter_context(nc.sbuf_tensor("W2T", [128, HC * 256], BF16))
    ph2_scope = ExitStack()

    def ph2(name, shape, dt=F32):
        return ph2_scope.enter_context(nc.sbuf_tensor(name, shape, dt))

    RowB = ph2("RowB", [1, NSLOT], BF16)
    ImpRow = ph2("ImpRow", [1, T], F32)
    Validb = ph2("Validb", [128, NSLOT], BF16)
    IdxT = ph2("IdxT", [128, 8], F32)
    IdxTi = ph2("IdxTi", [128, 8], I32)
    DmI = ph2("DmI", [128, 8], I32)
    PIota = ph2("PIota", [128, 1])
    Iota16 = ph2("Iota16", [128, 16])
    Mids = ph2("Mids", [128, 16])
    LoB = ph2("LoB", [128, 16])
    HiB = ph2("HiB", [128, 16])
    CandL = ph2("CandL", [128, 16])
    PC16 = ph2("PC16", [128, 16])
    GeI = ph2("GeI", [128, 16], I32)
    ValidT = ph2("ValidT", [128, SC], F32)
    Bis = ph2("Bis", [128, 8], F32)
    BisI = ph2("BisI", [128, 1], I32)
    ImpAll = ph2("ImpAll", [128, 64], F32)
    Cnt = ph2("Cnt", [8, 1], F32)
    CntRow = ph2("CntRow", [1, 8], F32)

    s_cst = b.sem("s_cst")
    s_w2l = b.sem("s_w2l")
    s_wa = b.sem("s_wa")
    s_wb = b.sem("s_wb")
    s_wc = b.sem("s_wc")
    s_wd = b.sem("s_wd")
    s_xra = b.sem("s_xra")
    s_xrb = b.sem("s_xrb")
    s_rga = b.sem("s_rga")
    s_rgb = b.sem("s_rgb")
    s_reg = b.sem("s_reg")
    s_lng = b.sem("s_lng")
    s_kva = b.sem("s_kva")
    s_kvb = b.sem("s_kvb")
    s_out = b.sem("s_out")
    s_impst = b.sem("s_impst")
    s_lngst = b.sem("s_lngst")
    s_kvst = b.sem("s_kvst")
    s_w2st = b.sem("s_w2st")
    s_qsa = b.sem("s_qsa")
    s_qsb = b.sem("s_qsb")
    s_spa = b.sem("s_spa")
    s_spb = b.sem("s_spb")
    s_sta = b.sem("s_sta")
    s_stb = b.sem("s_stb")
    s_cc = b.sem("s_cc")
    s_dbg = b.sem("s_dbg")
    s_sp = b.sem("s_sp")
    s_x1a = b.sem("s_x1a")
    s_x1b = b.sem("s_x1b")
    s_w2sA = b.sem("s_w2sA")
    s_w2sB = b.sem("s_w2sB")
    s_st = b.sem("s_st")
    s_scat = b.sem("s_scat")
    s_zro = b.sem("s_zro")
    s_va = b.sem("s_va")
    s_vb = b.sem("s_vb")
    c_pe = b.sem("c_pe")
    c_ac = b.sem("c_ac")
    c_ve = b.sem("c_ve")
    c_gp = b.sem("c_gp")

    def opd(eng, fn, inc=None, k=1):
        b.em(eng, fn)
        if inc is None:
            b.em(eng, lambda E: E.drain())
        else:
            b.em(eng, lambda E, s=inc.h, kk=k: E.drain().then_inc(s, kk))
            inc.n += k

    for dst, src in [
        (OnesC[:, :], ones_c_h), (Rind[:, :], rind_h), (Ident[:, :], ident_h),
        (OnesSq[:, :], ones_sq_h), (OnesR[:, :], ones_r_h), (Iota384[:, :], iota384_h),
        (Iota16[:, :], iota16_h),
        (BiasQ[:, :], bq_h), (BiasO[:, :], bo_h), (BiasKbh[:, :], bkbh_h),
        (BiasVr[:, :], bvrow_h), (BiT[:, :], bi_h), (Wi[:, :], wi_h),
        (H128[:, :], h128_h), (OnesCb[:, :], ones_cb_h), (OnesRb[:, :], ones_rb_h),
        (OnesSqB[:, :], ones_sqb_h),
    ]:
        b.dma("sync", s_cst, dst, src)
    for i4 in range(4):
        b.dma("gpsimd", s_w2l, W2T[:, i4 * 4096:(i4 + 1) * 4096],
              w2_h[:, i4 * 4096:(i4 + 1) * 4096])
    b.wait("vector", s_cst)
    b.op("vector", lambda v: v.tensor_copy(Wib[:, :], Wi[:, :]))
    b.op("vector", lambda v: v.memset(Bis[:, :], 0.0))
    b.op("vector", lambda v: v.memset(BisI[:, :], 0))
    b.op("vector", lambda v: v.memset(IdxT[:, :], 0.0))
    b.op("vector", lambda v: v.memset(EpsCol[:, :], EPS))
    b.op("vector", lambda v: v.memset(NegC[:, :], -CSTAB))
    opd("vector", lambda v: v.tensor_scalar(BiasQs[:, :], BiasQ[:, :], QSCALE, None, ALU.mult), inc=c_ve)

    def emit_ln_finalize(ps_sum, ps_sq):
        opd("vector", lambda v: v.tensor_scalar(Sq2[:, :], ps_sum, 1.0 / D, None, ALU.mult))
        opd("vector", lambda v: v.tensor_scalar(RstdB[:, :], ps_sq, 1.0 / D, None, ALU.mult))
        opd("vector", lambda v: v.tensor_mul(MrB[:, :], Sq2[:, :], Sq2[:, :]))
        opd("vector", lambda v: v.tensor_sub(RstdB[:, :], RstdB[:, :], MrB[:, :]), inc=c_ve)
        b.wait("scalar", c_ve)
        opd("scalar", lambda s: s.activation(RstdB[:, :], RstdB[:, :], AF.Ln, bias=EpsCol[:, :], scale=1.0))
        opd("scalar", lambda s: s.activation(RstdB[:, :], RstdB[:, :], AF.Exp, scale=-0.5), inc=c_ac)
        b.wait("vector", c_ac)
        b.op("vector", lambda v: v.scalar_tensor_tensor(MrB[:, :], Sq2[:, :], -1.0, RstdB[:, :],
                                                        ALU.mult, ALU.mult), inc=c_ve)
        return c_ve.n

    b.wait("tensor", s_cst)
    ac0, pe0 = c_ac.n, c_pe.n
    for cch in range(KC):
        par = cch % 2
        xsem = s_x1a if par == 0 else s_x1b
        xc = XS[:, par * 1024:(par + 1) * 1024]
        if cch >= 2:
            b.wait("sync", c_pe, pe0 + (cch - 2) + 1)
        b.dma("sync", xsem, xc, xT_h[cch * 128:(cch + 1) * 128, :])
        b.wait("scalar", xsem, xsem.n)
        sq = SqB[:, par * 1024:par * 1024 + 1024]
        if cch >= 2:
            b.wait("scalar", c_pe, pe0 + (cch - 2) + 1)
        b.op("scalar", lambda s, o=sq, i=xc: s.activation(o, i, AF.Square), inc=c_ac)
        b.wait("tensor", xsem, xsem.n)
        for hh in range(2):
            b.op("tensor", lambda t, o=PSa[:, hh * 512:hh * 512 + 512],
                 i=xc[:, hh * 512:hh * 512 + 512], stt=(cch == 0), spp=(cch == KC - 1):
                 t.matmul(o, OnesSq[:, :], i, start=stt, stop=spp))
        b.wait("tensor", c_ac, ac0 + cch + 1)
        for hh in range(2):
            b.op("tensor", lambda t, o=PSb[:, hh * 512:hh * 512 + 512],
                 i=sq[:, hh * 512:hh * 512 + 512], stt=(cch == 0), spp=(cch == KC - 1):
                 t.matmul(o, OnesSq[:, :], i, start=stt, stop=spp),
                 inc=(c_pe if hh == 1 else None))
    b.wait("vector", c_pe)
    ln1_fin_ve = emit_ln_finalize(PSa[:, 0:1024], PSb[:, 0:1024])
    ap_ve = {}
    for cch in range(KC):
        par = cch % 2
        xsem = s_x1a if par == 0 else s_x1b
        xq = "sync" if par == 0 else "scalar"
        xc = XS[:, par * 1024:(par + 1) * 1024]
        if cch >= 2:
            b.wait(xq, c_ve, ap_ve[cch - 2])
        else:
            b.wait(xq, c_ve, ln1_fin_ve)
        b.dma(xq, xsem, xc, xT_h[cch * 128:(cch + 1) * 128, :])
        b.wait("vector", xsem, xsem.n)
        dc = A[:, cch * T:(cch + 1) * T]
        tmp = SqB[:, par * 1024:par * 1024 + 1024]
        b.op("vector", lambda v, a=tmp, x=xc: v.tensor_tensor(a, x, RstdB[:, :], ALU.mult))
        b.op("vector", lambda v, a=tmp, d=dc: v.tensor_tensor(d, a, MrB[:, :], ALU.add),
             inc=c_ve)
        ap_ve[cch] = c_ve.n
    ln1_ve = c_ve.n

    gath_scope = ExitStack()
    Xtok = gath_scope.enter_context(nc.sbuf_tensor("Xtok", [128, 2 * D], F32))
    XgT = gath_scope.enter_context(nc.sbuf_tensor("XgT", [128, 16 * REG], F32))
    LnAll = gath_scope.enter_context(nc.sbuf_tensor("LnAll", [128, 16 * REG], BF16))
    b.wait("tensor", s_w2l)
    h_ve0 = c_ve.n
    for hc in range(HC):
        ps = PS[2 + (hc % 2)][:, 0:256]
        if hc >= 2:
            b.wait("tensor", c_ve, h_ve0 + (hc - 2) + 1)
        b.op("tensor", lambda t, o=ps, i=W2T[:, hc * 256:hc * 256 + 256]:
             t.matmul(o, H128[:, :], i, start=True, stop=True), inc=c_pe)
        b.wait("vector", c_pe)
        opd("vector", lambda v, o=W2T[:, hc * 256:hc * 256 + 256], i=ps:
            v.tensor_scalar(o, i, 1.0 / math.sqrt(H), None, ALU.mult), inc=c_ve)
    w2_h128_ve = c_ve.n

    b.wait("tensor", c_ve, ln1_ve)
    for cch in range(KC):
        for hh in range(2):
            b.op("tensor", lambda t, o=PSa[0:1, hh * 512:hh * 512 + 512],
                 i=A[:, cch * T + hh * 512:cch * T + hh * 512 + 512],
                 lw=Wib[:, cch:cch + 1], stt=(cch == 0), spp=(cch == KC - 1):
                 t.matmul(o, lw, i, start=stt, stop=spp),
                 inc=(c_pe if (cch == KC - 1 and hh == 1) else None))
    b.wait("vector", c_pe)
    b.op("vector", lambda v: v.tensor_scalar(ImpRow[:, :], PSa[0:1, 0:T], BiT[0:1, 0:1], None, ALU.add), inc=c_ve)
    b.wait("scalar", c_ve)
    b.dma("scalar", s_impst, imp_own_d.ap(), ImpRow[:, :])
    b.wait("gpsimd", s_impst, s_impst.n)
    b.op("gpsimd", lambda g: g.collective_compute(
        "AllGather", ALU.bypass, replica_groups=[list(range(NCORE))],
        ins=[imp_own_d.ap()], outs=[imp_all_d.ap()]), inc=s_cc, k=1)
    b.wait("gpsimd", s_cc, s_cc.n)
    b.dma("gpsimd", s_reg, ImpAll[:, :], imp_all_d.ap().rearrange("(p f) o -> p (f o)", p=128))
    b.op("gpsimd", lambda g: g.iota(PIota[:, :], [[1, 1]], channel_multiplier=1,
                                    allow_small_or_imprecise_dtypes=True), inc=c_gp)

    LO, HI, MID, PC = Bis[:, 0:1], Bis[:, 1:2], Bis[:, 2:3], Bis[:, 3:4]
    b.wait("vector", s_reg)
    b.op("vector", lambda v: v.memset(LO, -64.0))
    opd("vector", lambda v: v.memset(HI, 64.0), inc=c_ve)
    b.wait("tensor", c_ve)
    for it in range(42):
        opd("vector", lambda v: v.tensor_add(MID, LO, HI))
        opd("vector", lambda v: v.tensor_scalar(MID, MID, 0.5, None, ALU.mult))
        opd("vector", lambda v: v.tensor_scalar(Sq2[:, 0:64], ImpAll[:, :], MID[:, 0:1], None, ALU.is_gt))
        opd("vector", lambda v: v.reduce_sum(PC, Sq2[:, 0:64], axis=AX), inc=c_ve)
        b.wait("tensor", c_ve)
        b.op("tensor", lambda t: t.matmul(PSa[:, 0:1], OnesSq[:, :], PC, start=True, stop=True), inc=c_pe)
        b.wait("vector", c_pe)
        opd("vector", lambda v: v.tensor_scalar(BisI[:, 0:1], PSa[:, 0:1], float(RANK_T + 1), None, ALU.is_ge))
        opd("vector", lambda v: v.select(LO, BisI[:, 0:1], MID, LO))
        opd("vector", lambda v: v.tensor_scalar(BisI[:, 0:1], PSa[:, 0:1], float(RANK_T + 1), None, ALU.is_lt))
        opd("vector", lambda v: v.select(HI, BisI[:, 0:1], MID, HI), inc=c_ve)

    opd("vector", lambda v: v.tensor_scalar(Sq2[:, 0:64], ImpAll[:, :], HI[:, 0:1], None, ALU.is_gt))
    opd("vector", lambda v: v.reduce_sum(Bis[:, 4:5], Sq2[:, 0:64], axis=AX), inc=c_ve)
    b.wait("tensor", c_ve)
    b.op("tensor", lambda t: t.matmul(PSa[0:8, 0:1], Rind[:, :], Bis[:, 4:5], start=True, stop=True), inc=c_pe)
    b.wait("vector", c_pe)
    opd("vector", lambda v: v.tensor_scalar(Cnt[:, :], PSa[0:8, 0:1], float(REG), None, ALU.min), inc=c_ve)
    b.op("vector", lambda v: v.memset(Sq2[:, 0:128], 0.0))
    opd("vector", lambda v: v.tensor_copy(Sq2[0:8, 0:1], Cnt[:, :]), inc=c_ve)
    b.wait("tensor", c_ve)
    b.op("tensor", lambda t: t.transpose(PSb[:, 0:128], Sq2[:, 0:128], Ident[:, :]), inc=c_pe)
    b.wait("vector", c_pe)
    opd("vector", lambda v: v.tensor_copy(CntRow[:, :], PSb[0:1, 0:8]), inc=c_ve)
    for r in range(NCORE):
        if r < NCORE - 1:
            b.op("vector", lambda v, r=r: v.tensor_scalar(
                RowB[0:1, r * REG:(r + 1) * REG], Iota384[:, :], CntRow[0:1, r:r + 1], None, ALU.is_lt))
        else:
            opd("vector", lambda v, r=r: v.tensor_scalar(
                RowB[0:1, r * REG:(r + 1) * REG], Iota384[:, :], CntRow[0:1, r:r + 1], None, ALU.is_lt),
                inc=c_ve)
    b.wait("tensor", c_ve)
    vb0 = c_ve.n
    for hh in range(NSLOT // 512):
        ps = PS[hh % 2][:, 0:512]
        if hh >= 2:
            b.wait("tensor", c_ve, vb0 + (hh - 2) + 1)
        b.op("tensor", lambda t, o=ps, i=RowB[0:1, hh * 512:hh * 512 + 512]:
             t.matmul(o, OnesRb[:, :], i, start=True, stop=True), inc=c_pe)
        b.wait("vector", c_pe)
        b.op("vector", lambda v, o=Validb[:, hh * 512:hh * 512 + 512], i=ps: v.tensor_copy(o, i), inc=c_ve)

    def emit_row_to_cols(row_src_off, n_sub, dst_tile, dst_off):
        b.op("vector", lambda v: v.memset(Sq2[:, 0:1024], 0.0))
        opd("vector", lambda v, o=Sq2[0:1, 0:n_sub * 128],
            i=RowB[0:1, row_src_off:row_src_off + n_sub * 128]: v.tensor_copy(o, i), inc=c_ve)
        b.wait("tensor", c_ve)
        t0 = c_ve.n
        for sub in range(n_sub):
            ps = PSc[:, 128:256]
            if sub >= 1:
                b.wait("tensor", c_ve, t0 + sub)
            b.op("tensor", lambda t, o=ps, i=Sq2[:, sub * 128:sub * 128 + 128]:
                 t.transpose(o, i, Ident[:, :]), inc=c_pe)
            b.wait("vector", c_pe)
            opd("vector", lambda v, o=dst_tile[:, dst_off + sub:dst_off + sub + 1], i=ps[:, 0:1]:
                v.tensor_copy(o, i), inc=c_ve)

    for blk in range(3):
        emit_row_to_cols(blk * 1024, 8, ValidT, blk * 8)

    MRow = RowS[0:1, 1024:2048]
    opd("vector", lambda v: v.tensor_scalar(MRow, ImpRow[:, :], HI[0:1, 0:1], None, ALU.is_gt))
    opd("vector", lambda v: v.tensor_copy(SqB[0:1, 0:T], MRow))
    ca, cb = 0, 1024
    for si in range(10):
        d = 1 << si
        b.op("vector", lambda v, d=d, a=ca, o=cb:
             v.tensor_add(SqB[0:1, o + d:o + T], SqB[0:1, a + d:a + T], SqB[0:1, a:a + T - d]))
        opd("vector", lambda v, d=d, a=ca, o=cb:
            v.tensor_copy(SqB[0:1, o:o + d], SqB[0:1, a:a + d]))
        ca, cb = cb, ca
    b.op("vector", lambda v, a=ca: v.tensor_scalar(SqB[0:1, a:a + T], SqB[0:1, a:a + T],
                                                   -1.0 - 1024.0, None, ALU.add))
    b.op("vector", lambda v, a=ca: v.tensor_mul(SqB[0:1, a:a + T], SqB[0:1, a:a + T], MRow))
    b.op("vector", lambda v: v.memset(Sq2[:, 0:1024], 0.0), inc=c_ve)
    opd("vector", lambda v, a=ca: v.tensor_scalar(Sq2[0:1, 0:T], SqB[0:1, a:a + T],
                                                   1024.0, None, ALU.add), inc=c_ve)
    b.wait("tensor", c_ve)
    t0i = c_ve.n
    for sub in range(8):
        ps = PSc[:, 128:256]
        if sub >= 1:
            b.wait("tensor", c_ve, t0i + sub)
        b.op("tensor", lambda t, o=ps, i=Sq2[:, sub * 128:sub * 128 + 128]:
             t.transpose(o, i, Ident[:, :]), inc=c_pe)
        b.wait("vector", c_pe)
        opd("vector", lambda v, o=IdxT[:, sub:sub + 1], i=ps[:, 0:1]: v.tensor_copy(o, i), inc=c_ve)
    b.wait("vector", c_gp)
    opd("vector", lambda v: v.tensor_scalar(PIota[:, :], PIota[:, :], 384.0, None, ALU.add))
    opd("vector", lambda v: v.tensor_scalar(DmI[:, :], IdxT[:, :], 1024.0, None, ALU.is_ge))
    opd("vector", lambda v: v.tensor_scalar(Sq2[:, 8:16], OnesSq[:, 0:8], PIota[:, 0:1], None, ALU.mult))
    opd("vector", lambda v: v.select(IdxT[:, :], DmI[:, :], Sq2[:, 8:16], IdxT[:, :]))
    opd("vector", lambda v: v.tensor_copy(IdxTi[:, :], IdxT[:, :]), inc=c_ve)
    idx_ve = c_ve.n

    q_consumed_pe = {}
    q_evac_ac = {}

    def emit_q_oc(oc):
        par = oc % 2
        buf = WsA[:, par * 2048:(par + 1) * 2048]
        sem = s_wa if par == 0 else s_wb
        if oc == 0:
            b.wait("tensor", c_ve, g_fin_ve)
        if oc >= 2:
            b.wait("gpsimd", c_pe, q_consumed_pe[oc - 2])
        b.dma("gpsimd", sem, buf, wq_h[oc])
        b.wait("tensor", sem, sem.n)
        pst = PS[par]
        if oc >= 2:
            b.wait("tensor", c_ve, q_evac_ac[oc - 2])
        for kcc in range(16):
            for hh in range(2):
                b.op("tensor", lambda t, o=pst[:, hh * 512:hh * 512 + 512],
                     lw=buf[:, kcc * 128:kcc * 128 + 128],
                     i=A[:, kcc * T + hh * 512:kcc * T + hh * 512 + 512],
                     stt=(kcc == 0), spp=(kcc == 15):
                     t.matmul(o, lw, i, start=stt, stop=spp),
                     inc=(c_pe if (kcc == 15 and hh == 1) else None))
        q_consumed_pe[oc] = c_pe.n
        b.wait("vector", c_pe)
        qs = s_qsa if par == 0 else s_qsb
        if oc >= 2:
            b.wait("vector", qs, qs.n)
        b.op("vector", lambda v, o=QSt[:, par * T:(par + 1) * T], i=pst[:, 0:1024], bi=BiasQ[:, oc:oc + 1]:
             v.tensor_scalar(o, i, bi, QSCALE, ALU.add, ALU.mult), inc=c_ve)
        q_evac_ac[oc] = c_ve.n
        b.wait("scalar", c_ve)
        b.dma("scalar", qs, q_d.ap()[oc * 128:(oc + 1) * 128, :], QSt[:, par * T:(par + 1) * T])

    b.op("vector", lambda v: v.memset(Xtok[:, 0:D], 0.0), inc=c_ve)
    b.wait("sync", c_ve)
    for z in range(3):
        b.dma("sync", s_zro, gath_d.ap()[z * 128:(z + 1) * 128, :], Xtok[:, 0:D])
    b.wait("gpsimd", s_zro, s_zro.n)
    b.wait("gpsimd", c_ve, idx_ve)
    for tc in range(8):
        par = tc % 2
        xsem = s_xra if par == 0 else s_xrb
        dst = Xtok[:, par * D:(par + 1) * D]
        if tc >= 2:
            b.wait("sync", s_scat, 16 * (tc - 1))
        b.dma("sync", xsem, dst, xrows_h[tc * 128:(tc + 1) * 128, :])
        b.wait("gpsimd", xsem, xsem.n)
        b.dmac("gpsimd", s_scat, lambda g, src=dst, i=tc: g.indirect_dma_start(
            gath_d.ap(), IndirectOffsetOnAxis(ap=IdxTi[:, i:i + 1], axis=0), src, None))
    b.wait("sync", s_scat, s_scat.n)
    b.wait("tensor", c_ve, idx_ve)
    tr3_ve = {}
    tr3_pe = {}
    for r3 in range(3):
        dst = Xtok[:, (r3 % 2) * D:(r3 % 2 + 1) * D]
        if r3 == 2:
            b.wait("sync", c_pe, tr3_pe[0])
        b.dma("sync", s_xra, dst, gath_d.ap()[r3 * 128:(r3 + 1) * 128, :])
        xmark = s_xra.n
        if r3 >= 1:
            b.wait("tensor", c_ve, tr3_ve[r3 - 1])
        b.wait("tensor", s_xra, xmark)
        for kcc in range(16):
            ps = PS[2 + kcc // 8][:, (kcc % 8) * 128:(kcc % 8) * 128 + 128]
            b.op("tensor", lambda t, o=ps, i=dst[:, kcc * 128:kcc * 128 + 128]:
                 t.transpose(o, i, Ident[:, :]), inc=(c_pe if kcc == 15 else None))
        tr3_pe[r3] = c_pe.n
        b.wait("vector", c_pe)
        for kcc in range(16):
            ps = PS[2 + kcc // 8][:, (kcc % 8) * 128:(kcc % 8) * 128 + 128]
            b.op("vector", lambda v, o=XgT[:, kcc * REG + r3 * 128:kcc * REG + r3 * 128 + 128],
                 i=ps: v.tensor_copy(o, i), inc=(c_ve if kcc == 15 else None))
        tr3_ve[r3] = c_ve.n
    b.wait("tensor", c_ve, tr3_ve[2])
    b.wait("scalar", c_ve, tr3_ve[2])
    g_ac0 = c_ac.n
    gstat_pe = {}
    for kcc in range(16):
        sq = SqB[:, (kcc % 4) * 512:(kcc % 4) * 512 + REG]
        if kcc >= 4:
            b.wait("scalar", c_pe, gstat_pe[kcc - 4])
        b.op("scalar", lambda s, o=sq, i=XgT[:, kcc * REG:(kcc + 1) * REG]:
             s.activation(o, i, AF.Square), inc=c_ac)
        b.op("tensor", lambda t, o=PSa[:, 0:REG], i=XgT[:, kcc * REG:(kcc + 1) * REG],
             stt=(kcc == 0), spp=(kcc == 15):
             t.matmul(o, OnesSq[:, :], i, start=stt, stop=spp))
        b.wait("tensor", c_ac, g_ac0 + kcc + 1)
        b.op("tensor", lambda t, o=PSb[:, 0:REG], i=sq, stt=(kcc == 0), spp=(kcc == 15):
             t.matmul(o, OnesSq[:, :], i, start=stt, stop=spp), inc=c_pe)
        gstat_pe[kcc] = c_pe.n
    b.wait("vector", c_pe)
    R3, M3, S3 = RstdB[:, 0:REG], MrB[:, 0:REG], Sq2[:, 0:REG]
    b.op("vector", lambda v: v.tensor_scalar(S3, PSa[:, 0:REG], 1.0 / D, None, ALU.mult))
    b.op("vector", lambda v: v.tensor_scalar(R3, PSb[:, 0:REG], 1.0 / D, None, ALU.mult))
    b.op("vector", lambda v: v.tensor_mul(M3, S3, S3))
    b.op("vector", lambda v: v.tensor_sub(R3, R3, M3), inc=c_ve)
    b.wait("scalar", c_ve)
    b.op("scalar", lambda s: s.activation(R3, R3, AF.Ln, bias=EpsCol[:, :], scale=1.0))
    b.op("scalar", lambda s: s.activation(R3, R3, AF.Exp, scale=-0.5), inc=c_ac)
    b.wait("vector", c_ac)
    b.op("vector", lambda v: v.scalar_tensor_tensor(M3, S3, -1.0, R3, ALU.mult, ALU.mult),
         inc=c_ve)
    g_fin_ve = c_ve.n
    for kcc in range(16):
        sq = SqB[:, (kcc % 4) * 512:(kcc % 4) * 512 + REG]
        b.op("vector", lambda v, o=sq, i=XgT[:, kcc * REG:(kcc + 1) * REG]:
             v.tensor_tensor(o, i, R3, ALU.mult))
        b.op("vector", lambda v, o=LnAll[:, kcc * REG:(kcc + 1) * REG], i=sq:
             v.tensor_tensor(o, i, M3, ALU.add),
             inc=(c_ve if kcc == 15 else None))
    b.wait("scalar", c_ve)
    b.dma("scalar", s_lngst,
          bass.AP(lng_own_d, 0, [[16 * REG, 128], [REG, 16], [1, REG]]),
          LnAll[:, :])
    b.wait("gpsimd", s_lngst, s_lngst.n)
    b.op("gpsimd", lambda g: g.collective_compute(
        "AllGather", ALU.bypass, replica_groups=[list(range(NCORE))],
        ins=[lng_own_d.ap()], outs=[lng_all_d.ap()]), inc=s_cc, k=1)
    cc_lng = s_cc.n
    for oc in range(16):
        emit_q_oc(oc)

    b.wait("tensor", s_cst)
    b.op("tensor", lambda t: t.matmul(PSc[:, 256:512], OnesR[:, :], BiasVr[:, :], start=True, stop=True), inc=c_pe)
    b.wait("vector", c_pe)
    b.op("vector", lambda v: v.tensor_copy(BvB[:, :], PSc[:, 256:512]), inc=c_ve)

    gath_scope.close()
    kvb_scope = ExitStack()
    LnS = kvb_scope.enter_context(nc.sbuf_tensor("LnS", [128, 2 * 16 * REG], BF16))
    KOut = kvb_scope.enter_context(nc.sbuf_tensor("KOut", [128, 2 * NSLOT], BF16))
    VOut = kvb_scope.enter_context(nc.sbuf_tensor("VOut", [128, SC * 256], BF16))
    WkB = kvb_scope.enter_context(nc.sbuf_tensor("WkB", [128, 16 * 256], BF16))
    WvB = kvb_scope.enter_context(nc.sbuf_tensor("WvB", [128, 16 * 256], BF16))
    b.dma("gpsimd", s_wc, WkB[:, :], wkbh_h)
    b.dma("gpsimd", s_wc, WvB[:, :], wvbh_h)
    b.wait("gpsimd", s_cc, cc_lng)
    kv_pe = {}
    kv_ve = {}
    for r in range(NCORE):
        parl = r % 2
        lns = LnS[:, parl * 16 * REG:(parl + 1) * 16 * REG]
        if r >= 2:
            b.wait("gpsimd", c_pe, kv_pe[r - 2])
        b.dma("gpsimd", s_lng, lns,
              lng_all_d.ap()[r * 2048:(r + 1) * 2048, :].rearrange("(p kc) s -> p (kc s)", p=128))
        b.wait("tensor", s_lng, s_lng.n)
        if r == 0:
            b.wait("tensor", c_ve)
            b.wait("tensor", s_wc, s_wc.n)
        else:
            b.wait("tensor", c_ve, kv_ve[r - 1])
        for sub in range(2):
            ps = PS[sub][:, 0:REG]
            for kcc in range(16):
                b.op("tensor", lambda t, o=ps, lw=WkB[:, kcc * 256 + sub * 128:kcc * 256 + sub * 128 + 128],
                     i=lns[:, kcc * REG:(kcc + 1) * REG], stt=(kcc == 0), spp=(kcc == 15):
                     t.matmul(o, lw, i, start=stt, stop=spp),
                     inc=(c_pe if kcc == 15 else None))
            b.wait("vector", c_pe)
            b.op("vector", lambda v, o=KOut[:, sub * NSLOT + r * REG:sub * NSLOT + (r + 1) * REG],
                 i=ps, bi=BiasKbh[:, sub:sub + 1], vb=Validb[:, r * REG:(r + 1) * REG]:
                 v.scalar_tensor_tensor(o, i, bi, vb, ALU.add, ALU.mult), inc=c_ve)
        for scl in range(3):
            ps = PS[2 + scl % 2][:, 0:256]
            sgl = r * 3 + scl
            if scl >= 2:
                b.wait("tensor", c_ve, v_ve0 + (scl - 2) + 1)
            for kcc in range(16):
                b.op("tensor", lambda t, o=ps, lw=lns[:, kcc * REG + scl * 128:kcc * REG + scl * 128 + 128],
                     i=WvB[:, kcc * 256:(kcc + 1) * 256], stt=(kcc == 0), spp=(kcc == 15):
                     t.matmul(o, lw, i, start=stt, stop=spp),
                     inc=(c_pe if kcc == 15 else None))
            b.wait("vector", c_pe)
            if scl == 0:
                v_ve0 = c_ve.n
            b.op("vector", lambda v, i=ps: v.tensor_add(Sq2[:, 0:256], i, BvB[:, :]))
            b.op("vector", lambda v, o=VOut[:, sgl * 256:(sgl + 1) * 256], vt=ValidT[:, sgl:sgl + 1]:
                 v.tensor_scalar(o, Sq2[:, 0:256], vt, None, ALU.mult), inc=c_ve)
        kv_pe[r] = c_pe.n
        kv_ve[r] = c_ve.n
    b.wait("scalar", c_ve)
    for sub, (kv_own, kv_all) in enumerate([(kv_own_e_d, kv_all_e_d), (kv_own_o_d, kv_all_o_d)]):
        b.dma("scalar", s_kvst,
              bass.AP(kv_own, 0, [[NSLOT, 128], [1, NSLOT]]),
              KOut[:, sub * NSLOT:(sub + 1) * NSLOT])
        vsrc = bass.AP(VOut, sub * 128, [[SC * 256, 128], [256, SC], [1, 128]])
        b.dma("scalar", s_kvst,
              bass.AP(kv_own, 384 * 1024, [[SC * 128, 128], [128, SC], [1, 128]]),
              vsrc)
        b.wait("gpsimd", s_kvst, s_kvst.n)
        b.op("gpsimd", lambda g, i=kv_own, o=kv_all: g.collective_compute(
            "AllGather", ALU.bypass, replica_groups=[list(range(NCORE))],
            ins=[i.ap()], outs=[o.ap()]), inc=s_cc, k=1)
        if sub == 0:
            cc_kv_e = s_cc.n
    cc_kv_o = s_cc.n
    kvb_scope.close()
    ph2_scope.close()

    w2p_scope = ExitStack()
    W2P = w2p_scope.enter_context(nc.sbuf_tensor("W2P", [128, HC * 256], BF16))
    b.wait("vector", c_ve, w2_h128_ve)
    CW = 256
    cur, oth = W2T, W2P
    for si in range(6):
        half = 1 << si
        nblk = HC // (2 * half)
        apl = [[HC * CW, 128], [2 * half * CW, nblk], [1, half * CW]]
        ev_s = bass.AP(cur, 0, apl)
        od_s = bass.AP(cur, half * CW, apl)
        ev_d = bass.AP(oth, 0, apl)
        od_d = bass.AP(oth, half * CW, apl)
        b.op("vector", lambda v, a=ev_d, x=ev_s, y=od_s: v.tensor_add(a, x, y))
        b.op("vector", lambda v, a=od_d, x=ev_s, y=od_s: v.tensor_sub(a, x, y),
             inc=(c_ve if si == 5 else None))
        cur, oth = oth, cur
    w2fin = cur
    b.wait("scalar", c_ve)
    b.dma("scalar", s_w2st, w2t_own_d.ap().rearrange("(p hc) d -> p (hc d)", p=128), w2fin[:, :])
    b.wait("gpsimd", s_w2st, s_w2st.n)
    b.op("gpsimd", lambda g: g.collective_compute(
        "AllGather", ALU.bypass, replica_groups=[list(range(NCORE))],
        ins=[w2t_own_d.ap()], outs=[w2t_all_d.ap()]), inc=s_cc, k=1)
    cc_w2 = s_cc.n
    w2p_scope.close()
    w2t_scope.close()

    att_scope = ExitStack()
    KH = att_scope.enter_context(nc.sbuf_tensor("KH", [128, 2 * NSLOT], BF16))
    VH = att_scope.enter_context(nc.sbuf_tensor("VH", [128, 2 * SC * 128], BF16))
    QH = att_scope.enter_context(nc.sbuf_tensor("QH", [128, 2 * T], BF16))
    NRING = 8
    LAG = 5
    PR = att_scope.enter_context(nc.sbuf_tensor("PR", [128, NRING * T], BF16))
    b.wait("scalar", s_qsa, s_qsa.n)
    b.wait("scalar", s_qsb, s_qsb.n)
    b.wait("tensor", c_ve)
    NJ = NH * SC
    ORD = list(range(0, NH, 2)) + list(range(1, NH, 2))
    sc_pe = {}
    av_pe = {}
    den_pe = {}
    ex_ac = {}
    epi_ve = {}
    epi1_ve = {}
    ksem_at = {}
    vsem_at = {}

    def att_load(pos):
        h = ORD[pos]
        par = pos % 2
        hb = h // 2
        kv_all = kv_all_e_d if h % 2 == 0 else kv_all_o_d
        cc_need = cc_kv_e if h % 2 == 0 else cc_kv_o
        ksem = s_kva if par == 0 else s_kvb
        vsem = s_va if par == 0 else s_vb
        b.wait("sync", s_cc, cc_need)
        b.wait("gpsimd", s_cc, cc_need)
        b.wait("scalar", s_cc, cc_need)
        if pos >= 2:
            prev = av_pe[(pos - 2) * SC + SC - 1]
            b.wait("sync", c_pe, prev)
            b.wait("gpsimd", c_pe, prev)
            b.wait("scalar", c_pe, prev)
        b.dma("sync", ksem, KH[:, par * NSLOT:(par + 1) * NSLOT],
              bass.AP(kv_all, hb * 768 * 1024, [[NSLOT, 128], [1, NSLOT]]))
        b.dma("gpsimd", vsem, VH[:, par * SC * 128:(par + 1) * SC * 128],
              bass.AP(kv_all, hb * 768 * 1024 + 384 * 1024,
                      [[SC * 128, 128], [128, SC], [1, 128]]))
        b.dma("scalar", ksem, QH[:, par * T:(par + 1) * T], q_d.ap()[h * 128:(h + 1) * 128, :])
        ksem_at[pos] = (ksem, ksem.n)
        vsem_at[pos] = (vsem, vsem.n)

    def att_scores(j):
        pos, kcc = divmod(j, SC)
        par = pos % 2
        if kcc == 0:
            ksem, kn = ksem_at[pos]
            b.wait("tensor", ksem, kn)
        pst = PS[j % 2]
        if j >= 2:
            b.wait("tensor", c_ac, ex_ac[j - 2])
        for hh in range(2):
            b.op("tensor", lambda t, o=pst[:, hh * 512:hh * 512 + 512],
                 lw=KH[:, par * NSLOT + kcc * 128:par * NSLOT + kcc * 128 + 128],
                 i=QH[:, par * T + hh * 512:par * T + hh * 512 + 512]:
                 t.matmul(o, lw, i, start=True, stop=True),
                 inc=(c_pe if hh == 1 else None))
        sc_pe[j] = c_pe.n

    def att_exp(j):
        ring = j % NRING
        pst = PS[j % 2]
        b.wait("scalar", c_pe, sc_pe[j])
        if j >= NRING:
            b.wait("scalar", c_pe, av_pe[j - NRING])
        b.op("scalar", lambda s, o=PR[:, ring * T:(ring + 1) * T], i=pst[:, 0:1024]:
             s.activation(o, i, AF.Exp, bias=NegC[:, :], scale=1.0), inc=c_ac)
        ex_ac[j] = c_ac.n

    def att_denav(j):
        pos, kcc = divmod(j, SC)
        par = pos % 2
        ring = j % NRING
        b.wait("tensor", c_ac, ex_ac[j])
        if kcc == 0:
            vsem, vn = vsem_at[pos]
            b.wait("tensor", vsem, vn)
        if kcc == 0 and pos >= 1:
            b.wait("tensor", c_ve, epi1_ve[pos - 1])
        for hh in range(2):
            b.op("tensor", lambda t, o=PSc[:, hh * 512:hh * 512 + 512],
                 i=PR[:, ring * T + hh * 512:ring * T + hh * 512 + 512],
                 stt=(kcc == 0), spp=(kcc == SC - 1):
                 t.matmul(o, OnesSqB[:, :], i, start=stt, stop=spp),
                 inc=(c_pe if (kcc == SC - 1 and hh == 1) else None))
        if kcc == SC - 1:
            den_pe[pos] = c_pe.n
        if kcc == 0 and pos >= 1:
            b.wait("tensor", c_ve, epi_ve[pos - 1])
        for hh in range(2):
            b.op("tensor", lambda t, o=PSd[:, hh * 512:hh * 512 + 512],
                 lw=VH[:, par * SC * 128 + kcc * 128:par * SC * 128 + kcc * 128 + 128],
                 i=PR[:, ring * T + hh * 512:ring * T + hh * 512 + 512],
                 stt=(kcc == 0), spp=(kcc == SC - 1):
                 t.matmul(o, lw, i, start=stt, stop=spp),
                 inc=(c_pe if hh == 1 else None))
        av_pe[j] = c_pe.n

    def att_epilogue(pos):
        h = ORD[pos]
        b.wait("vector", c_pe, den_pe[pos])
        b.op("vector", lambda v: v.tensor_scalar(Sq2[:, 0:1024], PSc[:, 0:1024], PADC, None, ALU.add),
             inc=c_ve)
        epi1_ve[pos] = c_ve.n
        b.op("vector", lambda v: v.reciprocal(SqB[:, 0:1024], Sq2[:, 0:1024]))
        b.wait("vector", c_pe, av_pe[pos * SC + SC - 1])
        b.op("vector", lambda v, o=A[:, h * T:(h + 1) * T]:
             v.tensor_tensor(o, PSd[:, 0:1024], SqB[:, 0:1024], ALU.mult), inc=c_ve)
        epi_ve[pos] = c_ve.n

    att_load(0)
    att_load(1)
    for j in range(NJ + LAG):
        if j < NJ:
            pos, kcc = divmod(j, SC)
            if kcc == LAG and pos >= 1 and pos + 1 < NH:
                att_load(pos + 1)
            att_scores(j)
            att_exp(j)
        jj = j - LAG
        if jj >= 0:
            att_denav(jj)
            if jj % SC == SC - 1:
                att_epilogue(jj // SC)
    att_scope.close()

    XC = XS
    wo_pe = {}
    wo_ve = {}
    q_sq_pe = {}
    x1_written = {}
    ln2_ac0 = None
    for oc in range(16):
        par = oc % 2
        buf = WsA[:, par * 2048:(par + 1) * 2048]
        wsem = s_wa if par == 0 else s_wb
        xsem = s_x1a if par == 0 else s_x1b
        sps = s_spa if par == 0 else s_spb
        if oc >= 2:
            b.wait("gpsimd", c_pe, wo_pe[oc - 2])
            b.wait("sync", c_pe, q_sq_pe[oc - 2])
            b.wait("sync", sps, sps.n)
        b.dma("gpsimd", wsem, buf, wo_h[oc])
        b.dma("sync", xsem, XC[:, par * T:(par + 1) * T], xT_h[oc * 128:(oc + 1) * 128, :])
        b.wait("tensor", wsem, wsem.n)
        if oc == 0:
            b.wait("tensor", c_ve, epi_ve[NH - 1])
        pst = PS[par]
        for kcc in range(16):
            for hh in range(2):
                b.op("tensor", lambda t, o=pst[:, hh * 512:hh * 512 + 512],
                     lw=buf[:, kcc * 128:kcc * 128 + 128],
                     i=A[:, kcc * T + hh * 512:kcc * T + hh * 512 + 512],
                     stt=(kcc == 0), spp=(kcc == 15):
                     t.matmul(o, lw, i, start=stt, stop=spp),
                     inc=(c_pe if (kcc == 15 and hh == 1) else None))
        wo_pe[oc] = c_pe.n
        b.wait("vector", c_pe)
        b.wait("vector", xsem, xsem.n)
        xcv = XC[:, par * T:(par + 1) * T]
        b.op("vector", lambda v, o=xcv, i=pst[:, 0:1024], bi=BiasO[:, oc:oc + 1]:
             v.scalar_tensor_tensor(o, i, bi, o, ALU.add, ALU.add), inc=c_ve)
        wo_ve[oc] = c_ve.n
        b.wait("scalar", c_ve, wo_ve[oc])
        b.dma("scalar", sps, x1_d.ap()[oc * 128:(oc + 1) * 128, :], xcv)
        b.op("scalar", lambda s, o=Sq2[:, 0:1024], i=xcv: s.activation(o, i, AF.Square), inc=c_ac)
        if ln2_ac0 is None:
            ln2_ac0 = c_ac.n - 1
        b.wait("tensor", c_ve, wo_ve[oc])
        for hh in range(2):
            b.op("tensor", lambda t, o=PSc[:, hh * 512:hh * 512 + 512],
                 i=xcv[:, hh * 512:hh * 512 + 512], stt=(oc == 0), spp=(oc == 15):
                 t.matmul(o, OnesSq[:, :], i, start=stt, stop=spp))
        b.wait("tensor", c_ac, ln2_ac0 + oc + 1)
        for hh in range(2):
            b.op("tensor", lambda t, o=PSd[:, hh * 512:hh * 512 + 512],
                 i=Sq2[:, hh * 512:hh * 512 + 512], stt=(oc == 0), spp=(oc == 15):
                 t.matmul(o, OnesSq[:, :], i, start=stt, stop=spp),
                 inc=c_pe)
        q_sq_pe[oc] = c_pe.n
    b.wait("vector", c_pe)
    ln2_fin_ve = emit_ln_finalize(PSc[:, 0:1024], PSd[:, 0:1024])
    XL = XS
    ln2_ve = {}
    for q in ("sync", "scalar"):
        b.wait(q, c_pe)
        b.wait(q, s_spa)
        b.wait(q, s_spb)
    for cch in range(KC):
        par = cch % 2
        xsem = s_x1a if par == 0 else s_x1b
        xq = "sync" if par == 0 else "scalar"
        if cch >= 2:
            b.wait(xq, c_ve, ln2_ve[cch - 2])
        b.dma(xq, xsem, XL[:, par * T:(par + 1) * T], x1_d.ap()[cch * 128:(cch + 1) * 128, :])
        b.wait("vector", xsem, xsem.n)
        xc = XL[:, par * T:(par + 1) * T]
        tmp = SqB[:, par * 1024:par * 1024 + 1024]
        b.op("vector", lambda v, a=tmp, x=xc: v.tensor_tensor(a, x, RstdB[:, :], ALU.mult))
        b.op("vector", lambda v, a=tmp, d=A[:, cch * T:(cch + 1) * T]:
             v.tensor_tensor(d, a, MrB[:, :], ALU.add), inc=c_ve)
        ln2_ve[cch] = c_ve.n
    ln2_done_ve = c_ve.n

    mid_scope.close()
    ffn_scope = ExitStack()
    Hid = ffn_scope.enter_context(nc.sbuf_tensor("Hid", [128, HC * T], BF16, side="right"))
    hid_ve = {}
    ffn_pe = {}
    silu_ac = {}
    b.wait("tensor", c_ve, ln2_done_ve)
    for hc in range(HC):
        par = hc % 2
        b1 = WsA[:, par * 2048:(par + 1) * 2048]
        b3 = WsB[:, par * 2048:(par + 1) * 2048]
        s1 = s_wa if par == 0 else s_wb
        s3 = s_wc if par == 0 else s_wd
        if hc >= 2:
            b.wait("gpsimd", c_pe, ffn_pe[hc - 2])
        b.dma("gpsimd", s1, b1, w1_h[hc])
        b.dma("gpsimd", s3, b3, w3_h[hc])
        b.wait("tensor", s1, s1.n)
        b.wait("tensor", s3, s3.n)
        psg = PS[2 * par]
        psl = PS[2 * par + 1]
        if hc >= 2:
            b.wait("tensor", c_ve, hid_ve[hc - 2])
            b.wait("tensor", c_ac, silu_ac[hc - 2])
        for kcc in range(16):
            for hh in range(2):
                b.op("tensor", lambda t, o=psg[:, hh * 512:hh * 512 + 512],
                     lw=b1[:, kcc * 128:kcc * 128 + 128],
                     i=A[:, kcc * T + hh * 512:kcc * T + hh * 512 + 512],
                     stt=(kcc == 0), spp=(kcc == 15):
                     t.matmul(o, lw, i, start=stt, stop=spp),
                     inc=(c_pe if (kcc == 15 and hh == 1) else None))
        gate_pe = c_pe.n
        for kcc in range(16):
            for hh in range(2):
                b.op("tensor", lambda t, o=psl[:, hh * 512:hh * 512 + 512],
                     lw=b3[:, kcc * 128:kcc * 128 + 128],
                     i=A[:, kcc * T + hh * 512:kcc * T + hh * 512 + 512],
                     stt=(kcc == 0), spp=(kcc == 15):
                     t.matmul(o, lw, i, start=stt, stop=spp),
                     inc=(c_pe if (kcc == 15 and hh == 1) else None))
        ffn_pe[hc] = c_pe.n
        b.wait("scalar", c_pe, gate_pe)
        b.op("scalar", lambda s, o=QSt[:, par * T:(par + 1) * T], i=psg[:, 0:1024]:
             s.activation(o, i, AF.Silu), inc=c_ac)
        silu_ac[hc] = c_ac.n
        b.wait("vector", c_ac, silu_ac[hc])
        b.wait("vector", c_pe, ffn_pe[hc])
        b.op("vector", lambda v, o=Hid[:, hc * T:(hc + 1) * T], g=QSt[:, par * T:(par + 1) * T],
             l=psl[:, 0:1024]: v.tensor_tensor(o, g, l, ALU.mult), inc=c_ve)
        hid_ve[hc] = c_ve.n

    ws_scope.close()
    w2s_scope = ExitStack()
    W2S = w2s_scope.enter_context(nc.sbuf_tensor("W2S", [128, 2 * 8 * 256], BF16))
    X1R = XS
    YSt = w2s_scope.enter_context(nc.sbuf_tensor("YSt", [128, 1024], F32))
    b.wait("sync", s_cc, cc_w2)
    b.wait("sync", s_spa)
    b.wait("sync", s_spb)
    b.wait("sync", c_ve, ln2_done_ve)
    ld_i = 0
    w2ld_pe = {}
    ev_ve = {}
    pair_free_ve = {}
    ytr_ve = {}
    ytr_i = 0
    for r in range(NCORE):
        pair = (PS[0], PS[1])
        w2src = w2t_all_d.ap()[r * 128 * HC:(r + 1) * 128 * HC, :].rearrange(
            "(p hc) d -> p (hc d)", p=128)
        for oc2 in range(2):
            xsem = s_x1a if oc2 == 0 else s_x1b
            if r >= 1:
                b.wait("sync", c_ve, ev_ve[(r - 1, oc2)])
            b.dma("sync", xsem, X1R[:, oc2 * 1024:(oc2 + 1) * 1024],
                  x1_d.ap()[(r * 2 + oc2) * 128:(r * 2 + oc2 + 1) * 128, :])
        for lg in range(8):
            parw = ld_i % 2
            wsem = s_w2sA if parw == 0 else s_w2sB
            if ld_i >= 2:
                b.wait("sync", c_pe, w2ld_pe[ld_i - 2])
            elif ld_i == 0:
                b.wait("sync", c_pe, ffn_pe[HC - 1])
            b.dma("sync", wsem, W2S[:, parw * 2048:(parw + 1) * 2048],
                  w2src[:, lg * 2048:(lg + 1) * 2048])
            b.wait("tensor", wsem, wsem.n)
            if r == 0 and lg == 0:
                b.wait("tensor", c_ve, hid_ve[HC - 1])
                b.wait("tensor", c_ac, silu_ac[HC - 1])
            if lg == 0 and r >= 1:
                b.wait("tensor", c_ve, ev_ve[(r - 1, 1)])
            for hcl in range(8):
                hcg = lg * 8 + hcl
                for oc2 in range(2):
                    for th in range(2):
                        b.op("tensor", lambda t, o=pair[oc2][:, th * 512:(th + 1) * 512],
                             lw=W2S[:, parw * 2048 + hcl * 256 + oc2 * 128:
                                    parw * 2048 + hcl * 256 + oc2 * 128 + 128],
                             i=Hid[:, hcg * T + th * 512:hcg * T + th * 512 + 512],
                             stt=(hcg == 0), spp=(hcg == HC - 1):
                             t.matmul(o, lw, i, start=stt, stop=spp),
                             inc=(c_pe if (hcl == 7 and oc2 == 1 and th == 1) else None))
            w2ld_pe[ld_i] = c_pe.n
            ld_i += 1
        for oc2 in range(2):
            d2c = r * 2 + oc2
            xsem = s_x1a if oc2 == 0 else s_x1b
            ysem = s_sta
            b.wait("vector", c_pe, w2ld_pe[ld_i - 1])
            b.wait("vector", xsem, xsem.n)
            yf = SqB[:, oc2 * 1024:(oc2 + 1) * 1024]
            b.op("vector", lambda v, o=yf, i=pair[oc2][:, 0:1024],
                 x=X1R[:, oc2 * 1024:(oc2 + 1) * 1024]:
                 v.tensor_tensor(o, i, x, ALU.add), inc=c_ve)
            ev_ve[(r, oc2)] = c_ve.n
            b.wait("tensor", c_ve, ev_ve[(r, oc2)])
            if r * 2 + oc2 >= 1:
                b.wait("vector", ysem, ysem.n)
                b.wait("tensor", c_ve, ytr_ve[ytr_i - 1])
            for j in range(8):
                pstr = PS[2 + j // 4][:, (j % 4) * 128:(j % 4) * 128 + 128]
                b.op("tensor", lambda t, o=pstr, i=yf[:, j * 128:j * 128 + 128]:
                     t.transpose(o, i, Ident[:, :]), inc=(c_pe if j == 7 else None))
            b.wait("vector", c_pe)
            for j in range(8):
                pstr = PS[2 + j // 4][:, (j % 4) * 128:(j % 4) * 128 + 128]
                b.op("vector", lambda v, o=YSt[:, j * 128:(j + 1) * 128],
                     i=pstr: v.tensor_copy(o, i), inc=(c_ve if j == 7 else None))
            ytr_ve[ytr_i] = c_ve.n
            ytr_i += 1
            b.wait("scalar", c_ve, ytr_ve[ytr_i - 1])
            for j in range(8):
                b.dma("scalar", ysem,
                      y_h[j * 128:(j + 1) * 128, d2c * 128:(d2c + 1) * 128],
                      YSt[:, j * 128:(j + 1) * 128])
        pair_free_ve[r] = ytr_ve[ytr_i - 1]
    w2s_scope.close()
    ffn_scope.close()
    b.wait("scalar", s_sta, s_sta.n)
    b.wait("scalar", s_stb, s_stb.n)

    if verbose:
        print("sbuf bytes remaining:", nc.sbuf_bytes_remaining)
        print("instr counts:", {e: len(b.q[e]) for e in ENGS})

    with nc.Block() as block:
        b.replay(block)
    st.close()
    return nc


def _hadamard(n):
    h = np.array([[1.0]], dtype=np.float64)
    while h.shape[0] < n:
        h = np.block([[h, h], [h, -h]])
    return h


def _prep_inputs(inputs):
    f32 = np.float32
    x = np.asarray(inputs["L0_attn_x"], f32).reshape(S, D)
    Wq = np.asarray(inputs["L0_attn_Wq"], f32)
    bq = np.asarray(inputs["L0_attn_bq"], f32)
    Wk = np.asarray(inputs["L0_attn_Wk"], f32)
    bk = np.asarray(inputs["L0_attn_bk"], f32)
    Wv = np.asarray(inputs["L0_attn_Wv"], f32)
    bv = np.asarray(inputs["L0_attn_bv"], f32)
    Wo = np.asarray(inputs["L0_attn_Wo"], f32)
    bo = np.asarray(inputs["L0_attn_bo"], f32)
    Wi = np.asarray(inputs["L0_attn_Wi"], f32)
    bi = np.asarray(inputs["L0_attn_bi"], f32)
    W1 = np.asarray(inputs["L0_ffn_W1"], f32)
    W2 = np.asarray(inputs["L0_ffn_W2"], f32)
    W3 = np.asarray(inputs["L0_ffn_W3"], f32)

    def tile_oc(W):
        return np.ascontiguousarray(
            W.reshape(16, 128, 16, 128).transpose(2, 1, 0, 3).reshape(16, 128, 16 * 128))

    def tile_hc(Wh):
        return np.ascontiguousarray(
            Wh.reshape(64, 128, 16, 128).transpose(0, 3, 2, 1).reshape(64, 128, 16 * 128))

    wq_t = tile_oc(Wq)
    wo_t = tile_oc(Wo)
    w1_t = tile_hc(W1)
    w3_t = tile_hc(W3)
    bq_t = np.ascontiguousarray(bq.reshape(16, 128).T)
    bo_t = np.ascontiguousarray(bo.reshape(16, 128).T)
    bk_t = np.ascontiguousarray(bk.reshape(16, 128).T)
    wi_t = np.ascontiguousarray(Wi.reshape(16, 128).T)
    h128 = _hadamard(128).astype(ml_dtypes.bfloat16)
    ident = np.eye(128, dtype=f32)
    ones_c = np.ones((128, 1), f32)
    ones_cb = np.ones((128, 1), ml_dtypes.bfloat16)
    ones_r = np.ones((1, 128), f32)
    ones_rb = np.ones((1, 128), ml_dtypes.bfloat16)
    ones_sq = np.ones((128, 128), f32)
    ones_sqb = np.ones((128, 128), ml_dtypes.bfloat16)
    rind = np.zeros((128, 8), f32)
    for p in range(128):
        rind[p, p // 16] = 1.0
    iota384 = np.arange(REG, dtype=f32).reshape(1, REG)
    iota16 = np.broadcast_to(np.arange(1, 17, dtype=f32), (128, 16)).copy()

    xT = np.ascontiguousarray(x.T)
    maps = []
    for c in range(NCORE):
        sub0 = 2 * c
        m = dict(
            xT=np.ascontiguousarray(xT[:, c * T:(c + 1) * T]),
            xrows=np.ascontiguousarray(x[c * T:(c + 1) * T, :]),
            wq_t=wq_t, wo_t=wo_t, w1_t=w1_t, w3_t=w3_t,
            wk_bh=np.ascontiguousarray(
                Wk[:, sub0 * 128:(sub0 + 2) * 128].reshape(16, 128, 256)
                .transpose(1, 0, 2).reshape(128, 16 * 256)),
            wv_bh=np.ascontiguousarray(
                Wv[:, sub0 * 128:(sub0 + 2) * 128].reshape(16, 128, 256)
                .transpose(1, 0, 2).reshape(128, 16 * 256)),
            w2_slt=np.ascontiguousarray(
                W2[c * 256:(c + 1) * 256, :].reshape(256, 64, 128)
                .transpose(2, 1, 0).reshape(128, 64 * 256)),
            wi_t=wi_t, bq_t=bq_t, bo_t=bo_t,
            bk_bh=np.ascontiguousarray(bk_t[:, sub0:sub0 + 2]),
            bv_row=np.ascontiguousarray(bv[sub0 * 128:(sub0 + 2) * 128].reshape(1, 256)),
            bi=np.asarray(bi, f32).reshape(1, 1),
            ones_c=ones_c, ones_cb=ones_cb, ones_r=ones_r, ones_rb=ones_rb, ones_sq=ones_sq,
            ones_sqb=ones_sqb,
            ident=ident, h128=h128, rind=rind, iota384=iota384, iota16=iota16,
        )
        maps.append(m)
    return maps


_NC_CACHE = {}


def kernel(**inputs):
    maps = _prep_inputs(inputs)
    if "nc" not in _NC_CACHE:
        _NC_CACHE["nc"] = build()
    nc = _NC_CACHE["nc"]
    res = run_bass_kernel_spmd(nc, maps, core_ids=list(range(NCORE)))
    y = np.concatenate([np.asarray(res.results[c]["y"]) for c in range(NCORE)], axis=0)
    return y.reshape(1, S, D)


if __name__ == "__main__":
    nc = build(verbose=True)
    print("built ok")


# version 24
# speedup vs baseline: 1.0221x; 1.0221x over previous
import math
import numpy as np
import ml_dtypes
from contextlib import ExitStack

import concourse.bass as bass
import concourse.mybir as mybir
from concourse.bass_utils import run_bass_kernel_spmd
from concourse.bass import IndirectOffsetOnAxis

F32 = mybir.dt.float32
BF16 = mybir.dt.bfloat16
I32 = mybir.dt.int32
AF = mybir.ActivationFunctionType
ALU = mybir.AluOpType
AX = mybir.AxisListType.X

NCORE = 8
S, D, H = 8192, 2048, 8192
T = S // NCORE
KC = D // 128
HC = H // 128
NH, HD = 16, 128
REG = 384
NSLOT = NCORE * REG
SC = NSLOT // 128
RANK_T = 1638
CSTAB = 16.0
QSCALE = 1.0 / math.sqrt(HD)
PADC = float((S - NSLOT) * math.exp(-CSTAB))
EPS = 1e-5
NBIS17 = 11

ENGS = ("sync", "gpsimd", "tensor", "scalar", "vector")


class Sem:
    def __init__(self, stack, nc, name):
        self.h = stack.enter_context(nc.semaphore(name))
        self.n = 0


class B:
    def __init__(self, nc, stack):
        self.nc = nc
        self.q = {e: [] for e in ENGS}
        self.stack = stack

    def sem(self, name):
        return Sem(self.stack, self.nc, name)

    def em(self, eng, fn):
        self.q[eng].append(fn)

    trace = []

    def dma(self, eng, sem, out, in_):
        self.trace.append(("dma", eng, sem.h.name, sem.n + 16))
        self.em(eng, lambda E, o=out, i=in_, s=sem.h: E.dma_start(out=o, in_=i).then_inc(s, 16))
        sem.n += 16
        return sem.n

    def dmac(self, eng, sem, fn):
        self.em(eng, lambda E, f=fn, s=sem.h: f(E).then_inc(s, 16))
        sem.n += 16
        return sem.n

    def wait(self, eng, sem, val=None):
        v = sem.n if val is None else val
        self.trace.append(("wait", eng, sem.h.name, v))
        self.em(eng, lambda E, s=sem.h, v=v: E.wait_ge(s, v))

    def op(self, eng, fn, inc=None, k=1):
        if inc is None:
            self.em(eng, fn)
        else:
            self.em(eng, lambda E, f=fn, s=inc.h, kk=k: f(E).then_inc(s, kk))
            inc.n += k

    def replay(self, block):
        qs = self.q

        @block.sync
        def _(e):
            for f in qs["sync"]:
                f(e)

        @block.gpsimd
        def _(e):
            for f in qs["gpsimd"]:
                f(e)

        @block.tensor
        def _(e):
            for f in qs["tensor"]:
                f(e)

        @block.scalar
        def _(e):
            for f in qs["scalar"]:
                f(e)

        @block.vector
        def _(e):
            for f in qs["vector"]:
                f(e)


def build(verbose=False):
    nc = bass.Bass("TRN2", target_bir_lowering=False)
    st = ExitStack()
    b = B(nc, st)

    def inp(name, shape, dt=F32):
        return nc.dram_tensor(name, shape, dt, kind="ExternalInput").ap()

    xT_h = inp("xT", [D, T])
    xrows_h = inp("xrows", [T, D])
    wq_h = inp("wq_t", [16, 128, 16 * 128])
    wo_h = inp("wo_t", [16, 128, 16 * 128])
    wkbh_h = inp("wk_bh", [128, 16 * 256])
    wvbh_h = inp("wv_bh", [128, 16 * 256])
    w1_h = inp("w1_t", [64, 128, 16 * 128])
    w3_h = inp("w3_t", [64, 128, 16 * 128])
    w2_h = inp("w2_slt", [128, HC * 256])
    wi_h = inp("wi_t", [128, 16])
    bq_h = inp("bq_t", [128, 16])
    bo_h = inp("bo_t", [128, 16])
    bkbh_h = inp("bk_bh", [128, 2])
    bvrow_h = inp("bv_row", [1, 256])
    bi_h = inp("bi", [1, 1])
    ones_c_h = inp("ones_c", [128, 1])
    ones_cb_h = inp("ones_cb", [128, 1], BF16)
    ones_r_h = inp("ones_r", [1, 128])
    ones_rb_h = inp("ones_rb", [1, 128], BF16)
    ones_sq_h = inp("ones_sq", [128, 128])
    ones_sqb_h = inp("ones_sqb", [128, 128], BF16)
    ident_h = inp("ident", [128, 128])
    h128_h = inp("h128", [128, 128], BF16)
    rind_h = inp("rind", [128, 8])
    iota384_h = inp("iota384", [1, REG])
    iota16_h = inp("iota16", [128, 16])

    y_t = nc.dram_tensor("y", [T, D], F32, kind="ExternalOutput")
    y_h = y_t.ap()

    imp_own_d = nc.dram_tensor("imp_own_d", [T, 1], F32)
    imp_all_d = nc.dram_tensor("imp_all_d", [S, 1], F32)
    lng_own_d = nc.dram_tensor("lng_own_d", [128 * 16, REG], BF16)
    lng_all_d = nc.dram_tensor("lng_all_d", [NCORE * 16 * 128, REG], BF16)
    kv_own_e_d = nc.dram_tensor("kv_own_e_d", [768, 1024], BF16)
    kv_own_o_d = nc.dram_tensor("kv_own_o_d", [768, 1024], BF16)
    kv_all_e_d = nc.dram_tensor("kv_all_e_d", [NCORE * 768, 1024], BF16)
    kv_all_o_d = nc.dram_tensor("kv_all_o_d", [NCORE * 768, 1024], BF16)
    gath_d = nc.dram_tensor("gath_d", [512, D], F32)
    w2t_own_d = nc.dram_tensor("w2t_own_d", [128 * HC, 256], BF16)
    w2t_all_d = nc.dram_tensor("w2t_all_d", [NCORE * 128 * HC, 256], BF16)
    q_d = nc.dram_tensor("q_d", [NH * 128, T], BF16)
    x1_d = nc.dram_tensor("x1_d", [KC * 128, T], F32)

    PSa = nc.alloc_psum_tensor("PSa", [128, 1024], F32)
    PSb = nc.alloc_psum_tensor("PSb", [128, 1024], F32)
    PSc = nc.alloc_psum_tensor("PSc", [128, 1024], F32)
    PSd = nc.alloc_psum_tensor("PSd", [128, 1024], F32)
    PS = [PSa, PSb, PSc, PSd]

    A = nc.alloc_sbuf_tensor("A", [128, KC * T], BF16)
    SqB = nc.alloc_sbuf_tensor("SqB", [128, 2048], F32)
    QSt = nc.alloc_sbuf_tensor("QSt", [128, 2 * T], BF16)
    XS = nc.alloc_sbuf_tensor("XS", [128, 2048], F32)
    Ident = nc.alloc_sbuf_tensor("Ident", [128, 128], F32)
    ws_scope = ExitStack()
    WsA = ws_scope.enter_context(nc.sbuf_tensor("WsA", [128, 2 * 2048], BF16))
    WsB = ws_scope.enter_context(nc.sbuf_tensor("WsB", [128, 2 * 2048], BF16))
    mid_scope = ExitStack()

    def mid(name, shape, dt=F32):
        return mid_scope.enter_context(nc.sbuf_tensor(name, shape, dt))

    Sq2 = mid("Sq2", [128, 1024])
    RowS = mid("RowS", [1, 2048])
    RstdB = mid("RstdB", [128, 1024])
    MrB = mid("MrB", [128, 1024])
    OnesSq = mid("OnesSq", [128, 128])
    OnesSqB = mid("OnesSqB", [128, 128], BF16)
    OnesR = mid("OnesR", [1, 128])
    OnesRb = mid("OnesRb", [1, 128], BF16)
    OnesC = mid("OnesC", [128, 1])
    OnesCb = mid("OnesCb", [128, 1], BF16)
    Rind = mid("Rind", [128, 8])
    Iota384 = mid("Iota384", [1, REG])
    BiasQ = mid("BiasQ", [128, 16])
    BiasQs = mid("BiasQs", [128, 16])
    BiasO = mid("BiasO", [128, 16])
    BiasKbh = mid("BiasKbh", [128, 2])
    BiasVr = mid("BiasVr", [1, 256])
    BvB = mid("BvB", [128, 256])
    BiT = mid("BiT", [1, 1])
    Wi = mid("Wi", [128, 16])
    Wib = mid("Wib", [128, 16], BF16)
    H128 = mid("H128", [128, 128], BF16)
    EpsCol = mid("EpsCol", [128, 1])
    NegC = mid("NegC", [128, 1])
    w2t_scope = ExitStack()
    W2T = w2t_scope.enter_context(nc.sbuf_tensor("W2T", [128, HC * 256], BF16))
    ph2_scope = ExitStack()

    def ph2(name, shape, dt=F32):
        return ph2_scope.enter_context(nc.sbuf_tensor(name, shape, dt))

    RowB = ph2("RowB", [1, NSLOT], BF16)
    ImpRow = ph2("ImpRow", [1, T], F32)
    Validb = ph2("Validb", [128, NSLOT], BF16)
    IdxT = ph2("IdxT", [128, 8], F32)
    IdxTi = ph2("IdxTi", [128, 8], I32)
    DmI = ph2("DmI", [128, 8], I32)
    PIota = ph2("PIota", [128, 1])
    Iota16 = ph2("Iota16", [128, 16])
    Mids = ph2("Mids", [128, 16])
    LoB = ph2("LoB", [128, 16])
    HiB = ph2("HiB", [128, 16])
    CandL = ph2("CandL", [128, 16])
    PC16 = ph2("PC16", [128, 16])
    GeI = ph2("GeI", [128, 16], I32)
    ValidT = ph2("ValidT", [128, SC], F32)
    Bis = ph2("Bis", [128, 8], F32)
    BisI = ph2("BisI", [128, 1], I32)
    ImpAll = ph2("ImpAll", [128, 64], F32)
    Cnt = ph2("Cnt", [8, 1], F32)
    CntRow = ph2("CntRow", [1, 8], F32)

    s_cst = b.sem("s_cst")
    s_w2l = b.sem("s_w2l")
    s_wa = b.sem("s_wa")
    s_wb = b.sem("s_wb")
    s_wc = b.sem("s_wc")
    s_wd = b.sem("s_wd")
    s_xra = b.sem("s_xra")
    s_xrb = b.sem("s_xrb")
    s_rga = b.sem("s_rga")
    s_rgb = b.sem("s_rgb")
    s_reg = b.sem("s_reg")
    s_lng = b.sem("s_lng")
    s_kva = b.sem("s_kva")
    s_kvb = b.sem("s_kvb")
    s_out = b.sem("s_out")
    s_impst = b.sem("s_impst")
    s_lngst = b.sem("s_lngst")
    s_kvst = b.sem("s_kvst")
    s_w2st = b.sem("s_w2st")
    s_qsa = b.sem("s_qsa")
    s_qsb = b.sem("s_qsb")
    s_spa = b.sem("s_spa")
    s_spb = b.sem("s_spb")
    s_sta = b.sem("s_sta")
    s_stb = b.sem("s_stb")
    s_cc = b.sem("s_cc")
    s_dbg = b.sem("s_dbg")
    s_sp = b.sem("s_sp")
    s_x1a = b.sem("s_x1a")
    s_x1b = b.sem("s_x1b")
    s_w2sA = b.sem("s_w2sA")
    s_w2sB = b.sem("s_w2sB")
    s_st = b.sem("s_st")
    s_scat = b.sem("s_scat")
    s_zro = b.sem("s_zro")
    s_va = b.sem("s_va")
    s_vb = b.sem("s_vb")
    c_pe = b.sem("c_pe")
    c_ac = b.sem("c_ac")
    c_ve = b.sem("c_ve")
    c_gp = b.sem("c_gp")

    def opd(eng, fn, inc=None, k=1):
        b.em(eng, fn)
        if inc is None:
            b.em(eng, lambda E: E.drain())
        else:
            b.em(eng, lambda E, s=inc.h, kk=k: E.drain().then_inc(s, kk))
            inc.n += k

    for dst, src in [
        (OnesC[:, :], ones_c_h), (Rind[:, :], rind_h), (Ident[:, :], ident_h),
        (OnesSq[:, :], ones_sq_h), (OnesR[:, :], ones_r_h), (Iota384[:, :], iota384_h),
        (Iota16[:, :], iota16_h),
        (BiasQ[:, :], bq_h), (BiasO[:, :], bo_h), (BiasKbh[:, :], bkbh_h),
        (BiasVr[:, :], bvrow_h), (BiT[:, :], bi_h), (Wi[:, :], wi_h),
        (H128[:, :], h128_h), (OnesCb[:, :], ones_cb_h), (OnesRb[:, :], ones_rb_h),
        (OnesSqB[:, :], ones_sqb_h),
    ]:
        b.dma("sync", s_cst, dst, src)
    for i4 in range(4):
        b.dma("gpsimd", s_w2l, W2T[:, i4 * 4096:(i4 + 1) * 4096],
              w2_h[:, i4 * 4096:(i4 + 1) * 4096])
    b.wait("vector", s_cst)
    b.op("vector", lambda v: v.tensor_copy(Wib[:, :], Wi[:, :]))
    b.op("vector", lambda v: v.memset(Bis[:, :], 0.0))
    b.op("vector", lambda v: v.memset(BisI[:, :], 0))
    b.op("vector", lambda v: v.memset(IdxT[:, :], 0.0))
    b.op("vector", lambda v: v.memset(EpsCol[:, :], EPS))
    b.op("vector", lambda v: v.memset(NegC[:, :], -CSTAB))
    opd("vector", lambda v: v.tensor_scalar(BiasQs[:, :], BiasQ[:, :], QSCALE, None, ALU.mult), inc=c_ve)

    def emit_ln_finalize(ps_sum, ps_sq):
        opd("vector", lambda v: v.tensor_scalar(Sq2[:, :], ps_sum, 1.0 / D, None, ALU.mult))
        opd("vector", lambda v: v.tensor_scalar(RstdB[:, :], ps_sq, 1.0 / D, None, ALU.mult))
        opd("vector", lambda v: v.tensor_mul(MrB[:, :], Sq2[:, :], Sq2[:, :]))
        opd("vector", lambda v: v.tensor_sub(RstdB[:, :], RstdB[:, :], MrB[:, :]), inc=c_ve)
        b.wait("scalar", c_ve)
        opd("scalar", lambda s: s.activation(RstdB[:, :], RstdB[:, :], AF.Ln, bias=EpsCol[:, :], scale=1.0))
        opd("scalar", lambda s: s.activation(RstdB[:, :], RstdB[:, :], AF.Exp, scale=-0.5), inc=c_ac)
        b.wait("vector", c_ac)
        b.op("vector", lambda v: v.scalar_tensor_tensor(MrB[:, :], Sq2[:, :], -1.0, RstdB[:, :],
                                                        ALU.mult, ALU.mult), inc=c_ve)
        return c_ve.n

    b.wait("tensor", s_cst)
    ac0, pe0 = c_ac.n, c_pe.n
    for cch in range(KC):
        par = cch % 2
        xsem = s_x1a if par == 0 else s_x1b
        xc = XS[:, par * 1024:(par + 1) * 1024]
        if cch >= 2:
            b.wait("sync", c_pe, pe0 + (cch - 2) + 1)
        b.dma("sync", xsem, xc, xT_h[cch * 128:(cch + 1) * 128, :])
        b.wait("scalar", xsem, xsem.n)
        sq = SqB[:, par * 1024:par * 1024 + 1024]
        if cch >= 2:
            b.wait("scalar", c_pe, pe0 + (cch - 2) + 1)
        b.op("scalar", lambda s, o=sq, i=xc: s.activation(o, i, AF.Square), inc=c_ac)
        b.wait("tensor", xsem, xsem.n)
        for hh in range(2):
            b.op("tensor", lambda t, o=PSa[:, hh * 512:hh * 512 + 512],
                 i=xc[:, hh * 512:hh * 512 + 512], stt=(cch == 0), spp=(cch == KC - 1):
                 t.matmul(o, OnesSq[:, :], i, start=stt, stop=spp))
        b.wait("tensor", c_ac, ac0 + cch + 1)
        for hh in range(2):
            b.op("tensor", lambda t, o=PSb[:, hh * 512:hh * 512 + 512],
                 i=sq[:, hh * 512:hh * 512 + 512], stt=(cch == 0), spp=(cch == KC - 1):
                 t.matmul(o, OnesSq[:, :], i, start=stt, stop=spp),
                 inc=(c_pe if hh == 1 else None))
    b.wait("vector", c_pe)
    ln1_fin_ve = emit_ln_finalize(PSa[:, 0:1024], PSb[:, 0:1024])
    gath_scope = ExitStack()
    Xtok = gath_scope.enter_context(nc.sbuf_tensor("Xtok", [128, 2 * D], F32))
    XgT = gath_scope.enter_context(nc.sbuf_tensor("XgT", [128, 16 * REG], F32))
    LnAll = gath_scope.enter_context(nc.sbuf_tensor("LnAll", [128, 16 * REG], BF16))
    b.wait("tensor", s_w2l)
    h_ve0 = c_ve.n
    for hc in range(HC):
        ps = PS[2 + (hc % 2)][:, 0:256]
        if hc >= 2:
            b.wait("tensor", c_ve, h_ve0 + (hc - 2) + 1)
        b.op("tensor", lambda t, o=ps, i=W2T[:, hc * 256:hc * 256 + 256]:
             t.matmul(o, H128[:, :], i, start=True, stop=True), inc=c_pe)
        b.wait("vector", c_pe)
        opd("vector", lambda v, o=W2T[:, hc * 256:hc * 256 + 256], i=ps:
            v.tensor_scalar(o, i, 1.0 / math.sqrt(H), None, ALU.mult), inc=c_ve)
    w2_h128_ve = c_ve.n
    ap_ve = {}
    for cch in range(KC):
        par = cch % 2
        xsem = s_x1a if par == 0 else s_x1b
        xq = "sync" if par == 0 else "scalar"
        xc = XS[:, par * 1024:(par + 1) * 1024]
        if cch >= 2:
            b.wait(xq, c_ve, ap_ve[cch - 2])
        else:
            b.wait(xq, c_ve, ln1_fin_ve)
        b.dma(xq, xsem, xc, xT_h[cch * 128:(cch + 1) * 128, :])
        b.wait("vector", xsem, xsem.n)
        dc = A[:, cch * T:(cch + 1) * T]
        tmp = SqB[:, par * 1024:par * 1024 + 1024]
        b.op("vector", lambda v, a=tmp, x=xc: v.tensor_tensor(a, x, RstdB[:, :], ALU.mult))
        b.op("vector", lambda v, a=tmp, d=dc: v.tensor_tensor(d, a, MrB[:, :], ALU.add),
             inc=c_ve)
        ap_ve[cch] = c_ve.n
    ln1_ve = c_ve.n


    b.wait("tensor", c_ve, ln1_ve)
    for cch in range(KC):
        for hh in range(2):
            b.op("tensor", lambda t, o=PSa[0:1, hh * 512:hh * 512 + 512],
                 i=A[:, cch * T + hh * 512:cch * T + hh * 512 + 512],
                 lw=Wib[:, cch:cch + 1], stt=(cch == 0), spp=(cch == KC - 1):
                 t.matmul(o, lw, i, start=stt, stop=spp),
                 inc=(c_pe if (cch == KC - 1 and hh == 1) else None))
    b.wait("vector", c_pe)
    b.op("vector", lambda v: v.tensor_scalar(ImpRow[:, :], PSa[0:1, 0:T], BiT[0:1, 0:1], None, ALU.add), inc=c_ve)
    b.wait("scalar", c_ve)
    b.dma("scalar", s_impst, imp_own_d.ap(), ImpRow[:, :])
    b.wait("gpsimd", s_impst, s_impst.n)
    b.op("gpsimd", lambda g: g.collective_compute(
        "AllGather", ALU.bypass, replica_groups=[list(range(NCORE))],
        ins=[imp_own_d.ap()], outs=[imp_all_d.ap()]), inc=s_cc, k=1)
    b.wait("gpsimd", s_cc, s_cc.n)
    b.dma("gpsimd", s_reg, ImpAll[:, :], imp_all_d.ap().rearrange("(p f) o -> p (f o)", p=128))
    b.op("gpsimd", lambda g: g.iota(PIota[:, :], [[1, 1]], channel_multiplier=1,
                                    allow_small_or_imprecise_dtypes=True), inc=c_gp)

    LO, HI, MID, PC = Bis[:, 0:1], Bis[:, 1:2], Bis[:, 2:3], Bis[:, 3:4]
    b.wait("vector", s_reg)
    b.op("vector", lambda v: v.memset(LO, -64.0))
    opd("vector", lambda v: v.memset(HI, 64.0), inc=c_ve)
    b.wait("tensor", c_ve)
    for it in range(42):
        opd("vector", lambda v: v.tensor_add(MID, LO, HI))
        opd("vector", lambda v: v.tensor_scalar(MID, MID, 0.5, None, ALU.mult))
        opd("vector", lambda v: v.tensor_scalar(Sq2[:, 0:64], ImpAll[:, :], MID[:, 0:1], None, ALU.is_gt))
        opd("vector", lambda v: v.reduce_sum(PC, Sq2[:, 0:64], axis=AX), inc=c_ve)
        b.wait("tensor", c_ve)
        b.op("tensor", lambda t: t.matmul(PSa[:, 0:1], OnesSq[:, :], PC, start=True, stop=True), inc=c_pe)
        b.wait("vector", c_pe)
        opd("vector", lambda v: v.tensor_scalar(BisI[:, 0:1], PSa[:, 0:1], float(RANK_T + 1), None, ALU.is_ge))
        opd("vector", lambda v: v.select(LO, BisI[:, 0:1], MID, LO))
        opd("vector", lambda v: v.tensor_scalar(BisI[:, 0:1], PSa[:, 0:1], float(RANK_T + 1), None, ALU.is_lt))
        opd("vector", lambda v: v.select(HI, BisI[:, 0:1], MID, HI), inc=c_ve)

    opd("vector", lambda v: v.tensor_scalar(Sq2[:, 0:64], ImpAll[:, :], HI[:, 0:1], None, ALU.is_gt))
    opd("vector", lambda v: v.reduce_sum(Bis[:, 4:5], Sq2[:, 0:64], axis=AX), inc=c_ve)
    b.wait("tensor", c_ve)
    b.op("tensor", lambda t: t.matmul(PSa[0:8, 0:1], Rind[:, :], Bis[:, 4:5], start=True, stop=True), inc=c_pe)
    b.wait("vector", c_pe)
    opd("vector", lambda v: v.tensor_scalar(Cnt[:, :], PSa[0:8, 0:1], float(REG), None, ALU.min), inc=c_ve)
    b.op("vector", lambda v: v.memset(Sq2[:, 0:128], 0.0))
    opd("vector", lambda v: v.tensor_copy(Sq2[0:8, 0:1], Cnt[:, :]), inc=c_ve)
    b.wait("tensor", c_ve)
    b.op("tensor", lambda t: t.transpose(PSb[:, 0:128], Sq2[:, 0:128], Ident[:, :]), inc=c_pe)
    b.wait("vector", c_pe)
    opd("vector", lambda v: v.tensor_copy(CntRow[:, :], PSb[0:1, 0:8]), inc=c_ve)
    for r in range(NCORE):
        if r < NCORE - 1:
            b.op("vector", lambda v, r=r: v.tensor_scalar(
                RowB[0:1, r * REG:(r + 1) * REG], Iota384[:, :], CntRow[0:1, r:r + 1], None, ALU.is_lt))
        else:
            opd("vector", lambda v, r=r: v.tensor_scalar(
                RowB[0:1, r * REG:(r + 1) * REG], Iota384[:, :], CntRow[0:1, r:r + 1], None, ALU.is_lt),
                inc=c_ve)
    b.wait("tensor", c_ve)
    vb0 = c_ve.n
    for hh in range(NSLOT // 512):
        ps = PS[hh % 2][:, 0:512]
        if hh >= 2:
            b.wait("tensor", c_ve, vb0 + (hh - 2) + 1)
        b.op("tensor", lambda t, o=ps, i=RowB[0:1, hh * 512:hh * 512 + 512]:
             t.matmul(o, OnesRb[:, :], i, start=True, stop=True), inc=c_pe)
        b.wait("vector", c_pe)
        b.op("vector", lambda v, o=Validb[:, hh * 512:hh * 512 + 512], i=ps: v.tensor_copy(o, i), inc=c_ve)

    def emit_row_to_cols(row_src_off, n_sub, dst_tile, dst_off):
        b.op("vector", lambda v: v.memset(Sq2[:, 0:1024], 0.0))
        opd("vector", lambda v, o=Sq2[0:1, 0:n_sub * 128],
            i=RowB[0:1, row_src_off:row_src_off + n_sub * 128]: v.tensor_copy(o, i), inc=c_ve)
        b.wait("tensor", c_ve)
        t0 = c_ve.n
        for sub in range(n_sub):
            ps = PSc[:, 128:256]
            if sub >= 1:
                b.wait("tensor", c_ve, t0 + sub)
            b.op("tensor", lambda t, o=ps, i=Sq2[:, sub * 128:sub * 128 + 128]:
                 t.transpose(o, i, Ident[:, :]), inc=c_pe)
            b.wait("vector", c_pe)
            opd("vector", lambda v, o=dst_tile[:, dst_off + sub:dst_off + sub + 1], i=ps[:, 0:1]:
                v.tensor_copy(o, i), inc=c_ve)

    for blk in range(3):
        emit_row_to_cols(blk * 1024, 8, ValidT, blk * 8)

    MRow = RowS[0:1, 1024:2048]
    opd("vector", lambda v: v.tensor_scalar(MRow, ImpRow[:, :], HI[0:1, 0:1], None, ALU.is_gt))
    opd("vector", lambda v: v.tensor_copy(SqB[0:1, 0:T], MRow))
    ca, cb = 0, 1024
    for si in range(10):
        d = 1 << si
        b.op("vector", lambda v, d=d, a=ca, o=cb:
             v.tensor_add(SqB[0:1, o + d:o + T], SqB[0:1, a + d:a + T], SqB[0:1, a:a + T - d]))
        opd("vector", lambda v, d=d, a=ca, o=cb:
            v.tensor_copy(SqB[0:1, o:o + d], SqB[0:1, a:a + d]))
        ca, cb = cb, ca
    b.op("vector", lambda v, a=ca: v.tensor_scalar(SqB[0:1, a:a + T], SqB[0:1, a:a + T],
                                                   -1.0 - 1024.0, None, ALU.add))
    b.op("vector", lambda v, a=ca: v.tensor_mul(SqB[0:1, a:a + T], SqB[0:1, a:a + T], MRow))
    b.op("vector", lambda v: v.memset(Sq2[:, 0:1024], 0.0), inc=c_ve)
    opd("vector", lambda v, a=ca: v.tensor_scalar(Sq2[0:1, 0:T], SqB[0:1, a:a + T],
                                                   1024.0, None, ALU.add), inc=c_ve)
    b.wait("tensor", c_ve)
    t0i = c_ve.n
    for sub in range(8):
        ps = PSc[:, 128:256]
        if sub >= 1:
            b.wait("tensor", c_ve, t0i + sub)
        b.op("tensor", lambda t, o=ps, i=Sq2[:, sub * 128:sub * 128 + 128]:
             t.transpose(o, i, Ident[:, :]), inc=c_pe)
        b.wait("vector", c_pe)
        opd("vector", lambda v, o=IdxT[:, sub:sub + 1], i=ps[:, 0:1]: v.tensor_copy(o, i), inc=c_ve)
    b.wait("vector", c_gp)
    opd("vector", lambda v: v.tensor_scalar(PIota[:, :], PIota[:, :], 384.0, None, ALU.add))
    opd("vector", lambda v: v.tensor_scalar(DmI[:, :], IdxT[:, :], 1024.0, None, ALU.is_ge))
    opd("vector", lambda v: v.tensor_scalar(Sq2[:, 8:16], OnesSq[:, 0:8], PIota[:, 0:1], None, ALU.mult))
    opd("vector", lambda v: v.select(IdxT[:, :], DmI[:, :], Sq2[:, 8:16], IdxT[:, :]))
    opd("vector", lambda v: v.tensor_copy(IdxTi[:, :], IdxT[:, :]), inc=c_ve)
    idx_ve = c_ve.n

    q_consumed_pe = {}
    q_evac_ac = {}

    def emit_q_oc(oc):
        par = oc % 2
        buf = WsA[:, par * 2048:(par + 1) * 2048]
        sem = s_wa if par == 0 else s_wb
        if oc == 0:
            b.wait("tensor", c_ve, g_fin_ve)
        if oc >= 2:
            b.wait("gpsimd", c_pe, q_consumed_pe[oc - 2])
        b.dma("gpsimd", sem, buf, wq_h[oc])
        b.wait("tensor", sem, sem.n)
        pst = PS[par]
        if oc >= 2:
            b.wait("tensor", c_ve, q_evac_ac[oc - 2])
        for kcc in range(16):
            for hh in range(2):
                b.op("tensor", lambda t, o=pst[:, hh * 512:hh * 512 + 512],
                     lw=buf[:, kcc * 128:kcc * 128 + 128],
                     i=A[:, kcc * T + hh * 512:kcc * T + hh * 512 + 512],
                     stt=(kcc == 0), spp=(kcc == 15):
                     t.matmul(o, lw, i, start=stt, stop=spp),
                     inc=(c_pe if (kcc == 15 and hh == 1) else None))
        q_consumed_pe[oc] = c_pe.n
        b.wait("vector", c_pe)
        qs = s_qsa if par == 0 else s_qsb
        if oc >= 2:
            b.wait("vector", qs, qs.n)
        b.op("vector", lambda v, o=QSt[:, par * T:(par + 1) * T], i=pst[:, 0:1024], bi=BiasQ[:, oc:oc + 1]:
             v.tensor_scalar(o, i, bi, QSCALE, ALU.add, ALU.mult), inc=c_ve)
        q_evac_ac[oc] = c_ve.n
        b.wait("scalar", c_ve)
        b.dma("scalar", qs, q_d.ap()[oc * 128:(oc + 1) * 128, :], QSt[:, par * T:(par + 1) * T])

    b.op("vector", lambda v: v.memset(Xtok[:, 0:D], 0.0), inc=c_ve)
    b.wait("sync", c_ve)
    for z in range(3):
        b.dma("sync", s_zro, gath_d.ap()[z * 128:(z + 1) * 128, :], Xtok[:, 0:D])
    b.wait("gpsimd", s_zro, s_zro.n)
    b.wait("gpsimd", c_ve, idx_ve)
    for tc in range(8):
        par = tc % 2
        xsem = s_xra if par == 0 else s_xrb
        dst = Xtok[:, par * D:(par + 1) * D]
        if tc >= 2:
            b.wait("sync", s_scat, 16 * (tc - 1))
        b.dma("sync", xsem, dst, xrows_h[tc * 128:(tc + 1) * 128, :])
        b.wait("gpsimd", xsem, xsem.n)
        b.dmac("gpsimd", s_scat, lambda g, src=dst, i=tc: g.indirect_dma_start(
            gath_d.ap(), IndirectOffsetOnAxis(ap=IdxTi[:, i:i + 1], axis=0), src, None))
    b.wait("sync", s_scat, s_scat.n)
    b.wait("tensor", c_ve, idx_ve)
    tr3_ve = {}
    tr3_pe = {}
    for r3 in range(3):
        dst = Xtok[:, (r3 % 2) * D:(r3 % 2 + 1) * D]
        if r3 == 2:
            b.wait("sync", c_pe, tr3_pe[0])
        b.dma("sync", s_xra, dst, gath_d.ap()[r3 * 128:(r3 + 1) * 128, :])
        xmark = s_xra.n
        if r3 >= 1:
            b.wait("tensor", c_ve, tr3_ve[r3 - 1])
        b.wait("tensor", s_xra, xmark)
        for kcc in range(16):
            ps = PS[2 + kcc // 8][:, (kcc % 8) * 128:(kcc % 8) * 128 + 128]
            b.op("tensor", lambda t, o=ps, i=dst[:, kcc * 128:kcc * 128 + 128]:
                 t.transpose(o, i, Ident[:, :]), inc=(c_pe if kcc == 15 else None))
        tr3_pe[r3] = c_pe.n
        b.wait("vector", c_pe)
        for kcc in range(16):
            ps = PS[2 + kcc // 8][:, (kcc % 8) * 128:(kcc % 8) * 128 + 128]
            b.op("vector", lambda v, o=XgT[:, kcc * REG + r3 * 128:kcc * REG + r3 * 128 + 128],
                 i=ps: v.tensor_copy(o, i), inc=(c_ve if kcc == 15 else None))
        tr3_ve[r3] = c_ve.n
    b.wait("tensor", c_ve, tr3_ve[2])
    b.wait("scalar", c_ve, tr3_ve[2])
    g_ac0 = c_ac.n
    gstat_pe = {}
    for kcc in range(16):
        sq = SqB[:, (kcc % 4) * 512:(kcc % 4) * 512 + REG]
        if kcc >= 4:
            b.wait("scalar", c_pe, gstat_pe[kcc - 4])
        b.op("scalar", lambda s, o=sq, i=XgT[:, kcc * REG:(kcc + 1) * REG]:
             s.activation(o, i, AF.Square), inc=c_ac)
        b.op("tensor", lambda t, o=PSa[:, 0:REG], i=XgT[:, kcc * REG:(kcc + 1) * REG],
             stt=(kcc == 0), spp=(kcc == 15):
             t.matmul(o, OnesSq[:, :], i, start=stt, stop=spp))
        b.wait("tensor", c_ac, g_ac0 + kcc + 1)
        b.op("tensor", lambda t, o=PSb[:, 0:REG], i=sq, stt=(kcc == 0), spp=(kcc == 15):
             t.matmul(o, OnesSq[:, :], i, start=stt, stop=spp), inc=c_pe)
        gstat_pe[kcc] = c_pe.n
    b.wait("vector", c_pe)
    R3, M3, S3 = RstdB[:, 0:REG], MrB[:, 0:REG], Sq2[:, 0:REG]
    b.op("vector", lambda v: v.tensor_scalar(S3, PSa[:, 0:REG], 1.0 / D, None, ALU.mult))
    b.op("vector", lambda v: v.tensor_scalar(R3, PSb[:, 0:REG], 1.0 / D, None, ALU.mult))
    b.op("vector", lambda v: v.tensor_mul(M3, S3, S3))
    b.op("vector", lambda v: v.tensor_sub(R3, R3, M3), inc=c_ve)
    b.wait("scalar", c_ve)
    b.op("scalar", lambda s: s.activation(R3, R3, AF.Ln, bias=EpsCol[:, :], scale=1.0))
    b.op("scalar", lambda s: s.activation(R3, R3, AF.Exp, scale=-0.5), inc=c_ac)
    b.wait("vector", c_ac)
    b.op("vector", lambda v: v.scalar_tensor_tensor(M3, S3, -1.0, R3, ALU.mult, ALU.mult),
         inc=c_ve)
    g_fin_ve = c_ve.n
    for kcc in range(16):
        sq = SqB[:, (kcc % 4) * 512:(kcc % 4) * 512 + REG]
        b.op("vector", lambda v, o=sq, i=XgT[:, kcc * REG:(kcc + 1) * REG]:
             v.tensor_tensor(o, i, R3, ALU.mult))
        b.op("vector", lambda v, o=LnAll[:, kcc * REG:(kcc + 1) * REG], i=sq:
             v.tensor_tensor(o, i, M3, ALU.add),
             inc=(c_ve if kcc == 15 else None))
    b.wait("scalar", c_ve)
    b.dma("scalar", s_lngst,
          bass.AP(lng_own_d, 0, [[16 * REG, 128], [REG, 16], [1, REG]]),
          LnAll[:, :])
    b.wait("gpsimd", s_lngst, s_lngst.n)
    b.op("gpsimd", lambda g: g.collective_compute(
        "AllGather", ALU.bypass, replica_groups=[list(range(NCORE))],
        ins=[lng_own_d.ap()], outs=[lng_all_d.ap()]), inc=s_cc, k=1)
    cc_lng = s_cc.n
    for oc in range(16):
        emit_q_oc(oc)

    b.wait("tensor", s_cst)
    b.op("tensor", lambda t: t.matmul(PSc[:, 256:512], OnesR[:, :], BiasVr[:, :], start=True, stop=True), inc=c_pe)
    b.wait("vector", c_pe)
    b.op("vector", lambda v: v.tensor_copy(BvB[:, :], PSc[:, 256:512]), inc=c_ve)

    gath_scope.close()
    kvb_scope = ExitStack()
    LnS = kvb_scope.enter_context(nc.sbuf_tensor("LnS", [128, 2 * 16 * REG], BF16))
    KOut = kvb_scope.enter_context(nc.sbuf_tensor("KOut", [128, 2 * NSLOT], BF16))
    VOut = kvb_scope.enter_context(nc.sbuf_tensor("VOut", [128, SC * 256], BF16))
    WkB = kvb_scope.enter_context(nc.sbuf_tensor("WkB", [128, 16 * 256], BF16))
    WvB = kvb_scope.enter_context(nc.sbuf_tensor("WvB", [128, 16 * 256], BF16))
    b.dma("gpsimd", s_wc, WkB[:, :], wkbh_h)
    b.dma("gpsimd", s_wc, WvB[:, :], wvbh_h)
    b.wait("gpsimd", s_cc, cc_lng)
    kv_pe = {}
    kv_ve = {}
    for r in range(NCORE):
        parl = r % 2
        lns = LnS[:, parl * 16 * REG:(parl + 1) * 16 * REG]
        if r >= 2:
            b.wait("gpsimd", c_pe, kv_pe[r - 2])
        b.dma("gpsimd", s_lng, lns,
              lng_all_d.ap()[r * 2048:(r + 1) * 2048, :].rearrange("(p kc) s -> p (kc s)", p=128))
        b.wait("tensor", s_lng, s_lng.n)
        if r == 0:
            b.wait("tensor", c_ve)
            b.wait("tensor", s_wc, s_wc.n)
        else:
            b.wait("tensor", c_ve, kv_ve[r - 1])
        for sub in range(2):
            ps = PS[sub][:, 0:REG]
            for kcc in range(16):
                b.op("tensor", lambda t, o=ps, lw=WkB[:, kcc * 256 + sub * 128:kcc * 256 + sub * 128 + 128],
                     i=lns[:, kcc * REG:(kcc + 1) * REG], stt=(kcc == 0), spp=(kcc == 15):
                     t.matmul(o, lw, i, start=stt, stop=spp),
                     inc=(c_pe if kcc == 15 else None))
            b.wait("vector", c_pe)
            b.op("vector", lambda v, o=KOut[:, sub * NSLOT + r * REG:sub * NSLOT + (r + 1) * REG],
                 i=ps, bi=BiasKbh[:, sub:sub + 1], vb=Validb[:, r * REG:(r + 1) * REG]:
                 v.scalar_tensor_tensor(o, i, bi, vb, ALU.add, ALU.mult), inc=c_ve)
        for scl in range(3):
            ps = PS[2 + scl % 2][:, 0:256]
            sgl = r * 3 + scl
            if scl >= 2:
                b.wait("tensor", c_ve, v_ve0 + (scl - 2) + 1)
            for kcc in range(16):
                b.op("tensor", lambda t, o=ps, lw=lns[:, kcc * REG + scl * 128:kcc * REG + scl * 128 + 128],
                     i=WvB[:, kcc * 256:(kcc + 1) * 256], stt=(kcc == 0), spp=(kcc == 15):
                     t.matmul(o, lw, i, start=stt, stop=spp),
                     inc=(c_pe if kcc == 15 else None))
            b.wait("vector", c_pe)
            if scl == 0:
                v_ve0 = c_ve.n
            b.op("vector", lambda v, i=ps: v.tensor_add(Sq2[:, 0:256], i, BvB[:, :]))
            b.op("vector", lambda v, o=VOut[:, sgl * 256:(sgl + 1) * 256], vt=ValidT[:, sgl:sgl + 1]:
                 v.tensor_scalar(o, Sq2[:, 0:256], vt, None, ALU.mult), inc=c_ve)
        kv_pe[r] = c_pe.n
        kv_ve[r] = c_ve.n
    b.wait("scalar", c_ve)
    for sub, (kv_own, kv_all) in enumerate([(kv_own_e_d, kv_all_e_d), (kv_own_o_d, kv_all_o_d)]):
        b.dma("scalar", s_kvst,
              bass.AP(kv_own, 0, [[NSLOT, 128], [1, NSLOT]]),
              KOut[:, sub * NSLOT:(sub + 1) * NSLOT])
        vsrc = bass.AP(VOut, sub * 128, [[SC * 256, 128], [256, SC], [1, 128]])
        b.dma("scalar", s_kvst,
              bass.AP(kv_own, 384 * 1024, [[SC * 128, 128], [128, SC], [1, 128]]),
              vsrc)
        b.wait("gpsimd", s_kvst, s_kvst.n)
        b.op("gpsimd", lambda g, i=kv_own, o=kv_all: g.collective_compute(
            "AllGather", ALU.bypass, replica_groups=[list(range(NCORE))],
            ins=[i.ap()], outs=[o.ap()]), inc=s_cc, k=1)
        if sub == 0:
            cc_kv_e = s_cc.n
    cc_kv_o = s_cc.n
    kvb_scope.close()
    ph2_scope.close()

    w2p_scope = ExitStack()
    W2P = w2p_scope.enter_context(nc.sbuf_tensor("W2P", [128, HC * 256], BF16))
    b.wait("vector", c_ve, w2_h128_ve)
    CW = 256
    cur, oth = W2T, W2P
    for si in range(6):
        half = 1 << si
        nblk = HC // (2 * half)
        apl = [[HC * CW, 128], [2 * half * CW, nblk], [1, half * CW]]
        ev_s = bass.AP(cur, 0, apl)
        od_s = bass.AP(cur, half * CW, apl)
        ev_d = bass.AP(oth, 0, apl)
        od_d = bass.AP(oth, half * CW, apl)
        b.op("vector", lambda v, a=ev_d, x=ev_s, y=od_s: v.tensor_add(a, x, y))
        b.op("vector", lambda v, a=od_d, x=ev_s, y=od_s: v.tensor_sub(a, x, y),
             inc=(c_ve if si == 5 else None))
        cur, oth = oth, cur
    w2fin = cur
    b.wait("scalar", c_ve)
    b.dma("scalar", s_w2st, w2t_own_d.ap().rearrange("(p hc) d -> p (hc d)", p=128), w2fin[:, :])
    b.wait("gpsimd", s_w2st, s_w2st.n)
    b.op("gpsimd", lambda g: g.collective_compute(
        "AllGather", ALU.bypass, replica_groups=[list(range(NCORE))],
        ins=[w2t_own_d.ap()], outs=[w2t_all_d.ap()]), inc=s_cc, k=1)
    cc_w2 = s_cc.n
    w2p_scope.close()
    w2t_scope.close()

    att_scope = ExitStack()
    KH = att_scope.enter_context(nc.sbuf_tensor("KH", [128, 2 * NSLOT], BF16))
    VH = att_scope.enter_context(nc.sbuf_tensor("VH", [128, 2 * SC * 128], BF16))
    QH = att_scope.enter_context(nc.sbuf_tensor("QH", [128, 2 * T], BF16))
    NRING = 8
    LAG = 5
    PR = att_scope.enter_context(nc.sbuf_tensor("PR", [128, NRING * T], BF16))
    b.wait("scalar", s_qsa, s_qsa.n)
    b.wait("scalar", s_qsb, s_qsb.n)
    b.wait("tensor", c_ve)
    NJ = NH * SC
    ORD = list(range(0, NH, 2)) + list(range(1, NH, 2))
    sc_pe = {}
    av_pe = {}
    den_pe = {}
    ex_ac = {}
    epi_ve = {}
    epi1_ve = {}
    ksem_at = {}
    vsem_at = {}

    def att_load(pos):
        h = ORD[pos]
        par = pos % 2
        hb = h // 2
        kv_all = kv_all_e_d if h % 2 == 0 else kv_all_o_d
        cc_need = cc_kv_e if h % 2 == 0 else cc_kv_o
        ksem = s_kva if par == 0 else s_kvb
        vsem = s_va if par == 0 else s_vb
        b.wait("sync", s_cc, cc_need)
        b.wait("gpsimd", s_cc, cc_need)
        b.wait("scalar", s_cc, cc_need)
        if pos >= 2:
            prev = av_pe[(pos - 2) * SC + SC - 1]
            b.wait("sync", c_pe, prev)
            b.wait("gpsimd", c_pe, prev)
            b.wait("scalar", c_pe, prev)
        b.dma("sync", ksem, KH[:, par * NSLOT:(par + 1) * NSLOT],
              bass.AP(kv_all, hb * 768 * 1024, [[NSLOT, 128], [1, NSLOT]]))
        b.dma("gpsimd", vsem, VH[:, par * SC * 128:(par + 1) * SC * 128],
              bass.AP(kv_all, hb * 768 * 1024 + 384 * 1024,
                      [[SC * 128, 128], [128, SC], [1, 128]]))
        b.dma("scalar", ksem, QH[:, par * T:(par + 1) * T], q_d.ap()[h * 128:(h + 1) * 128, :])
        ksem_at[pos] = (ksem, ksem.n)
        vsem_at[pos] = (vsem, vsem.n)

    def att_scores(j):
        pos, kcc = divmod(j, SC)
        par = pos % 2
        if kcc == 0:
            ksem, kn = ksem_at[pos]
            b.wait("tensor", ksem, kn)
        pst = PS[j % 2]
        if j >= 2:
            b.wait("tensor", c_ac, ex_ac[j - 2])
        for hh in range(2):
            b.op("tensor", lambda t, o=pst[:, hh * 512:hh * 512 + 512],
                 lw=KH[:, par * NSLOT + kcc * 128:par * NSLOT + kcc * 128 + 128],
                 i=QH[:, par * T + hh * 512:par * T + hh * 512 + 512]:
                 t.matmul(o, lw, i, start=True, stop=True),
                 inc=(c_pe if hh == 1 else None))
        sc_pe[j] = c_pe.n

    def att_exp(j):
        ring = j % NRING
        pst = PS[j % 2]
        b.wait("scalar", c_pe, sc_pe[j])
        if j >= NRING:
            b.wait("scalar", c_pe, av_pe[j - NRING])
        b.op("scalar", lambda s, o=PR[:, ring * T:(ring + 1) * T], i=pst[:, 0:1024]:
             s.activation(o, i, AF.Exp, bias=NegC[:, :], scale=1.0), inc=c_ac)
        ex_ac[j] = c_ac.n

    def att_denav(j):
        pos, kcc = divmod(j, SC)
        par = pos % 2
        ring = j % NRING
        b.wait("tensor", c_ac, ex_ac[j])
        if kcc == 0:
            vsem, vn = vsem_at[pos]
            b.wait("tensor", vsem, vn)
        if kcc == 0 and pos >= 1:
            b.wait("tensor", c_ve, epi1_ve[pos - 1])
        for hh in range(2):
            b.op("tensor", lambda t, o=PSc[:, hh * 512:hh * 512 + 512],
                 i=PR[:, ring * T + hh * 512:ring * T + hh * 512 + 512],
                 stt=(kcc == 0), spp=(kcc == SC - 1):
                 t.matmul(o, OnesSqB[:, :], i, start=stt, stop=spp),
                 inc=(c_pe if (kcc == SC - 1 and hh == 1) else None))
        if kcc == SC - 1:
            den_pe[pos] = c_pe.n
        if kcc == 0 and pos >= 1:
            b.wait("tensor", c_ve, epi_ve[pos - 1])
        for hh in range(2):
            b.op("tensor", lambda t, o=PSd[:, hh * 512:hh * 512 + 512],
                 lw=VH[:, par * SC * 128 + kcc * 128:par * SC * 128 + kcc * 128 + 128],
                 i=PR[:, ring * T + hh * 512:ring * T + hh * 512 + 512],
                 stt=(kcc == 0), spp=(kcc == SC - 1):
                 t.matmul(o, lw, i, start=stt, stop=spp),
                 inc=(c_pe if hh == 1 else None))
        av_pe[j] = c_pe.n

    def att_epilogue(pos):
        h = ORD[pos]
        b.wait("vector", c_pe, den_pe[pos])
        b.op("vector", lambda v: v.tensor_scalar(Sq2[:, 0:1024], PSc[:, 0:1024], PADC, None, ALU.add),
             inc=c_ve)
        epi1_ve[pos] = c_ve.n
        b.op("vector", lambda v: v.reciprocal(SqB[:, 0:1024], Sq2[:, 0:1024]))
        b.wait("vector", c_pe, av_pe[pos * SC + SC - 1])
        b.op("vector", lambda v, o=A[:, h * T:(h + 1) * T]:
             v.tensor_tensor(o, PSd[:, 0:1024], SqB[:, 0:1024], ALU.mult), inc=c_ve)
        epi_ve[pos] = c_ve.n

    att_load(0)
    att_load(1)
    for j in range(NJ + LAG):
        if j < NJ:
            pos, kcc = divmod(j, SC)
            if kcc == LAG and pos >= 1 and pos + 1 < NH:
                att_load(pos + 1)
            att_scores(j)
            att_exp(j)
        jj = j - LAG
        if jj >= 0:
            att_denav(jj)
            if jj % SC == SC - 1:
                att_epilogue(jj // SC)
    att_scope.close()

    XC = XS
    wo_pe = {}
    wo_ve = {}
    q_sq_pe = {}
    x1_written = {}
    ln2_ac0 = None
    for oc in range(16):
        par = oc % 2
        buf = WsA[:, par * 2048:(par + 1) * 2048]
        wsem = s_wa if par == 0 else s_wb
        xsem = s_x1a if par == 0 else s_x1b
        sps = s_spa if par == 0 else s_spb
        if oc >= 2:
            b.wait("gpsimd", c_pe, wo_pe[oc - 2])
            b.wait("sync", c_pe, q_sq_pe[oc - 2])
            b.wait("sync", sps, sps.n)
        b.dma("gpsimd", wsem, buf, wo_h[oc])
        b.dma("sync", xsem, XC[:, par * T:(par + 1) * T], xT_h[oc * 128:(oc + 1) * 128, :])
        b.wait("tensor", wsem, wsem.n)
        if oc == 0:
            b.wait("tensor", c_ve, epi_ve[NH - 1])
        pst = PS[par]
        for kcc in range(16):
            for hh in range(2):
                b.op("tensor", lambda t, o=pst[:, hh * 512:hh * 512 + 512],
                     lw=buf[:, kcc * 128:kcc * 128 + 128],
                     i=A[:, kcc * T + hh * 512:kcc * T + hh * 512 + 512],
                     stt=(kcc == 0), spp=(kcc == 15):
                     t.matmul(o, lw, i, start=stt, stop=spp),
                     inc=(c_pe if (kcc == 15 and hh == 1) else None))
        wo_pe[oc] = c_pe.n
        b.wait("vector", c_pe)
        b.wait("vector", xsem, xsem.n)
        xcv = XC[:, par * T:(par + 1) * T]
        b.op("vector", lambda v, o=xcv, i=pst[:, 0:1024], bi=BiasO[:, oc:oc + 1]:
             v.scalar_tensor_tensor(o, i, bi, o, ALU.add, ALU.add), inc=c_ve)
        wo_ve[oc] = c_ve.n
        b.wait("scalar", c_ve, wo_ve[oc])
        b.dma("scalar", sps, x1_d.ap()[oc * 128:(oc + 1) * 128, :], xcv)
        b.op("scalar", lambda s, o=Sq2[:, 0:1024], i=xcv: s.activation(o, i, AF.Square), inc=c_ac)
        if ln2_ac0 is None:
            ln2_ac0 = c_ac.n - 1
        b.wait("tensor", c_ve, wo_ve[oc])
        for hh in range(2):
            b.op("tensor", lambda t, o=PSc[:, hh * 512:hh * 512 + 512],
                 i=xcv[:, hh * 512:hh * 512 + 512], stt=(oc == 0), spp=(oc == 15):
                 t.matmul(o, OnesSq[:, :], i, start=stt, stop=spp))
        b.wait("tensor", c_ac, ln2_ac0 + oc + 1)
        for hh in range(2):
            b.op("tensor", lambda t, o=PSd[:, hh * 512:hh * 512 + 512],
                 i=Sq2[:, hh * 512:hh * 512 + 512], stt=(oc == 0), spp=(oc == 15):
                 t.matmul(o, OnesSq[:, :], i, start=stt, stop=spp),
                 inc=c_pe)
        q_sq_pe[oc] = c_pe.n
    b.wait("vector", c_pe)
    ln2_fin_ve = emit_ln_finalize(PSc[:, 0:1024], PSd[:, 0:1024])
    XL = XS
    ln2_ve = {}
    for q in ("sync", "scalar"):
        b.wait(q, c_pe)
        b.wait(q, s_spa)
        b.wait(q, s_spb)
    for cch in range(KC):
        par = cch % 2
        xsem = s_x1a if par == 0 else s_x1b
        xq = "sync" if par == 0 else "scalar"
        if cch >= 2:
            b.wait(xq, c_ve, ln2_ve[cch - 2])
        b.dma(xq, xsem, XL[:, par * T:(par + 1) * T], x1_d.ap()[cch * 128:(cch + 1) * 128, :])
        b.wait("vector", xsem, xsem.n)
        xc = XL[:, par * T:(par + 1) * T]
        tmp = SqB[:, par * 1024:par * 1024 + 1024]
        b.op("vector", lambda v, a=tmp, x=xc: v.tensor_tensor(a, x, RstdB[:, :], ALU.mult))
        b.op("vector", lambda v, a=tmp, d=A[:, cch * T:(cch + 1) * T]:
             v.tensor_tensor(d, a, MrB[:, :], ALU.add), inc=c_ve)
        ln2_ve[cch] = c_ve.n
    ln2_done_ve = c_ve.n

    mid_scope.close()
    ffn_scope = ExitStack()
    Hid = ffn_scope.enter_context(nc.sbuf_tensor("Hid", [128, HC * T], BF16, side="right"))
    hid_ve = {}
    ffn_pe = {}
    silu_ac = {}
    b.wait("tensor", c_ve, ln2_done_ve)
    for hc in range(HC):
        par = hc % 2
        b1 = WsA[:, par * 2048:(par + 1) * 2048]
        b3 = WsB[:, par * 2048:(par + 1) * 2048]
        s1 = s_wa if par == 0 else s_wb
        s3 = s_wc if par == 0 else s_wd
        if hc >= 2:
            b.wait("gpsimd", c_pe, ffn_pe[hc - 2])
        b.dma("gpsimd", s1, b1, w1_h[hc])
        b.dma("gpsimd", s3, b3, w3_h[hc])
        b.wait("tensor", s1, s1.n)
        b.wait("tensor", s3, s3.n)
        psg = PS[2 * par]
        psl = PS[2 * par + 1]
        if hc >= 2:
            b.wait("tensor", c_ve, hid_ve[hc - 2])
            b.wait("tensor", c_ac, silu_ac[hc - 2])
        for kcc in range(16):
            for hh in range(2):
                b.op("tensor", lambda t, o=psg[:, hh * 512:hh * 512 + 512],
                     lw=b1[:, kcc * 128:kcc * 128 + 128],
                     i=A[:, kcc * T + hh * 512:kcc * T + hh * 512 + 512],
                     stt=(kcc == 0), spp=(kcc == 15):
                     t.matmul(o, lw, i, start=stt, stop=spp),
                     inc=(c_pe if (kcc == 15 and hh == 1) else None))
        gate_pe = c_pe.n
        for kcc in range(16):
            for hh in range(2):
                b.op("tensor", lambda t, o=psl[:, hh * 512:hh * 512 + 512],
                     lw=b3[:, kcc * 128:kcc * 128 + 128],
                     i=A[:, kcc * T + hh * 512:kcc * T + hh * 512 + 512],
                     stt=(kcc == 0), spp=(kcc == 15):
                     t.matmul(o, lw, i, start=stt, stop=spp),
                     inc=(c_pe if (kcc == 15 and hh == 1) else None))
        ffn_pe[hc] = c_pe.n
        b.wait("scalar", c_pe, gate_pe)
        b.op("scalar", lambda s, o=QSt[:, par * T:(par + 1) * T], i=psg[:, 0:1024]:
             s.activation(o, i, AF.Silu), inc=c_ac)
        silu_ac[hc] = c_ac.n
        b.wait("vector", c_ac, silu_ac[hc])
        b.wait("vector", c_pe, ffn_pe[hc])
        b.op("vector", lambda v, o=Hid[:, hc * T:(hc + 1) * T], g=QSt[:, par * T:(par + 1) * T],
             l=psl[:, 0:1024]: v.tensor_tensor(o, g, l, ALU.mult), inc=c_ve)
        hid_ve[hc] = c_ve.n

    ws_scope.close()
    w2s_scope = ExitStack()
    W2S = w2s_scope.enter_context(nc.sbuf_tensor("W2S", [128, 2 * 8 * 256], BF16))
    X1R = XS
    YSt = w2s_scope.enter_context(nc.sbuf_tensor("YSt", [128, 1024], F32))
    b.wait("sync", s_cc, cc_w2)
    b.wait("sync", s_spa)
    b.wait("sync", s_spb)
    b.wait("sync", c_ve, ln2_done_ve)
    ld_i = 0
    w2ld_pe = {}
    ev_ve = {}
    pair_free_ve = {}
    ytr_ve = {}
    ytr_i = 0
    for r in range(NCORE):
        pair = (PS[0], PS[1])
        w2src = w2t_all_d.ap()[r * 128 * HC:(r + 1) * 128 * HC, :].rearrange(
            "(p hc) d -> p (hc d)", p=128)
        for oc2 in range(2):
            xsem = s_x1a if oc2 == 0 else s_x1b
            if r >= 1:
                b.wait("sync", c_ve, ev_ve[(r - 1, oc2)])
            b.dma("sync", xsem, X1R[:, oc2 * 1024:(oc2 + 1) * 1024],
                  x1_d.ap()[(r * 2 + oc2) * 128:(r * 2 + oc2 + 1) * 128, :])
        for lg in range(8):
            parw = ld_i % 2
            wsem = s_w2sA if parw == 0 else s_w2sB
            if ld_i >= 2:
                b.wait("sync", c_pe, w2ld_pe[ld_i - 2])
            elif ld_i == 0:
                b.wait("sync", c_pe, ffn_pe[HC - 1])
            b.dma("sync", wsem, W2S[:, parw * 2048:(parw + 1) * 2048],
                  w2src[:, lg * 2048:(lg + 1) * 2048])
            b.wait("tensor", wsem, wsem.n)
            if r == 0 and lg == 0:
                b.wait("tensor", c_ve, hid_ve[HC - 1])
                b.wait("tensor", c_ac, silu_ac[HC - 1])
            if lg == 0 and r >= 1:
                b.wait("tensor", c_ve, ev_ve[(r - 1, 1)])
            for hcl in range(8):
                hcg = lg * 8 + hcl
                for oc2 in range(2):
                    for th in range(2):
                        b.op("tensor", lambda t, o=pair[oc2][:, th * 512:(th + 1) * 512],
                             lw=W2S[:, parw * 2048 + hcl * 256 + oc2 * 128:
                                    parw * 2048 + hcl * 256 + oc2 * 128 + 128],
                             i=Hid[:, hcg * T + th * 512:hcg * T + th * 512 + 512],
                             stt=(hcg == 0), spp=(hcg == HC - 1):
                             t.matmul(o, lw, i, start=stt, stop=spp),
                             inc=(c_pe if (hcl == 7 and oc2 == 1 and th == 1) else None))
            w2ld_pe[ld_i] = c_pe.n
            ld_i += 1
        for oc2 in range(2):
            d2c = r * 2 + oc2
            xsem = s_x1a if oc2 == 0 else s_x1b
            ysem = s_sta
            b.wait("vector", c_pe, w2ld_pe[ld_i - 1])
            b.wait("vector", xsem, xsem.n)
            yf = SqB[:, oc2 * 1024:(oc2 + 1) * 1024]
            b.op("vector", lambda v, o=yf, i=pair[oc2][:, 0:1024],
                 x=X1R[:, oc2 * 1024:(oc2 + 1) * 1024]:
                 v.tensor_tensor(o, i, x, ALU.add), inc=c_ve)
            ev_ve[(r, oc2)] = c_ve.n
            b.wait("tensor", c_ve, ev_ve[(r, oc2)])
            if r * 2 + oc2 >= 1:
                b.wait("vector", ysem, ysem.n)
                b.wait("tensor", c_ve, ytr_ve[ytr_i - 1])
            for j in range(8):
                pstr = PS[2 + j // 4][:, (j % 4) * 128:(j % 4) * 128 + 128]
                b.op("tensor", lambda t, o=pstr, i=yf[:, j * 128:j * 128 + 128]:
                     t.transpose(o, i, Ident[:, :]), inc=(c_pe if j == 7 else None))
            b.wait("vector", c_pe)
            for j in range(8):
                pstr = PS[2 + j // 4][:, (j % 4) * 128:(j % 4) * 128 + 128]
                b.op("vector", lambda v, o=YSt[:, j * 128:(j + 1) * 128],
                     i=pstr: v.tensor_copy(o, i), inc=(c_ve if j == 7 else None))
            ytr_ve[ytr_i] = c_ve.n
            ytr_i += 1
            b.wait("scalar", c_ve, ytr_ve[ytr_i - 1])
            for j in range(8):
                b.dma("scalar", ysem,
                      y_h[j * 128:(j + 1) * 128, d2c * 128:(d2c + 1) * 128],
                      YSt[:, j * 128:(j + 1) * 128])
        pair_free_ve[r] = ytr_ve[ytr_i - 1]
    w2s_scope.close()
    ffn_scope.close()
    b.wait("scalar", s_sta, s_sta.n)
    b.wait("scalar", s_stb, s_stb.n)

    if verbose:
        print("sbuf bytes remaining:", nc.sbuf_bytes_remaining)
        print("instr counts:", {e: len(b.q[e]) for e in ENGS})

    with nc.Block() as block:
        b.replay(block)
    st.close()
    return nc


def _hadamard(n):
    h = np.array([[1.0]], dtype=np.float64)
    while h.shape[0] < n:
        h = np.block([[h, h], [h, -h]])
    return h


def _prep_inputs(inputs):
    f32 = np.float32
    x = np.asarray(inputs["L0_attn_x"], f32).reshape(S, D)
    Wq = np.asarray(inputs["L0_attn_Wq"], f32)
    bq = np.asarray(inputs["L0_attn_bq"], f32)
    Wk = np.asarray(inputs["L0_attn_Wk"], f32)
    bk = np.asarray(inputs["L0_attn_bk"], f32)
    Wv = np.asarray(inputs["L0_attn_Wv"], f32)
    bv = np.asarray(inputs["L0_attn_bv"], f32)
    Wo = np.asarray(inputs["L0_attn_Wo"], f32)
    bo = np.asarray(inputs["L0_attn_bo"], f32)
    Wi = np.asarray(inputs["L0_attn_Wi"], f32)
    bi = np.asarray(inputs["L0_attn_bi"], f32)
    W1 = np.asarray(inputs["L0_ffn_W1"], f32)
    W2 = np.asarray(inputs["L0_ffn_W2"], f32)
    W3 = np.asarray(inputs["L0_ffn_W3"], f32)

    def tile_oc(W):
        return np.ascontiguousarray(
            W.reshape(16, 128, 16, 128).transpose(2, 1, 0, 3).reshape(16, 128, 16 * 128))

    def tile_hc(Wh):
        return np.ascontiguousarray(
            Wh.reshape(64, 128, 16, 128).transpose(0, 3, 2, 1).reshape(64, 128, 16 * 128))

    wq_t = tile_oc(Wq)
    wo_t = tile_oc(Wo)
    w1_t = tile_hc(W1)
    w3_t = tile_hc(W3)
    bq_t = np.ascontiguousarray(bq.reshape(16, 128).T)
    bo_t = np.ascontiguousarray(bo.reshape(16, 128).T)
    bk_t = np.ascontiguousarray(bk.reshape(16, 128).T)
    wi_t = np.ascontiguousarray(Wi.reshape(16, 128).T)
    h128 = _hadamard(128).astype(ml_dtypes.bfloat16)
    ident = np.eye(128, dtype=f32)
    ones_c = np.ones((128, 1), f32)
    ones_cb = np.ones((128, 1), ml_dtypes.bfloat16)
    ones_r = np.ones((1, 128), f32)
    ones_rb = np.ones((1, 128), ml_dtypes.bfloat16)
    ones_sq = np.ones((128, 128), f32)
    ones_sqb = np.ones((128, 128), ml_dtypes.bfloat16)
    rind = np.zeros((128, 8), f32)
    for p in range(128):
        rind[p, p // 16] = 1.0
    iota384 = np.arange(REG, dtype=f32).reshape(1, REG)
    iota16 = np.broadcast_to(np.arange(1, 17, dtype=f32), (128, 16)).copy()

    xT = np.ascontiguousarray(x.T)
    maps = []
    for c in range(NCORE):
        sub0 = 2 * c
        m = dict(
            xT=np.ascontiguousarray(xT[:, c * T:(c + 1) * T]),
            xrows=np.ascontiguousarray(x[c * T:(c + 1) * T, :]),
            wq_t=wq_t, wo_t=wo_t, w1_t=w1_t, w3_t=w3_t,
            wk_bh=np.ascontiguousarray(
                Wk[:, sub0 * 128:(sub0 + 2) * 128].reshape(16, 128, 256)
                .transpose(1, 0, 2).reshape(128, 16 * 256)),
            wv_bh=np.ascontiguousarray(
                Wv[:, sub0 * 128:(sub0 + 2) * 128].reshape(16, 128, 256)
                .transpose(1, 0, 2).reshape(128, 16 * 256)),
            w2_slt=np.ascontiguousarray(
                W2[c * 256:(c + 1) * 256, :].reshape(256, 64, 128)
                .transpose(2, 1, 0).reshape(128, 64 * 256)),
            wi_t=wi_t, bq_t=bq_t, bo_t=bo_t,
            bk_bh=np.ascontiguousarray(bk_t[:, sub0:sub0 + 2]),
            bv_row=np.ascontiguousarray(bv[sub0 * 128:(sub0 + 2) * 128].reshape(1, 256)),
            bi=np.asarray(bi, f32).reshape(1, 1),
            ones_c=ones_c, ones_cb=ones_cb, ones_r=ones_r, ones_rb=ones_rb, ones_sq=ones_sq,
            ones_sqb=ones_sqb,
            ident=ident, h128=h128, rind=rind, iota384=iota384, iota16=iota16,
        )
        maps.append(m)
    return maps


_NC_CACHE = {}


def kernel(**inputs):
    maps = _prep_inputs(inputs)
    if "nc" not in _NC_CACHE:
        _NC_CACHE["nc"] = build()
    nc = _NC_CACHE["nc"]
    res = run_bass_kernel_spmd(nc, maps, core_ids=list(range(NCORE)))
    y = np.concatenate([np.asarray(res.results[c]["y"]) for c in range(NCORE)], axis=0)
    return y.reshape(1, S, D)


if __name__ == "__main__":
    nc = build(verbose=True)
    print("built ok")


# version 27 (final)
# speedup vs baseline: 1.0288x; 1.0066x over previous
import math
import numpy as np
import ml_dtypes
from contextlib import ExitStack

import concourse.bass as bass
import concourse.mybir as mybir
from concourse.bass_utils import run_bass_kernel_spmd
from concourse.bass import IndirectOffsetOnAxis

F32 = mybir.dt.float32
BF16 = mybir.dt.bfloat16
I32 = mybir.dt.int32
AF = mybir.ActivationFunctionType
ALU = mybir.AluOpType
AX = mybir.AxisListType.X

NCORE = 8
S, D, H = 8192, 2048, 8192
T = S // NCORE
KC = D // 128
HC = H // 128
NH, HD = 16, 128
REG = 384
NSLOT = NCORE * REG
SC = NSLOT // 128
RANK_T = 1638
CSTAB = 16.0
QSCALE = 1.0 / math.sqrt(HD)
PADC = float((S - NSLOT) * math.exp(-CSTAB))
EPS = 1e-5
NBIS = 42

ENGS = ("sync", "gpsimd", "tensor", "scalar", "vector")


class Sem:
    def __init__(self, stack, nc, name):
        self.h = stack.enter_context(nc.semaphore(name))
        self.n = 0


class B:
    def __init__(self, nc, stack):
        self.nc = nc
        self.q = {e: [] for e in ENGS}
        self.stack = stack

    def sem(self, name):
        return Sem(self.stack, self.nc, name)

    def em(self, eng, fn):
        self.q[eng].append(fn)

    trace = []

    def dma(self, eng, sem, out, in_):
        self.trace.append(("dma", eng, sem.h.name, sem.n + 16))
        self.em(eng, lambda E, o=out, i=in_, s=sem.h: E.dma_start(out=o, in_=i).then_inc(s, 16))
        sem.n += 16
        return sem.n

    def dmac(self, eng, sem, fn):
        self.em(eng, lambda E, f=fn, s=sem.h: f(E).then_inc(s, 16))
        sem.n += 16
        return sem.n

    def wait(self, eng, sem, val=None):
        v = sem.n if val is None else val
        self.trace.append(("wait", eng, sem.h.name, v))
        self.em(eng, lambda E, s=sem.h, v=v: E.wait_ge(s, v))

    def op(self, eng, fn, inc=None, k=1):
        if inc is None:
            self.em(eng, fn)
        else:
            self.em(eng, lambda E, f=fn, s=inc.h, kk=k: f(E).then_inc(s, kk))
            inc.n += k

    def replay(self, block):
        qs = self.q

        @block.sync
        def _(e):
            for f in qs["sync"]:
                f(e)

        @block.gpsimd
        def _(e):
            for f in qs["gpsimd"]:
                f(e)

        @block.tensor
        def _(e):
            for f in qs["tensor"]:
                f(e)

        @block.scalar
        def _(e):
            for f in qs["scalar"]:
                f(e)

        @block.vector
        def _(e):
            for f in qs["vector"]:
                f(e)


def build(verbose=False):
    nc = bass.Bass("TRN2", target_bir_lowering=False)
    st = ExitStack()
    b = B(nc, st)

    def inp(name, shape, dt=F32):
        return nc.dram_tensor(name, shape, dt, kind="ExternalInput").ap()

    xT_h = inp("xT", [D, T])
    xrows_h = inp("xrows", [T, D])
    wq_h = inp("wq_t", [16, 128, 16 * 128])
    wo_h = inp("wo_t", [16, 128, 16 * 128])
    wkbh_h = inp("wk_bh", [128, 16 * 256])
    wvbh_h = inp("wv_bh", [128, 16 * 256])
    w1_h = inp("w1_t", [64, 128, 16 * 128])
    w3_h = inp("w3_t", [64, 128, 16 * 128])
    w2_h = inp("w2_slt", [128, HC * 256])
    wi_h = inp("wi_t", [128, 16])
    bq_h = inp("bq_t", [128, 16])
    bo_h = inp("bo_t", [128, 16])
    bkbh_h = inp("bk_bh", [128, 2])
    bvrow_h = inp("bv_row", [1, 256])
    bi_h = inp("bi", [1, 1])
    ones_c_h = inp("ones_c", [128, 1])
    ones_cb_h = inp("ones_cb", [128, 1], BF16)
    ones_r_h = inp("ones_r", [1, 128])
    ones_rb_h = inp("ones_rb", [1, 128], BF16)
    ones_sq_h = inp("ones_sq", [128, 128])
    ones_sqb_h = inp("ones_sqb", [128, 128], BF16)
    ident_h = inp("ident", [128, 128])
    h128_h = inp("h128", [128, 128], BF16)
    rind_h = inp("rind", [128, 8])
    iota384_h = inp("iota384", [1, REG])

    y_t = nc.dram_tensor("y", [T, D], F32, kind="ExternalOutput")
    y_h = y_t.ap()

    imp_own_d = nc.dram_tensor("imp_own_d", [T, 1], F32)
    imp_all_d = nc.dram_tensor("imp_all_d", [S, 1], F32)
    lng_own_d = nc.dram_tensor("lng_own_d", [128 * 16, REG], BF16)
    lng_all_d = nc.dram_tensor("lng_all_d", [NCORE * 16 * 128, REG], BF16)
    kv_own_e_d = nc.dram_tensor("kv_own_e_d", [768, 1024], BF16)
    kv_own_o_d = nc.dram_tensor("kv_own_o_d", [768, 1024], BF16)
    kv_all_e_d = nc.dram_tensor("kv_all_e_d", [NCORE * 768, 1024], BF16)
    kv_all_o_d = nc.dram_tensor("kv_all_o_d", [NCORE * 768, 1024], BF16)
    gath_d = nc.dram_tensor("gath_d", [512, D], F32)
    w2t_own_d = nc.dram_tensor("w2t_own_d", [128 * HC, 256], BF16)
    w2t_all_d = nc.dram_tensor("w2t_all_d", [NCORE * 128 * HC, 256], BF16)
    q_d = nc.dram_tensor("q_d", [NH * 128, T], BF16)
    x1_d = nc.dram_tensor("x1_d", [KC * 128, T], F32)

    PSa = nc.alloc_psum_tensor("PSa", [128, 1024], F32)
    PSb = nc.alloc_psum_tensor("PSb", [128, 1024], F32)
    PSc = nc.alloc_psum_tensor("PSc", [128, 1024], F32)
    PSd = nc.alloc_psum_tensor("PSd", [128, 1024], F32)
    PS = [PSa, PSb, PSc, PSd]

    A = nc.alloc_sbuf_tensor("A", [128, KC * T], BF16)
    SqB = nc.alloc_sbuf_tensor("SqB", [128, 2048], F32)
    QSt = nc.alloc_sbuf_tensor("QSt", [128, 2 * T], BF16)
    XS = nc.alloc_sbuf_tensor("XS", [128, 2048], F32)
    Ident = nc.alloc_sbuf_tensor("Ident", [128, 128], F32)
    ws_scope = ExitStack()
    WsA = ws_scope.enter_context(nc.sbuf_tensor("WsA", [128, 2 * 2048], BF16))
    WsB = ws_scope.enter_context(nc.sbuf_tensor("WsB", [128, 2 * 2048], BF16))
    mid_scope = ExitStack()

    def mid(name, shape, dt=F32):
        return mid_scope.enter_context(nc.sbuf_tensor(name, shape, dt))

    Sq2 = mid("Sq2", [128, 1024])
    RowS = mid("RowS", [1, 2048])
    RstdB = mid("RstdB", [128, 1024])
    MrB = mid("MrB", [128, 1024])
    OnesSq = mid("OnesSq", [128, 128])
    OnesSqB = mid("OnesSqB", [128, 128], BF16)
    OnesR = mid("OnesR", [1, 128])
    OnesRb = mid("OnesRb", [1, 128], BF16)
    OnesC = mid("OnesC", [128, 1])
    OnesCb = mid("OnesCb", [128, 1], BF16)
    Rind = mid("Rind", [128, 8])
    Iota384 = mid("Iota384", [1, REG])
    BiasQ = mid("BiasQ", [128, 16])
    BiasQs = mid("BiasQs", [128, 16])
    BiasO = mid("BiasO", [128, 16])
    BiasKbh = mid("BiasKbh", [128, 2])
    BiasVr = mid("BiasVr", [1, 256])
    BvB = mid("BvB", [128, 256])
    BiT = mid("BiT", [1, 1])
    Wi = mid("Wi", [128, 16])
    Wib = mid("Wib", [128, 16], BF16)
    H128 = mid("H128", [128, 128], BF16)
    EpsCol = mid("EpsCol", [128, 1])
    NegC = mid("NegC", [128, 1])
    w2t_scope = ExitStack()
    W2T = w2t_scope.enter_context(nc.sbuf_tensor("W2T", [128, HC * 256], BF16))
    ph2_scope = ExitStack()

    def ph2(name, shape, dt=F32):
        return ph2_scope.enter_context(nc.sbuf_tensor(name, shape, dt))

    RowB = ph2("RowB", [1, NSLOT], BF16)
    ImpRow = ph2("ImpRow", [1, T], F32)
    Validb = ph2("Validb", [128, NSLOT], BF16)
    IdxT = ph2("IdxT", [128, 8], F32)
    IdxTi = ph2("IdxTi", [128, 8], I32)
    DmI = ph2("DmI", [128, 8], I32)
    PIota = ph2("PIota", [128, 1])
    ValidT = ph2("ValidT", [128, SC], F32)
    Bis = ph2("Bis", [128, 8], F32)
    BisI = ph2("BisI", [128, 1], I32)
    ImpAll = ph2("ImpAll", [128, 64], F32)
    Cnt = ph2("Cnt", [8, 1], F32)
    CntRow = ph2("CntRow", [1, 8], F32)

    gath_scope = ExitStack()
    Xtok = gath_scope.enter_context(nc.sbuf_tensor("Xtok", [128, 2 * D], F32))
    XgT = gath_scope.enter_context(nc.sbuf_tensor("XgT", [128, 16 * REG], F32))
    LnAll = gath_scope.enter_context(nc.sbuf_tensor("LnAll", [128, 16 * REG], BF16))
    s_cst = b.sem("s_cst")
    s_w2l = b.sem("s_w2l")
    s_wa = b.sem("s_wa")
    s_wb = b.sem("s_wb")
    s_wc = b.sem("s_wc")
    s_wd = b.sem("s_wd")
    s_xra = b.sem("s_xra")
    s_xrb = b.sem("s_xrb")
    s_rga = b.sem("s_rga")
    s_rgb = b.sem("s_rgb")
    s_reg = b.sem("s_reg")
    s_lng = b.sem("s_lng")
    s_kva = b.sem("s_kva")
    s_kvb = b.sem("s_kvb")
    s_out = b.sem("s_out")
    s_impst = b.sem("s_impst")
    s_lngst = b.sem("s_lngst")
    s_kvst = b.sem("s_kvst")
    s_w2st = b.sem("s_w2st")
    s_qsa = b.sem("s_qsa")
    s_qsb = b.sem("s_qsb")
    s_spa = b.sem("s_spa")
    s_spb = b.sem("s_spb")
    s_sta = b.sem("s_sta")
    s_stb = b.sem("s_stb")
    s_cc = b.sem("s_cc")
    s_dbg = b.sem("s_dbg")
    s_sp = b.sem("s_sp")
    s_x1a = b.sem("s_x1a")
    s_x1b = b.sem("s_x1b")
    s_w2sA = b.sem("s_w2sA")
    s_w2sB = b.sem("s_w2sB")
    s_st = b.sem("s_st")
    s_scat = b.sem("s_scat")
    s_zro = b.sem("s_zro")
    s_va = b.sem("s_va")
    s_vb = b.sem("s_vb")
    c_pe = b.sem("c_pe")
    c_ac = b.sem("c_ac")
    c_ve = b.sem("c_ve")
    c_gp = b.sem("c_gp")

    def opd(eng, fn, inc=None, k=1):
        b.em(eng, fn)
        if inc is None:
            b.em(eng, lambda E: E.drain())
        else:
            b.em(eng, lambda E, s=inc.h, kk=k: E.drain().then_inc(s, kk))
            inc.n += k

    for dst, src in [
        (OnesC[:, :], ones_c_h), (Rind[:, :], rind_h), (Ident[:, :], ident_h),
        (OnesSq[:, :], ones_sq_h), (OnesR[:, :], ones_r_h), (Iota384[:, :], iota384_h),
        (BiasQ[:, :], bq_h), (BiasO[:, :], bo_h), (BiasKbh[:, :], bkbh_h),
        (BiasVr[:, :], bvrow_h), (BiT[:, :], bi_h), (Wi[:, :], wi_h),
        (H128[:, :], h128_h), (OnesCb[:, :], ones_cb_h), (OnesRb[:, :], ones_rb_h),
        (OnesSqB[:, :], ones_sqb_h),
    ]:
        b.dma("sync", s_cst, dst, src)
    for i4 in range(4):
        b.dma("gpsimd", s_w2l, W2T[:, i4 * 4096:(i4 + 1) * 4096],
              w2_h[:, i4 * 4096:(i4 + 1) * 4096])
    b.wait("vector", s_cst)
    b.op("vector", lambda v: v.tensor_copy(Wib[:, :], Wi[:, :]))
    b.op("vector", lambda v: v.memset(Bis[:, :], 0.0))
    b.op("vector", lambda v: v.memset(BisI[:, :], 0))
    b.op("vector", lambda v: v.memset(IdxT[:, :], 0.0))
    b.op("vector", lambda v: v.memset(EpsCol[:, :], EPS))
    b.op("vector", lambda v: v.memset(NegC[:, :], -CSTAB))
    opd("vector", lambda v: v.tensor_scalar(BiasQs[:, :], BiasQ[:, :], QSCALE, None, ALU.mult), inc=c_ve)

    def emit_ln_finalize(ps_sum, ps_sq):
        b.op("vector", lambda v: v.tensor_scalar(Sq2[:, :], ps_sum, 1.0 / D, None, ALU.mult))
        b.op("vector", lambda v: v.tensor_scalar(RstdB[:, :], ps_sq, 1.0 / D, None, ALU.mult))
        b.op("vector", lambda v: v.tensor_mul(MrB[:, :], Sq2[:, :], Sq2[:, :]))
        b.op("vector", lambda v: v.tensor_sub(RstdB[:, :], RstdB[:, :], MrB[:, :]), inc=c_ve)
        b.wait("scalar", c_ve)
        b.op("scalar", lambda s: s.activation(RstdB[:, :], RstdB[:, :], AF.Ln, bias=EpsCol[:, :], scale=1.0))
        b.op("scalar", lambda s: s.activation(RstdB[:, :], RstdB[:, :], AF.Exp, scale=-0.5), inc=c_ac)
        b.wait("vector", c_ac)
        b.op("vector", lambda v: v.scalar_tensor_tensor(MrB[:, :], Sq2[:, :], -1.0, RstdB[:, :],
                                                        ALU.mult, ALU.mult), inc=c_ve)
        return c_ve.n

    b.wait("tensor", s_cst)
    ac0, pe0 = c_ac.n, c_pe.n
    for cch in range(KC):
        par = cch % 2
        xsem = s_x1a if par == 0 else s_x1b
        xc = XS[:, par * 1024:(par + 1) * 1024]
        if cch >= 2:
            b.wait("sync", c_pe, pe0 + (cch - 2) + 1)
        b.dma("sync", xsem, xc, xT_h[cch * 128:(cch + 1) * 128, :])
        b.wait("scalar", xsem, xsem.n)
        sq = SqB[:, par * 1024:par * 1024 + 1024]
        if cch >= 2:
            b.wait("scalar", c_pe, pe0 + (cch - 2) + 1)
        b.op("scalar", lambda s, o=sq, i=xc: s.activation(o, i, AF.Square), inc=c_ac)
        b.wait("tensor", xsem, xsem.n)
        for hh in range(2):
            b.op("tensor", lambda t, o=PSa[:, hh * 512:hh * 512 + 512],
                 i=xc[:, hh * 512:hh * 512 + 512], stt=(cch == 0), spp=(cch == KC - 1):
                 t.matmul(o, OnesSq[:, :], i, start=stt, stop=spp))
        b.wait("tensor", c_ac, ac0 + cch + 1)
        for hh in range(2):
            b.op("tensor", lambda t, o=PSb[:, hh * 512:hh * 512 + 512],
                 i=sq[:, hh * 512:hh * 512 + 512], stt=(cch == 0), spp=(cch == KC - 1):
                 t.matmul(o, OnesSq[:, :], i, start=stt, stop=spp),
                 inc=(c_pe if hh == 1 else None))
    b.wait("vector", c_pe)
    ln1_fin_ve = emit_ln_finalize(PSa[:, 0:1024], PSb[:, 0:1024])
    ap_ve = {}
    for cch in range(KC):
        par = cch % 4
        xsem = (s_x1a, s_x1b, s_xra, s_xrb)[par]
        xq = ("sync", "scalar", "sync", "scalar")[par]
        xc = (XS[:, 0:1024], XS[:, 1024:2048],
              Xtok[:, 0:1024], Xtok[:, 1024:2048])[par]
        if cch >= 4:
            b.wait(xq, c_ve, ap_ve[cch - 4])
        else:
            b.wait(xq, c_ve, ln1_fin_ve)
        b.dma(xq, xsem, xc, xT_h[cch * 128:(cch + 1) * 128, :])
        b.wait("vector", xsem, xsem.n)
        dc = A[:, cch * T:(cch + 1) * T]
        tmp = SqB[:, (cch % 2) * 1024:(cch % 2) * 1024 + 1024]
        b.op("vector", lambda v, a=tmp, x=xc: v.tensor_tensor(a, x, RstdB[:, :], ALU.mult))
        b.op("vector", lambda v, a=tmp, d=dc: v.tensor_tensor(d, a, MrB[:, :], ALU.add),
             inc=c_ve)
        ap_ve[cch] = c_ve.n
    ln1_ve = c_ve.n

    b.wait("tensor", s_w2l)
    h_ve0 = c_ve.n
    for hc in range(HC):
        ps = PS[2 + (hc % 2)][:, 0:256]
        if hc >= 2:
            b.wait("tensor", c_ve, h_ve0 + (hc - 2) + 1)
        b.op("tensor", lambda t, o=ps, i=W2T[:, hc * 256:hc * 256 + 256]:
             t.matmul(o, H128[:, :], i, start=True, stop=True), inc=c_pe)
        b.wait("vector", c_pe)
        opd("vector", lambda v, o=W2T[:, hc * 256:hc * 256 + 256], i=ps:
            v.tensor_scalar(o, i, 1.0 / math.sqrt(H), None, ALU.mult), inc=c_ve)
    w2_h128_ve = c_ve.n

    b.wait("tensor", c_ve, ln1_ve)
    for cch in range(KC):
        for hh in range(2):
            b.op("tensor", lambda t, o=PSa[0:1, hh * 512:hh * 512 + 512],
                 i=A[:, cch * T + hh * 512:cch * T + hh * 512 + 512],
                 lw=Wib[:, cch:cch + 1], stt=(cch == 0), spp=(cch == KC - 1):
                 t.matmul(o, lw, i, start=stt, stop=spp),
                 inc=(c_pe if (cch == KC - 1 and hh == 1) else None))
    b.wait("vector", c_pe)
    b.op("vector", lambda v: v.tensor_scalar(ImpRow[:, :], PSa[0:1, 0:T], BiT[0:1, 0:1], None, ALU.add), inc=c_ve)
    b.wait("scalar", c_ve)
    b.dma("scalar", s_impst, imp_own_d.ap(), ImpRow[:, :])
    b.wait("gpsimd", s_impst, s_impst.n)
    b.op("gpsimd", lambda g: g.collective_compute(
        "AllGather", ALU.bypass, replica_groups=[list(range(NCORE))],
        ins=[imp_own_d.ap()], outs=[imp_all_d.ap()]), inc=s_cc, k=1)
    b.wait("gpsimd", s_cc, s_cc.n)
    b.dma("gpsimd", s_reg, ImpAll[:, :], imp_all_d.ap().rearrange("(p f) o -> p (f o)", p=128))
    b.op("gpsimd", lambda g: g.iota(PIota[:, :], [[1, 1]], channel_multiplier=1,
                                    allow_small_or_imprecise_dtypes=True), inc=c_gp)

    LO, HI, MID, PC = Bis[:, 0:1], Bis[:, 1:2], Bis[:, 2:3], Bis[:, 3:4]
    b.wait("vector", s_reg)
    b.op("vector", lambda v: v.memset(LO, -64.0))
    opd("vector", lambda v: v.memset(HI, 64.0), inc=c_ve)
    b.wait("tensor", c_ve)
    for it in range(NBIS):
        opd("vector", lambda v: v.tensor_add(MID, LO, HI))
        opd("vector", lambda v: v.tensor_scalar(MID, MID, 0.5, None, ALU.mult))
        opd("vector", lambda v: v.tensor_scalar(Sq2[:, 0:64], ImpAll[:, :], MID[:, 0:1], None, ALU.is_gt))
        opd("vector", lambda v: v.reduce_sum(PC, Sq2[:, 0:64], axis=AX), inc=c_ve)
        b.wait("tensor", c_ve)
        b.op("tensor", lambda t: t.matmul(PSa[:, 0:1], OnesSq[:, :], PC, start=True, stop=True), inc=c_pe)
        b.wait("vector", c_pe)
        opd("vector", lambda v: v.tensor_scalar(BisI[:, 0:1], PSa[:, 0:1], float(RANK_T + 1), None, ALU.is_ge))
        opd("vector", lambda v: v.select(LO, BisI[:, 0:1], MID, LO))
        opd("vector", lambda v: v.tensor_scalar(BisI[:, 0:1], PSa[:, 0:1], float(RANK_T + 1), None, ALU.is_lt))
        opd("vector", lambda v: v.select(HI, BisI[:, 0:1], MID, HI), inc=c_ve)

    opd("vector", lambda v: v.tensor_scalar(Sq2[:, 0:64], ImpAll[:, :], HI[:, 0:1], None, ALU.is_gt))
    opd("vector", lambda v: v.reduce_sum(Bis[:, 4:5], Sq2[:, 0:64], axis=AX), inc=c_ve)
    b.wait("tensor", c_ve)
    b.op("tensor", lambda t: t.matmul(PSa[0:8, 0:1], Rind[:, :], Bis[:, 4:5], start=True, stop=True), inc=c_pe)
    b.wait("vector", c_pe)
    opd("vector", lambda v: v.tensor_scalar(Cnt[:, :], PSa[0:8, 0:1], float(REG), None, ALU.min), inc=c_ve)
    b.op("vector", lambda v: v.memset(Sq2[:, 0:128], 0.0))
    opd("vector", lambda v: v.tensor_copy(Sq2[0:8, 0:1], Cnt[:, :]), inc=c_ve)
    b.wait("tensor", c_ve)
    b.op("tensor", lambda t: t.transpose(PSb[:, 0:128], Sq2[:, 0:128], Ident[:, :]), inc=c_pe)
    b.wait("vector", c_pe)
    opd("vector", lambda v: v.tensor_copy(CntRow[:, :], PSb[0:1, 0:8]), inc=c_ve)
    for r in range(NCORE):
        if r < NCORE - 1:
            b.op("vector", lambda v, r=r: v.tensor_scalar(
                RowB[0:1, r * REG:(r + 1) * REG], Iota384[:, :], CntRow[0:1, r:r + 1], None, ALU.is_lt))
        else:
            opd("vector", lambda v, r=r: v.tensor_scalar(
                RowB[0:1, r * REG:(r + 1) * REG], Iota384[:, :], CntRow[0:1, r:r + 1], None, ALU.is_lt),
                inc=c_ve)
    b.wait("tensor", c_ve)
    vb0 = c_ve.n
    for hh in range(NSLOT // 512):
        ps = PS[hh % 2][:, 0:512]
        if hh >= 2:
            b.wait("tensor", c_ve, vb0 + (hh - 2) + 1)
        b.op("tensor", lambda t, o=ps, i=RowB[0:1, hh * 512:hh * 512 + 512]:
             t.matmul(o, OnesRb[:, :], i, start=True, stop=True), inc=c_pe)
        b.wait("vector", c_pe)
        b.op("vector", lambda v, o=Validb[:, hh * 512:hh * 512 + 512], i=ps: v.tensor_copy(o, i), inc=c_ve)

    def emit_row_to_cols(row_src_off, n_sub, dst_tile, dst_off):
        b.op("vector", lambda v: v.memset(Sq2[:, 0:1024], 0.0))
        opd("vector", lambda v, o=Sq2[0:1, 0:n_sub * 128],
            i=RowB[0:1, row_src_off:row_src_off + n_sub * 128]: v.tensor_copy(o, i), inc=c_ve)
        b.wait("tensor", c_ve)
        t0 = c_ve.n
        for sub in range(n_sub):
            ps = PSc[:, 128:256]
            if sub >= 1:
                b.wait("tensor", c_ve, t0 + sub)
            b.op("tensor", lambda t, o=ps, i=Sq2[:, sub * 128:sub * 128 + 128]:
                 t.transpose(o, i, Ident[:, :]), inc=c_pe)
            b.wait("vector", c_pe)
            opd("vector", lambda v, o=dst_tile[:, dst_off + sub:dst_off + sub + 1], i=ps[:, 0:1]:
                v.tensor_copy(o, i), inc=c_ve)

    for blk in range(3):
        emit_row_to_cols(blk * 1024, 8, ValidT, blk * 8)

    MRow = RowS[0:1, 1024:2048]
    opd("vector", lambda v: v.tensor_scalar(MRow, ImpRow[:, :], HI[0:1, 0:1], None, ALU.is_gt))
    opd("vector", lambda v: v.tensor_copy(SqB[0:1, 0:T], MRow))
    ca, cb = 0, 1024
    for si in range(10):
        d = 1 << si
        b.op("vector", lambda v, d=d, a=ca, o=cb:
             v.tensor_add(SqB[0:1, o + d:o + T], SqB[0:1, a + d:a + T], SqB[0:1, a:a + T - d]))
        opd("vector", lambda v, d=d, a=ca, o=cb:
            v.tensor_copy(SqB[0:1, o:o + d], SqB[0:1, a:a + d]))
        ca, cb = cb, ca
    b.op("vector", lambda v, a=ca: v.tensor_scalar(SqB[0:1, a:a + T], SqB[0:1, a:a + T],
                                                   -1.0 - 1024.0, None, ALU.add))
    b.op("vector", lambda v, a=ca: v.tensor_mul(SqB[0:1, a:a + T], SqB[0:1, a:a + T], MRow))
    b.op("vector", lambda v: v.memset(Sq2[:, 0:1024], 0.0), inc=c_ve)
    opd("vector", lambda v, a=ca: v.tensor_scalar(Sq2[0:1, 0:T], SqB[0:1, a:a + T],
                                                   1024.0, None, ALU.add), inc=c_ve)
    b.wait("tensor", c_ve)
    t0i = c_ve.n
    for sub in range(8):
        ps = PSc[:, 128:256]
        if sub >= 1:
            b.wait("tensor", c_ve, t0i + sub)
        b.op("tensor", lambda t, o=ps, i=Sq2[:, sub * 128:sub * 128 + 128]:
             t.transpose(o, i, Ident[:, :]), inc=c_pe)
        b.wait("vector", c_pe)
        opd("vector", lambda v, o=IdxT[:, sub:sub + 1], i=ps[:, 0:1]: v.tensor_copy(o, i), inc=c_ve)
    b.wait("vector", c_gp)
    opd("vector", lambda v: v.tensor_scalar(PIota[:, :], PIota[:, :], 384.0, None, ALU.add))
    opd("vector", lambda v: v.tensor_scalar(DmI[:, :], IdxT[:, :], 1024.0, None, ALU.is_ge))
    opd("vector", lambda v: v.tensor_scalar(Sq2[:, 8:16], OnesSq[:, 0:8], PIota[:, 0:1], None, ALU.mult))
    opd("vector", lambda v: v.select(IdxT[:, :], DmI[:, :], Sq2[:, 8:16], IdxT[:, :]))
    opd("vector", lambda v: v.tensor_copy(IdxTi[:, :], IdxT[:, :]), inc=c_ve)
    idx_ve = c_ve.n

    b.op("vector", lambda v: v.memset(Xtok[:, 0:D], 0.0), inc=c_ve)
    b.wait("sync", c_ve)
    for z in range(3):
        b.dma("sync", s_zro, gath_d.ap()[z * 128:(z + 1) * 128, :], Xtok[:, 0:D])
    b.wait("gpsimd", s_zro, s_zro.n)
    b.wait("gpsimd", c_ve, idx_ve)
    for tc in range(8):
        par = tc % 2
        xsem = s_xra if par == 0 else s_xrb
        dst = Xtok[:, par * D:(par + 1) * D]
        if tc >= 2:
            b.wait("sync", s_scat, 16 * (tc - 1))
        b.dma("sync", xsem, dst, xrows_h[tc * 128:(tc + 1) * 128, :])
        b.wait("gpsimd", xsem, xsem.n)
        b.dmac("gpsimd", s_scat, lambda g, src=dst, i=tc: g.indirect_dma_start(
            gath_d.ap(), IndirectOffsetOnAxis(ap=IdxTi[:, i:i + 1], axis=0), src, None))
    b.wait("sync", s_scat, s_scat.n)
    b.wait("tensor", c_ve, idx_ve)
    tr3_ve = {}
    tr3_pe = {}
    for r3 in range(3):
        dst = Xtok[:, (r3 % 2) * D:(r3 % 2 + 1) * D]
        if r3 == 2:
            b.wait("sync", c_pe, tr3_pe[0])
        b.dma("sync", s_xra, dst, gath_d.ap()[r3 * 128:(r3 + 1) * 128, :])
        xmark = s_xra.n
        if r3 >= 1:
            b.wait("tensor", c_ve, tr3_ve[r3 - 1])
        b.wait("tensor", s_xra, xmark)
        for kcc in range(16):
            ps = PS[2 + kcc // 8][:, (kcc % 8) * 128:(kcc % 8) * 128 + 128]
            b.op("tensor", lambda t, o=ps, i=dst[:, kcc * 128:kcc * 128 + 128]:
                 t.transpose(o, i, Ident[:, :]), inc=(c_pe if kcc == 15 else None))
        tr3_pe[r3] = c_pe.n
        b.wait("vector", c_pe)
        for kcc in range(16):
            ps = PS[2 + kcc // 8][:, (kcc % 8) * 128:(kcc % 8) * 128 + 128]
            b.op("vector", lambda v, o=XgT[:, kcc * REG + r3 * 128:kcc * REG + r3 * 128 + 128],
                 i=ps: v.tensor_copy(o, i), inc=(c_ve if kcc == 15 else None))
        tr3_ve[r3] = c_ve.n
    b.wait("tensor", c_ve, tr3_ve[2])
    b.wait("scalar", c_ve, tr3_ve[2])
    g_ac0 = c_ac.n
    gstat_pe = {}
    for kcc in range(16):
        sq = SqB[:, (kcc % 4) * 512:(kcc % 4) * 512 + REG]
        if kcc >= 4:
            b.wait("scalar", c_pe, gstat_pe[kcc - 4])
        b.op("scalar", lambda s, o=sq, i=XgT[:, kcc * REG:(kcc + 1) * REG]:
             s.activation(o, i, AF.Square), inc=c_ac)
        b.op("tensor", lambda t, o=PSa[:, 0:REG], i=XgT[:, kcc * REG:(kcc + 1) * REG],
             stt=(kcc == 0), spp=(kcc == 15):
             t.matmul(o, OnesSq[:, :], i, start=stt, stop=spp))
        b.wait("tensor", c_ac, g_ac0 + kcc + 1)
        b.op("tensor", lambda t, o=PSb[:, 0:REG], i=sq, stt=(kcc == 0), spp=(kcc == 15):
             t.matmul(o, OnesSq[:, :], i, start=stt, stop=spp), inc=c_pe)
        gstat_pe[kcc] = c_pe.n
    b.wait("vector", c_pe)
    R3, M3, S3 = RstdB[:, 0:REG], MrB[:, 0:REG], Sq2[:, 0:REG]
    b.op("vector", lambda v: v.tensor_scalar(S3, PSa[:, 0:REG], 1.0 / D, None, ALU.mult))
    b.op("vector", lambda v: v.tensor_scalar(R3, PSb[:, 0:REG], 1.0 / D, None, ALU.mult))
    b.op("vector", lambda v: v.tensor_mul(M3, S3, S3))
    b.op("vector", lambda v: v.tensor_sub(R3, R3, M3), inc=c_ve)
    b.wait("scalar", c_ve)
    b.op("scalar", lambda s: s.activation(R3, R3, AF.Ln, bias=EpsCol[:, :], scale=1.0))
    b.op("scalar", lambda s: s.activation(R3, R3, AF.Exp, scale=-0.5), inc=c_ac)
    b.wait("vector", c_ac)
    b.op("vector", lambda v: v.scalar_tensor_tensor(M3, S3, -1.0, R3, ALU.mult, ALU.mult),
         inc=c_ve)
    for kcc in range(16):
        sq = SqB[:, (kcc % 4) * 512:(kcc % 4) * 512 + REG]
        b.op("vector", lambda v, o=sq, i=XgT[:, kcc * REG:(kcc + 1) * REG]:
             v.tensor_tensor(o, i, R3, ALU.mult))
        b.op("vector", lambda v, o=LnAll[:, kcc * REG:(kcc + 1) * REG], i=sq:
             v.tensor_tensor(o, i, M3, ALU.add),
             inc=(c_ve if kcc == 15 else None))
    b.wait("scalar", c_ve)
    b.dma("scalar", s_lngst,
          bass.AP(lng_own_d, 0, [[16 * REG, 128], [REG, 16], [1, REG]]),
          LnAll[:, :])
    b.wait("gpsimd", s_lngst, s_lngst.n)
    b.op("gpsimd", lambda g: g.collective_compute(
        "AllGather", ALU.bypass, replica_groups=[list(range(NCORE))],
        ins=[lng_own_d.ap()], outs=[lng_all_d.ap()]), inc=s_cc, k=1)
    cc_lng = s_cc.n

    b.wait("tensor", c_ve, idx_ve)
    q_consumed_pe = {}
    for oc in range(16):
        par = oc % 2
        buf = WsA[:, par * 2048:(par + 1) * 2048]
        sem = s_wa if par == 0 else s_wb
        if oc >= 2:
            b.wait("gpsimd", c_pe, q_consumed_pe[oc - 2])
        b.dma("gpsimd", sem, buf, wq_h[oc])
        b.wait("tensor", sem, sem.n)
        pst = PS[par]
        if oc >= 2:
            b.wait("tensor", c_ve, q_evac_ac[oc - 2])
        for kcc in range(16):
            for hh in range(2):
                b.op("tensor", lambda t, o=pst[:, hh * 512:hh * 512 + 512],
                     lw=buf[:, kcc * 128:kcc * 128 + 128],
                     i=A[:, kcc * T + hh * 512:kcc * T + hh * 512 + 512],
                     stt=(kcc == 0), spp=(kcc == 15):
                     t.matmul(o, lw, i, start=stt, stop=spp),
                     inc=(c_pe if (kcc == 15 and hh == 1) else None))
        q_consumed_pe[oc] = c_pe.n
        b.wait("vector", c_pe)
        qs = s_qsa if par == 0 else s_qsb
        if oc >= 2:
            b.wait("vector", qs, qs.n)
        b.op("vector", lambda v, o=QSt[:, par * T:(par + 1) * T], i=pst[:, 0:1024], bi=BiasQ[:, oc:oc + 1]:
             v.tensor_scalar(o, i, bi, QSCALE, ALU.add, ALU.mult), inc=c_ve)
        if oc == 0:
            q_evac_ac = {}
        q_evac_ac[oc] = c_ve.n
        b.wait("scalar", c_ve)
        b.dma("scalar", qs, q_d.ap()[oc * 128:(oc + 1) * 128, :], QSt[:, par * T:(par + 1) * T])

    b.wait("tensor", s_cst)
    b.op("tensor", lambda t: t.matmul(PSc[:, 256:512], OnesR[:, :], BiasVr[:, :], start=True, stop=True), inc=c_pe)
    b.wait("vector", c_pe)
    b.op("vector", lambda v: v.tensor_copy(BvB[:, :], PSc[:, 256:512]), inc=c_ve)

    gath_scope.close()
    kvb_scope = ExitStack()
    LnS = kvb_scope.enter_context(nc.sbuf_tensor("LnS", [128, 2 * 16 * REG], BF16))
    KOut = kvb_scope.enter_context(nc.sbuf_tensor("KOut", [128, 2 * NSLOT], BF16))
    VOut = kvb_scope.enter_context(nc.sbuf_tensor("VOut", [128, SC * 256], BF16))
    WkB = kvb_scope.enter_context(nc.sbuf_tensor("WkB", [128, 16 * 256], BF16))
    WvB = kvb_scope.enter_context(nc.sbuf_tensor("WvB", [128, 16 * 256], BF16))
    b.dma("gpsimd", s_wc, WkB[:, :], wkbh_h)
    b.dma("gpsimd", s_wc, WvB[:, :], wvbh_h)
    b.wait("gpsimd", s_cc, cc_lng)
    kv_pe = {}
    kv_ve = {}
    for r in range(NCORE):
        parl = r % 2
        lns = LnS[:, parl * 16 * REG:(parl + 1) * 16 * REG]
        if r >= 2:
            b.wait("gpsimd", c_pe, kv_pe[r - 2])
        b.dma("gpsimd", s_lng, lns,
              lng_all_d.ap()[r * 2048:(r + 1) * 2048, :].rearrange("(p kc) s -> p (kc s)", p=128))
        b.wait("tensor", s_lng, s_lng.n)
        if r == 0:
            b.wait("tensor", c_ve)
            b.wait("tensor", s_wc, s_wc.n)
        else:
            b.wait("tensor", c_ve, kv_ve[r - 1])
        for sub in range(2):
            ps = PS[sub][:, 0:REG]
            for kcc in range(16):
                b.op("tensor", lambda t, o=ps, lw=WkB[:, kcc * 256 + sub * 128:kcc * 256 + sub * 128 + 128],
                     i=lns[:, kcc * REG:(kcc + 1) * REG], stt=(kcc == 0), spp=(kcc == 15):
                     t.matmul(o, lw, i, start=stt, stop=spp),
                     inc=(c_pe if kcc == 15 else None))
            b.wait("vector", c_pe)
            b.op("vector", lambda v, o=KOut[:, sub * NSLOT + r * REG:sub * NSLOT + (r + 1) * REG],
                 i=ps, bi=BiasKbh[:, sub:sub + 1], vb=Validb[:, r * REG:(r + 1) * REG]:
                 v.scalar_tensor_tensor(o, i, bi, vb, ALU.add, ALU.mult), inc=c_ve)
        for scl in range(3):
            ps = PS[2 + scl % 2][:, 0:256]
            sgl = r * 3 + scl
            if scl >= 2:
                b.wait("tensor", c_ve, v_ve0 + (scl - 2) + 1)
            for kcc in range(16):
                b.op("tensor", lambda t, o=ps, lw=lns[:, kcc * REG + scl * 128:kcc * REG + scl * 128 + 128],
                     i=WvB[:, kcc * 256:(kcc + 1) * 256], stt=(kcc == 0), spp=(kcc == 15):
                     t.matmul(o, lw, i, start=stt, stop=spp),
                     inc=(c_pe if kcc == 15 else None))
            b.wait("vector", c_pe)
            if scl == 0:
                v_ve0 = c_ve.n
            b.op("vector", lambda v, i=ps: v.tensor_add(Sq2[:, 0:256], i, BvB[:, :]))
            b.op("vector", lambda v, o=VOut[:, sgl * 256:(sgl + 1) * 256], vt=ValidT[:, sgl:sgl + 1]:
                 v.tensor_scalar(o, Sq2[:, 0:256], vt, None, ALU.mult), inc=c_ve)
        kv_pe[r] = c_pe.n
        kv_ve[r] = c_ve.n
    b.wait("scalar", c_ve)
    for sub, (kv_own, kv_all) in enumerate([(kv_own_e_d, kv_all_e_d), (kv_own_o_d, kv_all_o_d)]):
        b.dma("scalar", s_kvst,
              bass.AP(kv_own, 0, [[NSLOT, 128], [1, NSLOT]]),
              KOut[:, sub * NSLOT:(sub + 1) * NSLOT])
        vsrc = bass.AP(VOut, sub * 128, [[SC * 256, 128], [256, SC], [1, 128]])
        b.dma("scalar", s_kvst,
              bass.AP(kv_own, 384 * 1024, [[SC * 128, 128], [128, SC], [1, 128]]),
              vsrc)
        b.wait("gpsimd", s_kvst, s_kvst.n)
        b.op("gpsimd", lambda g, i=kv_own, o=kv_all: g.collective_compute(
            "AllGather", ALU.bypass, replica_groups=[list(range(NCORE))],
            ins=[i.ap()], outs=[o.ap()]), inc=s_cc, k=1)
        if sub == 0:
            cc_kv_e = s_cc.n
    cc_kv_o = s_cc.n
    kvb_scope.close()
    ph2_scope.close()

    w2p_scope = ExitStack()
    W2P = w2p_scope.enter_context(nc.sbuf_tensor("W2P", [128, HC * 256], BF16))
    b.wait("vector", c_ve, w2_h128_ve)
    CW = 256
    cur, oth = W2T, W2P
    for si in range(6):
        half = 1 << si
        nblk = HC // (2 * half)
        apl = [[HC * CW, 128], [2 * half * CW, nblk], [1, half * CW]]
        ev_s = bass.AP(cur, 0, apl)
        od_s = bass.AP(cur, half * CW, apl)
        ev_d = bass.AP(oth, 0, apl)
        od_d = bass.AP(oth, half * CW, apl)
        b.op("vector", lambda v, a=ev_d, x=ev_s, y=od_s: v.tensor_add(a, x, y))
        b.op("vector", lambda v, a=od_d, x=ev_s, y=od_s: v.tensor_sub(a, x, y),
             inc=(c_ve if si == 5 else None))
        cur, oth = oth, cur
    w2fin = cur
    b.wait("scalar", c_ve)
    b.dma("scalar", s_w2st, w2t_own_d.ap().rearrange("(p hc) d -> p (hc d)", p=128), w2fin[:, :])
    b.wait("gpsimd", s_w2st, s_w2st.n)
    b.op("gpsimd", lambda g: g.collective_compute(
        "AllGather", ALU.bypass, replica_groups=[list(range(NCORE))],
        ins=[w2t_own_d.ap()], outs=[w2t_all_d.ap()]), inc=s_cc, k=1)
    cc_w2 = s_cc.n
    w2p_scope.close()
    w2t_scope.close()

    att_scope = ExitStack()
    KH = att_scope.enter_context(nc.sbuf_tensor("KH", [128, 2 * NSLOT], BF16))
    VH = att_scope.enter_context(nc.sbuf_tensor("VH", [128, 2 * SC * 128], BF16))
    QH = att_scope.enter_context(nc.sbuf_tensor("QH", [128, 2 * T], BF16))
    NRING = 8
    LAG = 5
    PR = att_scope.enter_context(nc.sbuf_tensor("PR", [128, NRING * T], BF16))
    b.wait("scalar", s_qsa, s_qsa.n)
    b.wait("scalar", s_qsb, s_qsb.n)
    b.wait("tensor", c_ve)
    NJ = NH * SC
    ORD = list(range(0, NH, 2)) + list(range(1, NH, 2))
    sc_pe = {}
    av_pe = {}
    den_pe = {}
    ex_ac = {}
    epi_ve = {}
    epi1_ve = {}
    ksem_at = {}
    vsem_at = {}

    def att_load(pos):
        h = ORD[pos]
        par = pos % 2
        hb = h // 2
        kv_all = kv_all_e_d if h % 2 == 0 else kv_all_o_d
        cc_need = cc_kv_e if h % 2 == 0 else cc_kv_o
        ksem = s_kva if par == 0 else s_kvb
        vsem = s_va if par == 0 else s_vb
        b.wait("sync", s_cc, cc_need)
        b.wait("gpsimd", s_cc, cc_need)
        b.wait("scalar", s_cc, cc_need)
        if pos >= 2:
            prev = av_pe[(pos - 2) * SC + SC - 1]
            b.wait("sync", c_pe, prev)
            b.wait("gpsimd", c_pe, prev)
            b.wait("scalar", c_pe, prev)
        b.dma("sync", ksem, KH[:, par * NSLOT:(par + 1) * NSLOT],
              bass.AP(kv_all, hb * 768 * 1024, [[NSLOT, 128], [1, NSLOT]]))
        b.dma("gpsimd", vsem, VH[:, par * SC * 128:(par + 1) * SC * 128],
              bass.AP(kv_all, hb * 768 * 1024 + 384 * 1024,
                      [[SC * 128, 128], [128, SC], [1, 128]]))
        b.dma("scalar", ksem, QH[:, par * T:(par + 1) * T], q_d.ap()[h * 128:(h + 1) * 128, :])
        ksem_at[pos] = (ksem, ksem.n)
        vsem_at[pos] = (vsem, vsem.n)

    def att_scores(j):
        pos, kcc = divmod(j, SC)
        par = pos % 2
        if kcc == 0:
            ksem, kn = ksem_at[pos]
            b.wait("tensor", ksem, kn)
        pst = PS[j % 2]
        if j >= 2:
            b.wait("tensor", c_ac, ex_ac[j - 2])
        for hh in range(2):
            b.op("tensor", lambda t, o=pst[:, hh * 512:hh * 512 + 512],
                 lw=KH[:, par * NSLOT + kcc * 128:par * NSLOT + kcc * 128 + 128],
                 i=QH[:, par * T + hh * 512:par * T + hh * 512 + 512]:
                 t.matmul(o, lw, i, start=True, stop=True),
                 inc=(c_pe if hh == 1 else None))
        sc_pe[j] = c_pe.n

    def att_exp(j):
        ring = j % NRING
        pst = PS[j % 2]
        b.wait("scalar", c_pe, sc_pe[j])
        if j >= NRING:
            b.wait("scalar", c_pe, av_pe[j - NRING])
        b.op("scalar", lambda s, o=PR[:, ring * T:(ring + 1) * T], i=pst[:, 0:1024]:
             s.activation(o, i, AF.Exp, bias=NegC[:, :], scale=1.0), inc=c_ac)
        ex_ac[j] = c_ac.n

    def att_denav(j):
        pos, kcc = divmod(j, SC)
        par = pos % 2
        ring = j % NRING
        b.wait("tensor", c_ac, ex_ac[j])
        if kcc == 0:
            vsem, vn = vsem_at[pos]
            b.wait("tensor", vsem, vn)
        if kcc == 0 and pos >= 1:
            b.wait("tensor", c_ve, epi1_ve[pos - 1])
        for hh in range(2):
            b.op("tensor", lambda t, o=PSc[:, hh * 512:hh * 512 + 512],
                 i=PR[:, ring * T + hh * 512:ring * T + hh * 512 + 512],
                 stt=(kcc == 0), spp=(kcc == SC - 1):
                 t.matmul(o, OnesSqB[:, :], i, start=stt, stop=spp),
                 inc=(c_pe if (kcc == SC - 1 and hh == 1) else None))
        if kcc == SC - 1:
            den_pe[pos] = c_pe.n
        if kcc == 0 and pos >= 1:
            b.wait("tensor", c_ve, epi_ve[pos - 1])
        for hh in range(2):
            b.op("tensor", lambda t, o=PSd[:, hh * 512:hh * 512 + 512],
                 lw=VH[:, par * SC * 128 + kcc * 128:par * SC * 128 + kcc * 128 + 128],
                 i=PR[:, ring * T + hh * 512:ring * T + hh * 512 + 512],
                 stt=(kcc == 0), spp=(kcc == SC - 1):
                 t.matmul(o, lw, i, start=stt, stop=spp),
                 inc=(c_pe if hh == 1 else None))
        av_pe[j] = c_pe.n

    def att_epilogue(pos):
        h = ORD[pos]
        b.wait("vector", c_pe, den_pe[pos])
        b.op("vector", lambda v: v.tensor_scalar(Sq2[:, 0:1024], PSc[:, 0:1024], PADC, None, ALU.add),
             inc=c_ve)
        epi1_ve[pos] = c_ve.n
        b.op("vector", lambda v: v.reciprocal(SqB[:, 0:1024], Sq2[:, 0:1024]))
        b.wait("vector", c_pe, av_pe[pos * SC + SC - 1])
        b.op("vector", lambda v, o=A[:, h * T:(h + 1) * T]:
             v.tensor_tensor(o, PSd[:, 0:1024], SqB[:, 0:1024], ALU.mult), inc=c_ve)
        epi_ve[pos] = c_ve.n

    att_load(0)
    att_load(1)
    for j in range(NJ + LAG):
        if j < NJ:
            pos, kcc = divmod(j, SC)
            if kcc == LAG and pos >= 1 and pos + 1 < NH:
                att_load(pos + 1)
            att_scores(j)
            att_exp(j)
        jj = j - LAG
        if jj >= 0:
            att_denav(jj)
            if jj % SC == SC - 1:
                att_epilogue(jj // SC)
    att_scope.close()

    XC = XS
    wo_pe = {}
    wo_ve = {}
    q_sq_pe = {}
    x1_written = {}
    ln2_ac0 = None
    for oc in range(16):
        par = oc % 2
        buf = WsA[:, par * 2048:(par + 1) * 2048]
        wsem = s_wa if par == 0 else s_wb
        xsem = s_x1a if par == 0 else s_x1b
        sps = s_spa if par == 0 else s_spb
        if oc >= 2:
            b.wait("gpsimd", c_pe, wo_pe[oc - 2])
            b.wait("sync", c_pe, q_sq_pe[oc - 2])
            b.wait("sync", sps, sps.n)
        b.dma("gpsimd", wsem, buf, wo_h[oc])
        b.dma("sync", xsem, XC[:, par * T:(par + 1) * T], xT_h[oc * 128:(oc + 1) * 128, :])
        b.wait("tensor", wsem, wsem.n)
        if oc == 0:
            b.wait("tensor", c_ve, epi_ve[NH - 1])
        pst = PS[par]
        for kcc in range(16):
            for hh in range(2):
                b.op("tensor", lambda t, o=pst[:, hh * 512:hh * 512 + 512],
                     lw=buf[:, kcc * 128:kcc * 128 + 128],
                     i=A[:, kcc * T + hh * 512:kcc * T + hh * 512 + 512],
                     stt=(kcc == 0), spp=(kcc == 15):
                     t.matmul(o, lw, i, start=stt, stop=spp),
                     inc=(c_pe if (kcc == 15 and hh == 1) else None))
        wo_pe[oc] = c_pe.n
        b.wait("vector", c_pe)
        b.wait("vector", xsem, xsem.n)
        xcv = XC[:, par * T:(par + 1) * T]
        b.op("vector", lambda v, o=xcv, i=pst[:, 0:1024], bi=BiasO[:, oc:oc + 1]:
             v.scalar_tensor_tensor(o, i, bi, o, ALU.add, ALU.add), inc=c_ve)
        wo_ve[oc] = c_ve.n
        b.wait("scalar", c_ve, wo_ve[oc])
        b.dma("scalar", sps, x1_d.ap()[oc * 128:(oc + 1) * 128, :], xcv)
        b.op("scalar", lambda s, o=Sq2[:, 0:1024], i=xcv: s.activation(o, i, AF.Square), inc=c_ac)
        if ln2_ac0 is None:
            ln2_ac0 = c_ac.n - 1
        b.wait("tensor", c_ve, wo_ve[oc])
        for hh in range(2):
            b.op("tensor", lambda t, o=PSc[:, hh * 512:hh * 512 + 512],
                 i=xcv[:, hh * 512:hh * 512 + 512], stt=(oc == 0), spp=(oc == 15):
                 t.matmul(o, OnesSq[:, :], i, start=stt, stop=spp))
        b.wait("tensor", c_ac, ln2_ac0 + oc + 1)
        for hh in range(2):
            b.op("tensor", lambda t, o=PSd[:, hh * 512:hh * 512 + 512],
                 i=Sq2[:, hh * 512:hh * 512 + 512], stt=(oc == 0), spp=(oc == 15):
                 t.matmul(o, OnesSq[:, :], i, start=stt, stop=spp),
                 inc=c_pe)
        q_sq_pe[oc] = c_pe.n
    b.wait("vector", c_pe)
    ln2_fin_ve = emit_ln_finalize(PSc[:, 0:1024], PSd[:, 0:1024])
    XL = XS
    ln2_ve = {}
    for q in ("sync", "scalar"):
        b.wait(q, c_pe)
        b.wait(q, s_spa)
        b.wait(q, s_spb)
    for cch in range(KC):
        par = cch % 2
        xsem = s_x1a if par == 0 else s_x1b
        xq = "sync" if par == 0 else "scalar"
        if cch >= 2:
            b.wait(xq, c_ve, ln2_ve[cch - 2])
        b.dma(xq, xsem, XL[:, par * T:(par + 1) * T], x1_d.ap()[cch * 128:(cch + 1) * 128, :])
        b.wait("vector", xsem, xsem.n)
        xc = XL[:, par * T:(par + 1) * T]
        tmp = SqB[:, par * 1024:par * 1024 + 1024]
        b.op("vector", lambda v, a=tmp, x=xc: v.tensor_tensor(a, x, RstdB[:, :], ALU.mult))
        b.op("vector", lambda v, a=tmp, d=A[:, cch * T:(cch + 1) * T]:
             v.tensor_tensor(d, a, MrB[:, :], ALU.add), inc=c_ve)
        ln2_ve[cch] = c_ve.n
    ln2_done_ve = c_ve.n

    mid_scope.close()
    ffn_scope = ExitStack()
    Hid = ffn_scope.enter_context(nc.sbuf_tensor("Hid", [128, HC * T], BF16, side="right"))
    hid_ve = {}
    ffn_pe = {}
    silu_ac = {}
    b.wait("tensor", c_ve, ln2_done_ve)
    for hc in range(HC):
        par = hc % 2
        b1 = WsA[:, par * 2048:(par + 1) * 2048]
        b3 = WsB[:, par * 2048:(par + 1) * 2048]
        s1 = s_wa if par == 0 else s_wb
        s3 = s_wc if par == 0 else s_wd
        if hc >= 2:
            b.wait("gpsimd", c_pe, ffn_pe[hc - 2])
        b.dma("gpsimd", s1, b1, w1_h[hc])
        b.dma("gpsimd", s3, b3, w3_h[hc])
        b.wait("tensor", s1, s1.n)
        b.wait("tensor", s3, s3.n)
        psg = PS[2 * par]
        psl = PS[2 * par + 1]
        if hc >= 2:
            b.wait("tensor", c_ve, hid_ve[hc - 2])
            b.wait("tensor", c_ac, silu_ac[hc - 2])
        for kcc in range(16):
            for hh in range(2):
                b.op("tensor", lambda t, o=psg[:, hh * 512:hh * 512 + 512],
                     lw=b1[:, kcc * 128:kcc * 128 + 128],
                     i=A[:, kcc * T + hh * 512:kcc * T + hh * 512 + 512],
                     stt=(kcc == 0), spp=(kcc == 15):
                     t.matmul(o, lw, i, start=stt, stop=spp),
                     inc=(c_pe if (kcc == 15 and hh == 1) else None))
        gate_pe = c_pe.n
        for kcc in range(16):
            for hh in range(2):
                b.op("tensor", lambda t, o=psl[:, hh * 512:hh * 512 + 512],
                     lw=b3[:, kcc * 128:kcc * 128 + 128],
                     i=A[:, kcc * T + hh * 512:kcc * T + hh * 512 + 512],
                     stt=(kcc == 0), spp=(kcc == 15):
                     t.matmul(o, lw, i, start=stt, stop=spp),
                     inc=(c_pe if (kcc == 15 and hh == 1) else None))
        ffn_pe[hc] = c_pe.n
        b.wait("scalar", c_pe, gate_pe)
        b.op("scalar", lambda s, o=QSt[:, par * T:(par + 1) * T], i=psg[:, 0:1024]:
             s.activation(o, i, AF.Silu), inc=c_ac)
        silu_ac[hc] = c_ac.n
        b.wait("vector", c_ac, silu_ac[hc])
        b.wait("vector", c_pe, ffn_pe[hc])
        b.op("vector", lambda v, o=Hid[:, hc * T:(hc + 1) * T], g=QSt[:, par * T:(par + 1) * T],
             l=psl[:, 0:1024]: v.tensor_tensor(o, g, l, ALU.mult), inc=c_ve)
        hid_ve[hc] = c_ve.n

    ws_scope.close()
    w2s_scope = ExitStack()
    W2S = w2s_scope.enter_context(nc.sbuf_tensor("W2S", [128, 2 * 8 * 256], BF16))
    X1R = XS
    YSt = w2s_scope.enter_context(nc.sbuf_tensor("YSt", [128, 1024], F32))
    b.wait("sync", s_cc, cc_w2)
    b.wait("sync", s_spa)
    b.wait("sync", s_spb)
    b.wait("sync", c_ve, ln2_done_ve)
    ld_i = 0
    w2ld_pe = {}
    ev_ve = {}
    pair_free_ve = {}
    ytr_ve = {}
    ytr_i = 0
    for r in range(NCORE):
        pair = (PS[0], PS[1])
        w2src = w2t_all_d.ap()[r * 128 * HC:(r + 1) * 128 * HC, :].rearrange(
            "(p hc) d -> p (hc d)", p=128)
        for oc2 in range(2):
            xsem = s_x1a if oc2 == 0 else s_x1b
            if r >= 1:
                b.wait("scalar", c_ve, ev_ve[(r - 1, oc2)])
            b.dma("scalar", xsem, X1R[:, oc2 * 1024:(oc2 + 1) * 1024],
                  x1_d.ap()[(r * 2 + oc2) * 128:(r * 2 + oc2 + 1) * 128, :])
        for lg in range(8):
            parw = ld_i % 2
            wsem = s_w2sA if parw == 0 else s_w2sB
            if ld_i >= 2:
                b.wait("sync", c_pe, w2ld_pe[ld_i - 2])
            elif ld_i == 0:
                b.wait("sync", c_pe, ffn_pe[HC - 1])
            b.dma("sync", wsem, W2S[:, parw * 2048:(parw + 1) * 2048],
                  w2src[:, lg * 2048:(lg + 1) * 2048])
            b.wait("tensor", wsem, wsem.n)
            if r == 0 and lg == 0:
                b.wait("tensor", c_ve, hid_ve[HC - 1])
                b.wait("tensor", c_ac, silu_ac[HC - 1])
            if lg == 0 and r >= 1:
                b.wait("tensor", c_ve, ev_ve[(r - 1, 1)])
            for hcl in range(8):
                hcg = lg * 8 + hcl
                for oc2 in range(2):
                    for th in range(2):
                        b.op("tensor", lambda t, o=pair[oc2][:, th * 512:(th + 1) * 512],
                             lw=W2S[:, parw * 2048 + hcl * 256 + oc2 * 128:
                                    parw * 2048 + hcl * 256 + oc2 * 128 + 128],
                             i=Hid[:, hcg * T + th * 512:hcg * T + th * 512 + 512],
                             stt=(hcg == 0), spp=(hcg == HC - 1):
                             t.matmul(o, lw, i, start=stt, stop=spp),
                             inc=(c_pe if (hcl == 7 and oc2 == 1 and th == 1) else None))
            w2ld_pe[ld_i] = c_pe.n
            ld_i += 1
        for oc2 in range(2):
            d2c = r * 2 + oc2
            xsem = s_x1a if oc2 == 0 else s_x1b
            ysem = s_sta
            b.wait("vector", c_pe, w2ld_pe[ld_i - 1])
            b.wait("vector", xsem, xsem.n)
            yf = SqB[:, oc2 * 1024:(oc2 + 1) * 1024]
            b.op("vector", lambda v, o=yf, i=pair[oc2][:, 0:1024],
                 x=X1R[:, oc2 * 1024:(oc2 + 1) * 1024]:
                 v.tensor_tensor(o, i, x, ALU.add), inc=c_ve)
            ev_ve[(r, oc2)] = c_ve.n
            b.wait("tensor", c_ve, ev_ve[(r, oc2)])
            if r * 2 + oc2 >= 1:
                b.wait("vector", ysem, ysem.n)
                b.wait("tensor", c_ve, ytr_ve[ytr_i - 1])
            for j in range(8):
                pstr = PS[2 + j // 4][:, (j % 4) * 128:(j % 4) * 128 + 128]
                b.op("tensor", lambda t, o=pstr, i=yf[:, j * 128:j * 128 + 128]:
                     t.transpose(o, i, Ident[:, :]), inc=(c_pe if j == 7 else None))
            b.wait("vector", c_pe)
            for j in range(8):
                pstr = PS[2 + j // 4][:, (j % 4) * 128:(j % 4) * 128 + 128]
                b.op("vector", lambda v, o=YSt[:, j * 128:(j + 1) * 128],
                     i=pstr: v.tensor_copy(o, i), inc=(c_ve if j == 7 else None))
            ytr_ve[ytr_i] = c_ve.n
            ytr_i += 1
            b.wait("scalar", c_ve, ytr_ve[ytr_i - 1])
            for j in range(8):
                b.dma("scalar", ysem,
                      y_h[j * 128:(j + 1) * 128, d2c * 128:(d2c + 1) * 128],
                      YSt[:, j * 128:(j + 1) * 128])
        pair_free_ve[r] = ytr_ve[ytr_i - 1]
    w2s_scope.close()
    ffn_scope.close()
    b.wait("scalar", s_sta, s_sta.n)
    b.wait("scalar", s_stb, s_stb.n)

    if verbose:
        print("sbuf bytes remaining:", nc.sbuf_bytes_remaining)
        print("instr counts:", {e: len(b.q[e]) for e in ENGS})

    with nc.Block() as block:
        b.replay(block)
    st.close()
    return nc


def _hadamard(n):
    h = np.array([[1.0]], dtype=np.float64)
    while h.shape[0] < n:
        h = np.block([[h, h], [h, -h]])
    return h


def _prep_inputs(inputs):
    f32 = np.float32
    x = np.asarray(inputs["L0_attn_x"], f32).reshape(S, D)
    Wq = np.asarray(inputs["L0_attn_Wq"], f32)
    bq = np.asarray(inputs["L0_attn_bq"], f32)
    Wk = np.asarray(inputs["L0_attn_Wk"], f32)
    bk = np.asarray(inputs["L0_attn_bk"], f32)
    Wv = np.asarray(inputs["L0_attn_Wv"], f32)
    bv = np.asarray(inputs["L0_attn_bv"], f32)
    Wo = np.asarray(inputs["L0_attn_Wo"], f32)
    bo = np.asarray(inputs["L0_attn_bo"], f32)
    Wi = np.asarray(inputs["L0_attn_Wi"], f32)
    bi = np.asarray(inputs["L0_attn_bi"], f32)
    W1 = np.asarray(inputs["L0_ffn_W1"], f32)
    W2 = np.asarray(inputs["L0_ffn_W2"], f32)
    W3 = np.asarray(inputs["L0_ffn_W3"], f32)

    def tile_oc(W):
        return np.ascontiguousarray(
            W.reshape(16, 128, 16, 128).transpose(2, 1, 0, 3).reshape(16, 128, 16 * 128))

    def tile_hc(Wh):
        return np.ascontiguousarray(
            Wh.reshape(64, 128, 16, 128).transpose(0, 3, 2, 1).reshape(64, 128, 16 * 128))

    wq_t = tile_oc(Wq)
    wo_t = tile_oc(Wo)
    w1_t = tile_hc(W1)
    w3_t = tile_hc(W3)
    bq_t = np.ascontiguousarray(bq.reshape(16, 128).T)
    bo_t = np.ascontiguousarray(bo.reshape(16, 128).T)
    bk_t = np.ascontiguousarray(bk.reshape(16, 128).T)
    wi_t = np.ascontiguousarray(Wi.reshape(16, 128).T)
    h128 = _hadamard(128).astype(ml_dtypes.bfloat16)
    ident = np.eye(128, dtype=f32)
    ones_c = np.ones((128, 1), f32)
    ones_cb = np.ones((128, 1), ml_dtypes.bfloat16)
    ones_r = np.ones((1, 128), f32)
    ones_rb = np.ones((1, 128), ml_dtypes.bfloat16)
    ones_sq = np.ones((128, 128), f32)
    ones_sqb = np.ones((128, 128), ml_dtypes.bfloat16)
    rind = np.zeros((128, 8), f32)
    for p in range(128):
        rind[p, p // 16] = 1.0
    iota384 = np.arange(REG, dtype=f32).reshape(1, REG)

    xT = np.ascontiguousarray(x.T)
    maps = []
    for c in range(NCORE):
        sub0 = 2 * c
        m = dict(
            xT=np.ascontiguousarray(xT[:, c * T:(c + 1) * T]),
            xrows=np.ascontiguousarray(x[c * T:(c + 1) * T, :]),
            wq_t=wq_t, wo_t=wo_t, w1_t=w1_t, w3_t=w3_t,
            wk_bh=np.ascontiguousarray(
                Wk[:, sub0 * 128:(sub0 + 2) * 128].reshape(16, 128, 256)
                .transpose(1, 0, 2).reshape(128, 16 * 256)),
            wv_bh=np.ascontiguousarray(
                Wv[:, sub0 * 128:(sub0 + 2) * 128].reshape(16, 128, 256)
                .transpose(1, 0, 2).reshape(128, 16 * 256)),
            w2_slt=np.ascontiguousarray(
                W2[c * 256:(c + 1) * 256, :].reshape(256, 64, 128)
                .transpose(2, 1, 0).reshape(128, 64 * 256)),
            wi_t=wi_t, bq_t=bq_t, bo_t=bo_t,
            bk_bh=np.ascontiguousarray(bk_t[:, sub0:sub0 + 2]),
            bv_row=np.ascontiguousarray(bv[sub0 * 128:(sub0 + 2) * 128].reshape(1, 256)),
            bi=np.asarray(bi, f32).reshape(1, 1),
            ones_c=ones_c, ones_cb=ones_cb, ones_r=ones_r, ones_rb=ones_rb, ones_sq=ones_sq,
            ones_sqb=ones_sqb,
            ident=ident, h128=h128, rind=rind, iota384=iota384,
        )
        maps.append(m)
    return maps


_NC_CACHE = {}


def kernel(**inputs):
    maps = _prep_inputs(inputs)
    if "nc" not in _NC_CACHE:
        _NC_CACHE["nc"] = build()
    nc = _NC_CACHE["nc"]
    res = run_bass_kernel_spmd(nc, maps, core_ids=list(range(NCORE)))
    y = np.concatenate([np.asarray(res.results[c]["y"]) for c in range(NCORE)], axis=0)
    return y.reshape(1, S, D)


if __name__ == "__main__":
    nc = build(verbose=True)
    print("built ok")


# version 28
# speedup vs baseline: 1.0386x; 1.0095x over previous
import math
import numpy as np
import ml_dtypes
from contextlib import ExitStack

import concourse.bass as bass
import concourse.mybir as mybir
from concourse.bass_utils import run_bass_kernel_spmd
from concourse.bass import IndirectOffsetOnAxis

F32 = mybir.dt.float32
BF16 = mybir.dt.bfloat16
I32 = mybir.dt.int32
AF = mybir.ActivationFunctionType
ALU = mybir.AluOpType
AX = mybir.AxisListType.X

NCORE = 8
S, D, H = 8192, 2048, 8192
T = S // NCORE
KC = D // 128
HC = H // 128
NH, HD = 16, 128
REG = 384
NSLOT = NCORE * REG
SC = NSLOT // 128
RANK_T = 1638
CSTAB = 16.0
QSCALE = 1.0 / math.sqrt(HD)
PADC = float((S - NSLOT) * math.exp(-CSTAB))
EPS = 1e-5
NBIS = 42

ENGS = ("sync", "gpsimd", "tensor", "scalar", "vector")


class Sem:
    def __init__(self, stack, nc, name):
        self.h = stack.enter_context(nc.semaphore(name))
        self.n = 0


class B:
    def __init__(self, nc, stack):
        self.nc = nc
        self.q = {e: [] for e in ENGS}
        self.stack = stack

    def sem(self, name):
        return Sem(self.stack, self.nc, name)

    def em(self, eng, fn):
        self.q[eng].append(fn)

    trace = []

    def dma(self, eng, sem, out, in_):
        self.trace.append(("dma", eng, sem.h.name, sem.n + 16))
        self.em(eng, lambda E, o=out, i=in_, s=sem.h: E.dma_start(out=o, in_=i).then_inc(s, 16))
        sem.n += 16
        return sem.n

    def dmac(self, eng, sem, fn):
        self.em(eng, lambda E, f=fn, s=sem.h: f(E).then_inc(s, 16))
        sem.n += 16
        return sem.n

    def wait(self, eng, sem, val=None):
        v = sem.n if val is None else val
        self.trace.append(("wait", eng, sem.h.name, v))
        self.em(eng, lambda E, s=sem.h, v=v: E.wait_ge(s, v))

    def op(self, eng, fn, inc=None, k=1):
        if inc is None:
            self.em(eng, fn)
        else:
            self.em(eng, lambda E, f=fn, s=inc.h, kk=k: f(E).then_inc(s, kk))
            inc.n += k

    def replay(self, block):
        qs = self.q

        @block.sync
        def _(e):
            for f in qs["sync"]:
                f(e)

        @block.gpsimd
        def _(e):
            for f in qs["gpsimd"]:
                f(e)

        @block.tensor
        def _(e):
            for f in qs["tensor"]:
                f(e)

        @block.scalar
        def _(e):
            for f in qs["scalar"]:
                f(e)

        @block.vector
        def _(e):
            for f in qs["vector"]:
                f(e)


def build(verbose=False):
    nc = bass.Bass("TRN2", target_bir_lowering=False)
    st = ExitStack()
    b = B(nc, st)

    def inp(name, shape, dt=F32):
        return nc.dram_tensor(name, shape, dt, kind="ExternalInput").ap()

    xT_h = inp("xT", [D, T])
    xrows_h = inp("xrows", [T, D])
    wq_h = inp("wq_t", [16, 128, 16 * 128])
    wo_h = inp("wo_t", [16, 128, 16 * 128])
    wkbh_h = inp("wk_bh", [128, 16 * 256])
    wvbh_h = inp("wv_bh", [128, 16 * 256])
    w1_h = inp("w1_t", [64, 128, 16 * 128])
    w3_h = inp("w3_t", [64, 128, 16 * 128])
    w2_h = inp("w2_slt", [128, HC * 256])
    wi_h = inp("wi_t", [128, 16])
    bq_h = inp("bq_t", [128, 16])
    bo_h = inp("bo_t", [128, 16])
    bkbh_h = inp("bk_bh", [128, 2])
    bvrow_h = inp("bv_row", [1, 256])
    bi_h = inp("bi", [1, 1])
    ones_c_h = inp("ones_c", [128, 1])
    ones_cb_h = inp("ones_cb", [128, 1], BF16)
    ones_r_h = inp("ones_r", [1, 128])
    ones_rb_h = inp("ones_rb", [1, 128], BF16)
    ones_sq_h = inp("ones_sq", [128, 128])
    ones_sqb_h = inp("ones_sqb", [128, 128], BF16)
    ident_h = inp("ident", [128, 128])
    h128_h = inp("h128", [128, 128], BF16)
    rind_h = inp("rind", [128, 8])
    iota384_h = inp("iota384", [1, REG])

    y_t = nc.dram_tensor("y", [T, D], F32, kind="ExternalOutput")
    y_h = y_t.ap()

    imp_own_d = nc.dram_tensor("imp_own_d", [T, 1], F32)
    imp_all_d = nc.dram_tensor("imp_all_d", [S, 1], F32)
    lng_own_d = nc.dram_tensor("lng_own_d", [128 * 16, REG], BF16)
    lng_all_d = nc.dram_tensor("lng_all_d", [NCORE * 16 * 128, REG], BF16)
    kv_own_e_d = nc.dram_tensor("kv_own_e_d", [768, 1024], BF16)
    kv_own_o_d = nc.dram_tensor("kv_own_o_d", [768, 1024], BF16)
    kv_all_e_d = nc.dram_tensor("kv_all_e_d", [NCORE * 768, 1024], BF16)
    kv_all_o_d = nc.dram_tensor("kv_all_o_d", [NCORE * 768, 1024], BF16)
    gath_d = nc.dram_tensor("gath_d", [512, D], F32)
    w2t_own_d = nc.dram_tensor("w2t_own_d", [128 * HC, 256], BF16)
    w2t_all_d = nc.dram_tensor("w2t_all_d", [NCORE * 128 * HC, 256], BF16)
    q_d = nc.dram_tensor("q_d", [NH * 128, T], BF16)
    x1_d = nc.dram_tensor("x1_d", [KC * 128, T], F32)

    PSa = nc.alloc_psum_tensor("PSa", [128, 1024], F32)
    PSb = nc.alloc_psum_tensor("PSb", [128, 1024], F32)
    PSc = nc.alloc_psum_tensor("PSc", [128, 1024], F32)
    PSd = nc.alloc_psum_tensor("PSd", [128, 1024], F32)
    PS = [PSa, PSb, PSc, PSd]

    A = nc.alloc_sbuf_tensor("A", [128, KC * T], BF16)
    SqB = nc.alloc_sbuf_tensor("SqB", [128, 2048], F32)
    QSt = nc.alloc_sbuf_tensor("QSt", [128, 2 * T], BF16)
    XS = nc.alloc_sbuf_tensor("XS", [128, 2048], F32)
    Ident = nc.alloc_sbuf_tensor("Ident", [128, 128], F32)
    ws_scope = ExitStack()
    WsA = ws_scope.enter_context(nc.sbuf_tensor("WsA", [128, 2 * 2048], BF16))
    WsB = ws_scope.enter_context(nc.sbuf_tensor("WsB", [128, 2 * 2048], BF16))
    mid_scope = ExitStack()

    def mid(name, shape, dt=F32):
        return mid_scope.enter_context(nc.sbuf_tensor(name, shape, dt))

    Sq2 = mid("Sq2", [128, 1024])
    RowS = mid("RowS", [1, 2048])
    RstdB = mid("RstdB", [128, 1024])
    MrB = mid("MrB", [128, 1024])
    OnesSq = mid("OnesSq", [128, 128])
    OnesSqB = mid("OnesSqB", [128, 128], BF16)
    OnesR = mid("OnesR", [1, 128])
    OnesRb = mid("OnesRb", [1, 128], BF16)
    OnesC = mid("OnesC", [128, 1])
    OnesCb = mid("OnesCb", [128, 1], BF16)
    Rind = mid("Rind", [128, 8])
    Iota384 = mid("Iota384", [1, REG])
    BiasQ = mid("BiasQ", [128, 16])
    BiasQs = mid("BiasQs", [128, 16])
    BiasO = mid("BiasO", [128, 16])
    BiasKbh = mid("BiasKbh", [128, 2])
    BiasVr = mid("BiasVr", [1, 256])
    BvB = mid("BvB", [128, 256])
    BiT = mid("BiT", [1, 1])
    Wi = mid("Wi", [128, 16])
    Wib = mid("Wib", [128, 16], BF16)
    H128 = mid("H128", [128, 128], BF16)
    EpsCol = mid("EpsCol", [128, 1])
    NegC = mid("NegC", [128, 1])
    w2t_scope = ExitStack()
    W2T = w2t_scope.enter_context(nc.sbuf_tensor("W2T", [128, HC * 256], BF16))
    ph2_scope = ExitStack()

    def ph2(name, shape, dt=F32):
        return ph2_scope.enter_context(nc.sbuf_tensor(name, shape, dt))

    RowB = ph2("RowB", [1, NSLOT], BF16)
    ImpRow = ph2("ImpRow", [1, T], F32)
    Validb = ph2("Validb", [128, NSLOT], BF16)
    IdxT = ph2("IdxT", [128, 8], F32)
    IdxTi = ph2("IdxTi", [128, 8], I32)
    DmI = ph2("DmI", [128, 8], I32)
    PIota = ph2("PIota", [128, 1])
    ValidT = ph2("ValidT", [128, SC], F32)
    Bis = ph2("Bis", [128, 8], F32)
    BisI = ph2("BisI", [128, 1], I32)
    ImpAll = ph2("ImpAll", [128, 64], F32)
    Cnt = ph2("Cnt", [8, 1], F32)
    CntRow = ph2("CntRow", [1, 8], F32)

    gath_scope = ExitStack()
    Xtok = gath_scope.enter_context(nc.sbuf_tensor("Xtok", [128, 2 * D], F32))
    XgT = gath_scope.enter_context(nc.sbuf_tensor("XgT", [128, 16 * REG], F32))
    LnAll = gath_scope.enter_context(nc.sbuf_tensor("LnAll", [128, 16 * REG], BF16))
    s_cst = b.sem("s_cst")
    s_w2l = b.sem("s_w2l")
    s_wa = b.sem("s_wa")
    s_wb = b.sem("s_wb")
    s_wc = b.sem("s_wc")
    s_wd = b.sem("s_wd")
    s_xra = b.sem("s_xra")
    s_xrb = b.sem("s_xrb")
    s_rga = b.sem("s_rga")
    s_rgb = b.sem("s_rgb")
    s_reg = b.sem("s_reg")
    s_lng = b.sem("s_lng")
    s_kva = b.sem("s_kva")
    s_kvb = b.sem("s_kvb")
    s_out = b.sem("s_out")
    s_impst = b.sem("s_impst")
    s_lngst = b.sem("s_lngst")
    s_kvst = b.sem("s_kvst")
    s_w2st = b.sem("s_w2st")
    s_qsa = b.sem("s_qsa")
    s_qsb = b.sem("s_qsb")
    s_spa = b.sem("s_spa")
    s_spb = b.sem("s_spb")
    s_sta = b.sem("s_sta")
    s_stb = b.sem("s_stb")
    s_cc = b.sem("s_cc")
    s_dbg = b.sem("s_dbg")
    s_sp = b.sem("s_sp")
    s_x1a = b.sem("s_x1a")
    s_x1b = b.sem("s_x1b")
    s_w2sA = b.sem("s_w2sA")
    s_w2sB = b.sem("s_w2sB")
    s_st = b.sem("s_st")
    s_scat = b.sem("s_scat")
    s_zro = b.sem("s_zro")
    s_va = b.sem("s_va")
    s_vb = b.sem("s_vb")
    c_pe = b.sem("c_pe")
    c_ac = b.sem("c_ac")
    c_ve = b.sem("c_ve")
    c_gp = b.sem("c_gp")

    def opd(eng, fn, inc=None, k=1):
        b.em(eng, fn)
        if inc is None:
            b.em(eng, lambda E: E.drain())
        else:
            b.em(eng, lambda E, s=inc.h, kk=k: E.drain().then_inc(s, kk))
            inc.n += k

    for dst, src in [
        (OnesC[:, :], ones_c_h), (Rind[:, :], rind_h), (Ident[:, :], ident_h),
        (OnesSq[:, :], ones_sq_h), (OnesR[:, :], ones_r_h), (Iota384[:, :], iota384_h),
        (BiasQ[:, :], bq_h), (BiasO[:, :], bo_h), (BiasKbh[:, :], bkbh_h),
        (BiasVr[:, :], bvrow_h), (BiT[:, :], bi_h), (Wi[:, :], wi_h),
        (H128[:, :], h128_h), (OnesCb[:, :], ones_cb_h), (OnesRb[:, :], ones_rb_h),
        (OnesSqB[:, :], ones_sqb_h),
    ]:
        b.dma("sync", s_cst, dst, src)
    for i4 in range(4):
        b.dma("gpsimd", s_w2l, W2T[:, i4 * 4096:(i4 + 1) * 4096],
              w2_h[:, i4 * 4096:(i4 + 1) * 4096])
    b.wait("vector", s_cst)
    b.op("vector", lambda v: v.tensor_copy(Wib[:, :], Wi[:, :]))
    b.op("vector", lambda v: v.memset(Bis[:, :], 0.0))
    b.op("vector", lambda v: v.memset(BisI[:, :], 0))
    b.op("vector", lambda v: v.memset(IdxT[:, :], 0.0))
    b.op("vector", lambda v: v.memset(EpsCol[:, :], EPS))
    b.op("vector", lambda v: v.memset(NegC[:, :], -CSTAB))
    opd("vector", lambda v: v.tensor_scalar(BiasQs[:, :], BiasQ[:, :], QSCALE, None, ALU.mult), inc=c_ve)

    def emit_ln_finalize(ps_sum, ps_sq):
        b.op("vector", lambda v: v.tensor_scalar(Sq2[:, :], ps_sum, 1.0 / D, None, ALU.mult))
        b.op("vector", lambda v: v.tensor_scalar(RstdB[:, :], ps_sq, 1.0 / D, None, ALU.mult))
        b.op("vector", lambda v: v.tensor_mul(MrB[:, :], Sq2[:, :], Sq2[:, :]))
        b.op("vector", lambda v: v.tensor_sub(RstdB[:, :], RstdB[:, :], MrB[:, :]), inc=c_ve)
        b.wait("scalar", c_ve)
        b.op("scalar", lambda s: s.activation(RstdB[:, :], RstdB[:, :], AF.Ln, bias=EpsCol[:, :], scale=1.0))
        b.op("scalar", lambda s: s.activation(RstdB[:, :], RstdB[:, :], AF.Exp, scale=-0.5), inc=c_ac)
        b.wait("vector", c_ac)
        b.op("vector", lambda v: v.scalar_tensor_tensor(MrB[:, :], Sq2[:, :], -1.0, RstdB[:, :],
                                                        ALU.mult, ALU.mult), inc=c_ve)
        return c_ve.n

    b.wait("tensor", s_cst)
    ac0, pe0 = c_ac.n, c_pe.n
    for cch in range(KC):
        par = cch % 2
        xsem = s_x1a if par == 0 else s_x1b
        xc = XS[:, par * 1024:(par + 1) * 1024]
        if cch >= 2:
            b.wait("sync", c_pe, pe0 + (cch - 2) + 1)
        b.dma("sync", xsem, xc, xT_h[cch * 128:(cch + 1) * 128, :])
        b.wait("scalar", xsem, xsem.n)
        sq = SqB[:, par * 1024:par * 1024 + 1024]
        if cch >= 2:
            b.wait("scalar", c_pe, pe0 + (cch - 2) + 1)
        b.op("scalar", lambda s, o=sq, i=xc: s.activation(o, i, AF.Square), inc=c_ac)
        b.wait("tensor", xsem, xsem.n)
        for hh in range(2):
            b.op("tensor", lambda t, o=PSa[:, hh * 512:hh * 512 + 512],
                 i=xc[:, hh * 512:hh * 512 + 512], stt=(cch == 0), spp=(cch == KC - 1):
                 t.matmul(o, OnesSq[:, :], i, start=stt, stop=spp))
        b.wait("tensor", c_ac, ac0 + cch + 1)
        for hh in range(2):
            b.op("tensor", lambda t, o=PSb[:, hh * 512:hh * 512 + 512],
                 i=sq[:, hh * 512:hh * 512 + 512], stt=(cch == 0), spp=(cch == KC - 1):
                 t.matmul(o, OnesSq[:, :], i, start=stt, stop=spp),
                 inc=(c_pe if hh == 1 else None))
    b.wait("vector", c_pe)
    ln1_fin_ve = emit_ln_finalize(PSa[:, 0:1024], PSb[:, 0:1024])
    ap_ve = {}
    for cch in range(KC):
        par = cch % 4
        xsem = (s_x1a, s_x1b, s_xra, s_xrb)[par]
        xq = ("sync", "scalar", "sync", "scalar")[par]
        xc = (XS[:, 0:1024], XS[:, 1024:2048],
              Xtok[:, 0:1024], Xtok[:, 1024:2048])[par]
        if cch >= 4:
            b.wait(xq, c_ve, ap_ve[cch - 4])
        else:
            b.wait(xq, c_ve, ln1_fin_ve)
        b.dma(xq, xsem, xc, xT_h[cch * 128:(cch + 1) * 128, :])
        b.wait("vector", xsem, xsem.n)
        dc = A[:, cch * T:(cch + 1) * T]
        tmp = SqB[:, (cch % 2) * 1024:(cch % 2) * 1024 + 1024]
        b.op("vector", lambda v, a=tmp, x=xc: v.tensor_tensor(a, x, RstdB[:, :], ALU.mult))
        b.op("vector", lambda v, a=tmp, d=dc: v.tensor_tensor(d, a, MrB[:, :], ALU.add),
             inc=c_ve)
        ap_ve[cch] = c_ve.n
    ln1_ve = c_ve.n

    b.wait("tensor", s_w2l)
    h_ve0 = c_ve.n
    for hc in range(HC):
        ps = PS[2 + (hc % 2)][:, 0:256]
        if hc >= 2:
            b.wait("tensor", c_ve, h_ve0 + (hc - 2) + 1)
        b.op("tensor", lambda t, o=ps, i=W2T[:, hc * 256:hc * 256 + 256]:
             t.matmul(o, H128[:, :], i, start=True, stop=True), inc=c_pe)
        b.wait("vector", c_pe)
        opd("vector", lambda v, o=W2T[:, hc * 256:hc * 256 + 256], i=ps:
            v.tensor_scalar(o, i, 1.0 / math.sqrt(H), None, ALU.mult), inc=c_ve)
    w2_h128_ve = c_ve.n

    b.wait("tensor", c_ve, ln1_ve)
    for cch in range(KC):
        for hh in range(2):
            b.op("tensor", lambda t, o=PSa[0:1, hh * 512:hh * 512 + 512],
                 i=A[:, cch * T + hh * 512:cch * T + hh * 512 + 512],
                 lw=Wib[:, cch:cch + 1], stt=(cch == 0), spp=(cch == KC - 1):
                 t.matmul(o, lw, i, start=stt, stop=spp),
                 inc=(c_pe if (cch == KC - 1 and hh == 1) else None))
    b.wait("vector", c_pe)
    b.op("vector", lambda v: v.tensor_scalar(ImpRow[:, :], PSa[0:1, 0:T], BiT[0:1, 0:1], None, ALU.add), inc=c_ve)
    b.wait("scalar", c_ve)
    b.dma("scalar", s_impst, imp_own_d.ap(), ImpRow[:, :])
    b.wait("gpsimd", s_impst, s_impst.n)
    b.op("gpsimd", lambda g: g.collective_compute(
        "AllGather", ALU.bypass, replica_groups=[list(range(NCORE))],
        ins=[imp_own_d.ap()], outs=[imp_all_d.ap()]), inc=s_cc, k=1)
    b.wait("gpsimd", s_cc, s_cc.n)
    b.dma("gpsimd", s_reg, ImpAll[:, :], imp_all_d.ap().rearrange("(p f) o -> p (f o)", p=128))
    b.op("gpsimd", lambda g: g.iota(PIota[:, :], [[1, 1]], channel_multiplier=1,
                                    allow_small_or_imprecise_dtypes=True), inc=c_gp)

    LO, HI, MID, PC = Bis[:, 0:1], Bis[:, 1:2], Bis[:, 2:3], Bis[:, 3:4]
    b.wait("vector", s_reg)
    b.op("vector", lambda v: v.memset(LO, -64.0))
    opd("vector", lambda v: v.memset(HI, 64.0), inc=c_ve)
    b.wait("tensor", c_ve)
    for it in range(NBIS):
        opd("vector", lambda v: v.tensor_add(MID, LO, HI))
        opd("vector", lambda v: v.tensor_scalar(MID, MID, 0.5, None, ALU.mult))
        opd("vector", lambda v: v.tensor_scalar(Sq2[:, 0:64], ImpAll[:, :], MID[:, 0:1], None, ALU.is_gt))
        opd("vector", lambda v: v.reduce_sum(PC, Sq2[:, 0:64], axis=AX), inc=c_ve)
        b.wait("tensor", c_ve)
        b.op("tensor", lambda t: t.matmul(PSa[:, 0:1], OnesSq[:, :], PC, start=True, stop=True), inc=c_pe)
        b.wait("vector", c_pe)
        opd("vector", lambda v: v.tensor_scalar(BisI[:, 0:1], PSa[:, 0:1], float(RANK_T + 1), None, ALU.is_ge))
        opd("vector", lambda v: v.select(LO, BisI[:, 0:1], MID, LO))
        opd("vector", lambda v: v.tensor_scalar(BisI[:, 0:1], PSa[:, 0:1], float(RANK_T + 1), None, ALU.is_lt))
        opd("vector", lambda v: v.select(HI, BisI[:, 0:1], MID, HI), inc=c_ve)

    opd("vector", lambda v: v.tensor_scalar(Sq2[:, 0:64], ImpAll[:, :], HI[:, 0:1], None, ALU.is_gt))
    opd("vector", lambda v: v.reduce_sum(Bis[:, 4:5], Sq2[:, 0:64], axis=AX), inc=c_ve)
    b.wait("tensor", c_ve)
    b.op("tensor", lambda t: t.matmul(PSa[0:8, 0:1], Rind[:, :], Bis[:, 4:5], start=True, stop=True), inc=c_pe)
    b.wait("vector", c_pe)
    opd("vector", lambda v: v.tensor_scalar(Cnt[:, :], PSa[0:8, 0:1], float(REG), None, ALU.min), inc=c_ve)
    b.op("vector", lambda v: v.memset(Sq2[:, 0:128], 0.0))
    opd("vector", lambda v: v.tensor_copy(Sq2[0:8, 0:1], Cnt[:, :]), inc=c_ve)
    b.wait("tensor", c_ve)
    b.op("tensor", lambda t: t.transpose(PSb[:, 0:128], Sq2[:, 0:128], Ident[:, :]), inc=c_pe)
    b.wait("vector", c_pe)
    opd("vector", lambda v: v.tensor_copy(CntRow[:, :], PSb[0:1, 0:8]), inc=c_ve)
    for r in range(NCORE):
        if r < NCORE - 1:
            b.op("vector", lambda v, r=r: v.tensor_scalar(
                RowB[0:1, r * REG:(r + 1) * REG], Iota384[:, :], CntRow[0:1, r:r + 1], None, ALU.is_lt))
        else:
            opd("vector", lambda v, r=r: v.tensor_scalar(
                RowB[0:1, r * REG:(r + 1) * REG], Iota384[:, :], CntRow[0:1, r:r + 1], None, ALU.is_lt),
                inc=c_ve)
    b.wait("tensor", c_ve)
    vb0 = c_ve.n
    for hh in range(NSLOT // 512):
        ps = PS[hh % 2][:, 0:512]
        if hh >= 2:
            b.wait("tensor", c_ve, vb0 + (hh - 2) + 1)
        b.op("tensor", lambda t, o=ps, i=RowB[0:1, hh * 512:hh * 512 + 512]:
             t.matmul(o, OnesRb[:, :], i, start=True, stop=True), inc=c_pe)
        b.wait("vector", c_pe)
        b.op("vector", lambda v, o=Validb[:, hh * 512:hh * 512 + 512], i=ps: v.tensor_copy(o, i), inc=c_ve)

    def emit_row_to_cols(row_src_off, n_sub, dst_tile, dst_off):
        b.op("vector", lambda v: v.memset(Sq2[:, 0:1024], 0.0))
        opd("vector", lambda v, o=Sq2[0:1, 0:n_sub * 128],
            i=RowB[0:1, row_src_off:row_src_off + n_sub * 128]: v.tensor_copy(o, i), inc=c_ve)
        b.wait("tensor", c_ve)
        t0 = c_ve.n
        for sub in range(n_sub):
            ps = PSc[:, 128:256]
            if sub >= 1:
                b.wait("tensor", c_ve, t0 + sub)
            b.op("tensor", lambda t, o=ps, i=Sq2[:, sub * 128:sub * 128 + 128]:
                 t.transpose(o, i, Ident[:, :]), inc=c_pe)
            b.wait("vector", c_pe)
            opd("vector", lambda v, o=dst_tile[:, dst_off + sub:dst_off + sub + 1], i=ps[:, 0:1]:
                v.tensor_copy(o, i), inc=c_ve)

    for blk in range(3):
        emit_row_to_cols(blk * 1024, 8, ValidT, blk * 8)

    MRow = RowS[0:1, 1024:2048]
    opd("vector", lambda v: v.tensor_scalar(MRow, ImpRow[:, :], HI[0:1, 0:1], None, ALU.is_gt))
    opd("vector", lambda v: v.tensor_copy(SqB[0:1, 0:T], MRow))
    ca, cb = 0, 1024
    for si in range(10):
        d = 1 << si
        b.op("vector", lambda v, d=d, a=ca, o=cb:
             v.tensor_add(SqB[0:1, o + d:o + T], SqB[0:1, a + d:a + T], SqB[0:1, a:a + T - d]))
        opd("vector", lambda v, d=d, a=ca, o=cb:
            v.tensor_copy(SqB[0:1, o:o + d], SqB[0:1, a:a + d]))
        ca, cb = cb, ca
    b.op("vector", lambda v, a=ca: v.tensor_scalar(SqB[0:1, a:a + T], SqB[0:1, a:a + T],
                                                   -1.0 - 1024.0, None, ALU.add))
    b.op("vector", lambda v, a=ca: v.tensor_mul(SqB[0:1, a:a + T], SqB[0:1, a:a + T], MRow))
    b.op("vector", lambda v: v.memset(Sq2[:, 0:1024], 0.0), inc=c_ve)
    opd("vector", lambda v, a=ca: v.tensor_scalar(Sq2[0:1, 0:T], SqB[0:1, a:a + T],
                                                   1024.0, None, ALU.add), inc=c_ve)
    b.wait("tensor", c_ve)
    t0i = c_ve.n
    for sub in range(8):
        ps = PSc[:, 128:256]
        if sub >= 1:
            b.wait("tensor", c_ve, t0i + sub)
        b.op("tensor", lambda t, o=ps, i=Sq2[:, sub * 128:sub * 128 + 128]:
             t.transpose(o, i, Ident[:, :]), inc=c_pe)
        b.wait("vector", c_pe)
        opd("vector", lambda v, o=IdxT[:, sub:sub + 1], i=ps[:, 0:1]: v.tensor_copy(o, i), inc=c_ve)
    b.wait("vector", c_gp)
    opd("vector", lambda v: v.tensor_scalar(PIota[:, :], PIota[:, :], 384.0, None, ALU.add))
    opd("vector", lambda v: v.tensor_scalar(DmI[:, :], IdxT[:, :], 1024.0, None, ALU.is_ge))
    opd("vector", lambda v: v.tensor_scalar(Sq2[:, 8:16], OnesSq[:, 0:8], PIota[:, 0:1], None, ALU.mult))
    opd("vector", lambda v: v.select(IdxT[:, :], DmI[:, :], Sq2[:, 8:16], IdxT[:, :]))
    opd("vector", lambda v: v.tensor_copy(IdxTi[:, :], IdxT[:, :]), inc=c_ve)
    idx_ve = c_ve.n

    b.op("vector", lambda v: v.memset(Xtok[:, 0:D], 0.0), inc=c_ve)
    b.wait("sync", c_ve)
    for z in range(3):
        b.dma("sync", s_zro, gath_d.ap()[z * 128:(z + 1) * 128, :], Xtok[:, 0:D])
    b.wait("gpsimd", s_zro, s_zro.n)
    b.wait("gpsimd", c_ve, idx_ve)
    for tc in range(8):
        par = tc % 2
        xsem = s_xra if par == 0 else s_xrb
        dst = Xtok[:, par * D:(par + 1) * D]
        if tc >= 2:
            b.wait("sync", s_scat, 16 * (tc - 1))
        b.dma("sync", xsem, dst, xrows_h[tc * 128:(tc + 1) * 128, :])
        b.wait("gpsimd", xsem, xsem.n)
        b.dmac("gpsimd", s_scat, lambda g, src=dst, i=tc: g.indirect_dma_start(
            gath_d.ap(), IndirectOffsetOnAxis(ap=IdxTi[:, i:i + 1], axis=0), src, None))
    b.wait("sync", s_scat, s_scat.n)
    b.wait("tensor", c_ve, idx_ve)
    tr3_ve = {}
    tr3_pe = {}
    for r3 in range(3):
        dst = Xtok[:, (r3 % 2) * D:(r3 % 2 + 1) * D]
        if r3 == 2:
            b.wait("sync", c_pe, tr3_pe[0])
        b.dma("sync", s_xra, dst, gath_d.ap()[r3 * 128:(r3 + 1) * 128, :])
        xmark = s_xra.n
        if r3 >= 1:
            b.wait("tensor", c_ve, tr3_ve[r3 - 1])
        b.wait("tensor", s_xra, xmark)
        for kcc in range(16):
            ps = PS[2 + kcc // 8][:, (kcc % 8) * 128:(kcc % 8) * 128 + 128]
            b.op("tensor", lambda t, o=ps, i=dst[:, kcc * 128:kcc * 128 + 128]:
                 t.transpose(o, i, Ident[:, :]), inc=(c_pe if kcc == 15 else None))
        tr3_pe[r3] = c_pe.n
        b.wait("vector", c_pe)
        for kcc in range(16):
            ps = PS[2 + kcc // 8][:, (kcc % 8) * 128:(kcc % 8) * 128 + 128]
            b.op("vector", lambda v, o=XgT[:, kcc * REG + r3 * 128:kcc * REG + r3 * 128 + 128],
                 i=ps: v.tensor_copy(o, i), inc=(c_ve if kcc == 15 else None))
        tr3_ve[r3] = c_ve.n
    b.wait("tensor", c_ve, tr3_ve[2])
    b.wait("scalar", c_ve, tr3_ve[2])
    g_ac0 = c_ac.n
    gstat_pe = {}
    for kcc in range(16):
        sq = SqB[:, (kcc % 4) * 512:(kcc % 4) * 512 + REG]
        if kcc >= 4:
            b.wait("scalar", c_pe, gstat_pe[kcc - 4])
        b.op("scalar", lambda s, o=sq, i=XgT[:, kcc * REG:(kcc + 1) * REG]:
             s.activation(o, i, AF.Square), inc=c_ac)
        b.op("tensor", lambda t, o=PSa[:, 0:REG], i=XgT[:, kcc * REG:(kcc + 1) * REG],
             stt=(kcc == 0), spp=(kcc == 15):
             t.matmul(o, OnesSq[:, :], i, start=stt, stop=spp))
        b.wait("tensor", c_ac, g_ac0 + kcc + 1)
        b.op("tensor", lambda t, o=PSb[:, 0:REG], i=sq, stt=(kcc == 0), spp=(kcc == 15):
             t.matmul(o, OnesSq[:, :], i, start=stt, stop=spp), inc=c_pe)
        gstat_pe[kcc] = c_pe.n
    b.wait("vector", c_pe)
    R3, M3, S3 = RstdB[:, 0:REG], MrB[:, 0:REG], Sq2[:, 0:REG]
    b.op("vector", lambda v: v.tensor_scalar(S3, PSa[:, 0:REG], 1.0 / D, None, ALU.mult))
    b.op("vector", lambda v: v.tensor_scalar(R3, PSb[:, 0:REG], 1.0 / D, None, ALU.mult))
    b.op("vector", lambda v: v.tensor_mul(M3, S3, S3))
    b.op("vector", lambda v: v.tensor_sub(R3, R3, M3), inc=c_ve)
    b.wait("scalar", c_ve)
    b.op("scalar", lambda s: s.activation(R3, R3, AF.Ln, bias=EpsCol[:, :], scale=1.0))
    b.op("scalar", lambda s: s.activation(R3, R3, AF.Exp, scale=-0.5), inc=c_ac)
    b.wait("vector", c_ac)
    b.op("vector", lambda v: v.scalar_tensor_tensor(M3, S3, -1.0, R3, ALU.mult, ALU.mult),
         inc=c_ve)
    for kcc in range(16):
        sq = SqB[:, (kcc % 4) * 512:(kcc % 4) * 512 + REG]
        b.op("vector", lambda v, o=sq, i=XgT[:, kcc * REG:(kcc + 1) * REG]:
             v.tensor_tensor(o, i, R3, ALU.mult))
        b.op("vector", lambda v, o=LnAll[:, kcc * REG:(kcc + 1) * REG], i=sq:
             v.tensor_tensor(o, i, M3, ALU.add),
             inc=(c_ve if kcc == 15 else None))
    b.wait("scalar", c_ve)
    b.dma("scalar", s_lngst,
          bass.AP(lng_own_d, 0, [[16 * REG, 128], [REG, 16], [1, REG]]),
          LnAll[:, :])
    b.wait("gpsimd", s_lngst, s_lngst.n)
    b.op("gpsimd", lambda g: g.collective_compute(
        "AllGather", ALU.bypass, replica_groups=[list(range(NCORE))],
        ins=[lng_own_d.ap()], outs=[lng_all_d.ap()]), inc=s_cc, k=1)
    cc_lng = s_cc.n

    b.wait("tensor", c_ve, idx_ve)
    q_consumed_pe = {}
    for oc in range(16):
        par = oc % 2
        buf = WsA[:, par * 2048:(par + 1) * 2048]
        sem = s_wa if par == 0 else s_wb
        if oc >= 2:
            b.wait("gpsimd", c_pe, q_consumed_pe[oc - 2])
        b.dma("gpsimd", sem, buf, wq_h[oc])
        b.wait("tensor", sem, sem.n)
        pst = PS[par]
        if oc >= 2:
            b.wait("tensor", c_ve, q_evac_ac[oc - 2])
        for kcc in range(16):
            for hh in range(2):
                b.op("tensor", lambda t, o=pst[:, hh * 512:hh * 512 + 512],
                     lw=buf[:, kcc * 128:kcc * 128 + 128],
                     i=A[:, kcc * T + hh * 512:kcc * T + hh * 512 + 512],
                     stt=(kcc == 0), spp=(kcc == 15):
                     t.matmul(o, lw, i, start=stt, stop=spp),
                     inc=(c_pe if (kcc == 15 and hh == 1) else None))
        q_consumed_pe[oc] = c_pe.n
        b.wait("vector", c_pe)
        qs = s_qsa if par == 0 else s_qsb
        if oc >= 2:
            b.wait("vector", qs, qs.n)
        b.op("vector", lambda v, o=QSt[:, par * T:(par + 1) * T], i=pst[:, 0:1024], bi=BiasQ[:, oc:oc + 1]:
             v.tensor_scalar(o, i, bi, QSCALE, ALU.add, ALU.mult), inc=c_ve)
        if oc == 0:
            q_evac_ac = {}
        q_evac_ac[oc] = c_ve.n
        b.wait("scalar", c_ve)
        b.dma("scalar", qs, q_d.ap()[oc * 128:(oc + 1) * 128, :], QSt[:, par * T:(par + 1) * T])

    b.wait("tensor", s_cst)
    b.op("tensor", lambda t: t.matmul(PSc[:, 256:512], OnesR[:, :], BiasVr[:, :], start=True, stop=True), inc=c_pe)
    b.wait("vector", c_pe)
    b.op("vector", lambda v: v.tensor_copy(BvB[:, :], PSc[:, 256:512]), inc=c_ve)

    gath_scope.close()
    kvb_scope = ExitStack()
    LnS = kvb_scope.enter_context(nc.sbuf_tensor("LnS", [128, 2 * 16 * REG], BF16))
    KOut = kvb_scope.enter_context(nc.sbuf_tensor("KOut", [128, 2 * NSLOT], BF16))
    VOut = kvb_scope.enter_context(nc.sbuf_tensor("VOut", [128, SC * 256], BF16))
    WkB = kvb_scope.enter_context(nc.sbuf_tensor("WkB", [128, 16 * 256], BF16))
    WvB = kvb_scope.enter_context(nc.sbuf_tensor("WvB", [128, 16 * 256], BF16))
    b.dma("gpsimd", s_wc, WkB[:, :], wkbh_h)
    b.dma("gpsimd", s_wc, WvB[:, :], wvbh_h)
    b.wait("gpsimd", s_cc, cc_lng)
    kv_pe = {}
    kv_ve = {}
    for r in range(NCORE):
        parl = r % 2
        lns = LnS[:, parl * 16 * REG:(parl + 1) * 16 * REG]
        if r >= 2:
            b.wait("gpsimd", c_pe, kv_pe[r - 2])
        b.dma("gpsimd", s_lng, lns,
              lng_all_d.ap()[r * 2048:(r + 1) * 2048, :].rearrange("(p kc) s -> p (kc s)", p=128))
        b.wait("tensor", s_lng, s_lng.n)
        if r == 0:
            b.wait("tensor", c_ve)
            b.wait("tensor", s_wc, s_wc.n)
        else:
            b.wait("tensor", c_ve, kv_ve[r - 1])
        for sub in range(2):
            ps = PS[sub][:, 0:REG]
            for kcc in range(16):
                b.op("tensor", lambda t, o=ps, lw=WkB[:, kcc * 256 + sub * 128:kcc * 256 + sub * 128 + 128],
                     i=lns[:, kcc * REG:(kcc + 1) * REG], stt=(kcc == 0), spp=(kcc == 15):
                     t.matmul(o, lw, i, start=stt, stop=spp),
                     inc=(c_pe if kcc == 15 else None))
            b.wait("vector", c_pe)
            b.op("vector", lambda v, o=KOut[:, sub * NSLOT + r * REG:sub * NSLOT + (r + 1) * REG],
                 i=ps, bi=BiasKbh[:, sub:sub + 1], vb=Validb[:, r * REG:(r + 1) * REG]:
                 v.scalar_tensor_tensor(o, i, bi, vb, ALU.add, ALU.mult), inc=c_ve)
        for scl in range(3):
            ps = PS[2 + scl % 2][:, 0:256]
            sgl = r * 3 + scl
            if scl >= 2:
                b.wait("tensor", c_ve, v_ve0 + (scl - 2) + 1)
            for kcc in range(16):
                b.op("tensor", lambda t, o=ps, lw=lns[:, kcc * REG + scl * 128:kcc * REG + scl * 128 + 128],
                     i=WvB[:, kcc * 256:(kcc + 1) * 256], stt=(kcc == 0), spp=(kcc == 15):
                     t.matmul(o, lw, i, start=stt, stop=spp),
                     inc=(c_pe if kcc == 15 else None))
            b.wait("vector", c_pe)
            if scl == 0:
                v_ve0 = c_ve.n
            b.op("vector", lambda v, i=ps: v.tensor_add(Sq2[:, 0:256], i, BvB[:, :]))
            b.op("vector", lambda v, o=VOut[:, sgl * 256:(sgl + 1) * 256], vt=ValidT[:, sgl:sgl + 1]:
                 v.tensor_scalar(o, Sq2[:, 0:256], vt, None, ALU.mult), inc=c_ve)
        kv_pe[r] = c_pe.n
        kv_ve[r] = c_ve.n
    b.wait("scalar", c_ve)
    for sub, (kv_own, kv_all) in enumerate([(kv_own_e_d, kv_all_e_d), (kv_own_o_d, kv_all_o_d)]):
        b.dma("scalar", s_kvst,
              bass.AP(kv_own, 0, [[NSLOT, 128], [1, NSLOT]]),
              KOut[:, sub * NSLOT:(sub + 1) * NSLOT])
        vsrc = bass.AP(VOut, sub * 128, [[SC * 256, 128], [256, SC], [1, 128]])
        b.dma("scalar", s_kvst,
              bass.AP(kv_own, 384 * 1024, [[SC * 128, 128], [128, SC], [1, 128]]),
              vsrc)
        b.wait("gpsimd", s_kvst, s_kvst.n)
        b.op("gpsimd", lambda g, i=kv_own, o=kv_all: g.collective_compute(
            "AllGather", ALU.bypass, replica_groups=[list(range(NCORE))],
            ins=[i.ap()], outs=[o.ap()]), inc=s_cc, k=1)
        if sub == 0:
            cc_kv_e = s_cc.n
    cc_kv_o = s_cc.n
    kvb_scope.close()
    ph2_scope.close()

    w2p_scope = ExitStack()
    W2P = w2p_scope.enter_context(nc.sbuf_tensor("W2P", [128, HC * 256], BF16))
    b.wait("vector", c_ve, w2_h128_ve)
    CW = 256
    cur, oth = W2T, W2P
    for si in range(6):
        half = 1 << si
        nblk = HC // (2 * half)
        apl = [[HC * CW, 128], [2 * half * CW, nblk], [1, half * CW]]
        ev_s = bass.AP(cur, 0, apl)
        od_s = bass.AP(cur, half * CW, apl)
        ev_d = bass.AP(oth, 0, apl)
        od_d = bass.AP(oth, half * CW, apl)
        b.op("vector", lambda v, a=ev_d, x=ev_s, y=od_s: v.tensor_add(a, x, y))
        b.op("vector", lambda v, a=od_d, x=ev_s, y=od_s: v.tensor_sub(a, x, y),
             inc=(c_ve if si == 5 else None))
        cur, oth = oth, cur
    w2fin = cur
    b.wait("scalar", c_ve)
    b.dma("scalar", s_w2st, w2t_own_d.ap().rearrange("(p hc) d -> p (hc d)", p=128), w2fin[:, :])
    b.wait("gpsimd", s_w2st, s_w2st.n)
    b.op("gpsimd", lambda g: g.collective_compute(
        "AllGather", ALU.bypass, replica_groups=[list(range(NCORE))],
        ins=[w2t_own_d.ap()], outs=[w2t_all_d.ap()]), inc=s_cc, k=1)
    cc_w2 = s_cc.n
    w2p_scope.close()
    w2t_scope.close()

    att_scope = ExitStack()
    KH = att_scope.enter_context(nc.sbuf_tensor("KH", [128, 2 * NSLOT], BF16))
    VH = att_scope.enter_context(nc.sbuf_tensor("VH", [128, 2 * SC * 128], BF16))
    QH = att_scope.enter_context(nc.sbuf_tensor("QH", [128, 2 * T], BF16))
    NRING = 8
    LAG = 5
    PR = att_scope.enter_context(nc.sbuf_tensor("PR", [128, NRING * T], BF16))
    b.wait("scalar", s_qsa, s_qsa.n)
    b.wait("scalar", s_qsb, s_qsb.n)
    b.wait("tensor", c_ve)
    NJ = NH * SC
    ORD = list(range(0, NH, 2)) + list(range(1, NH, 2))
    sc_pe = {}
    av_pe = {}
    den_pe = {}
    ex_ac = {}
    epi_ve = {}
    epi1_ve = {}
    ksem_at = {}
    vsem_at = {}

    def att_load(pos):
        h = ORD[pos]
        par = pos % 2
        hb = h // 2
        kv_all = kv_all_e_d if h % 2 == 0 else kv_all_o_d
        cc_need = cc_kv_e if h % 2 == 0 else cc_kv_o
        ksem = s_kva if par == 0 else s_kvb
        vsem = s_va if par == 0 else s_vb
        b.wait("sync", s_cc, cc_need)
        b.wait("gpsimd", s_cc, cc_need)
        b.wait("scalar", s_cc, cc_need)
        if pos >= 2:
            prev = av_pe[(pos - 2) * SC + SC - 1]
            b.wait("sync", c_pe, prev)
            b.wait("gpsimd", c_pe, prev)
            b.wait("scalar", c_pe, prev)
        b.dma("sync", ksem, KH[:, par * NSLOT:(par + 1) * NSLOT],
              bass.AP(kv_all, hb * 768 * 1024, [[NSLOT, 128], [1, NSLOT]]))
        b.dma("gpsimd", vsem, VH[:, par * SC * 128:(par + 1) * SC * 128],
              bass.AP(kv_all, hb * 768 * 1024 + 384 * 1024,
                      [[SC * 128, 128], [128, SC], [1, 128]]))
        b.dma("scalar", ksem, QH[:, par * T:(par + 1) * T], q_d.ap()[h * 128:(h + 1) * 128, :])
        ksem_at[pos] = (ksem, ksem.n)
        vsem_at[pos] = (vsem, vsem.n)

    def att_scores(j):
        pos, kcc = divmod(j, SC)
        par = pos % 2
        if kcc == 0:
            ksem, kn = ksem_at[pos]
            b.wait("tensor", ksem, kn)
        pst = PS[j % 2]
        if j >= 2:
            b.wait("tensor", c_ac, ex_ac[j - 2])
        for hh in range(2):
            b.op("tensor", lambda t, o=pst[:, hh * 512:hh * 512 + 512],
                 lw=KH[:, par * NSLOT + kcc * 128:par * NSLOT + kcc * 128 + 128],
                 i=QH[:, par * T + hh * 512:par * T + hh * 512 + 512]:
                 t.matmul(o, lw, i, start=True, stop=True),
                 inc=(c_pe if hh == 1 else None))
        sc_pe[j] = c_pe.n

    def att_exp(j):
        ring = j % NRING
        pst = PS[j % 2]
        b.wait("scalar", c_pe, sc_pe[j])
        if j >= NRING:
            b.wait("scalar", c_pe, av_pe[j - NRING])
        b.op("scalar", lambda s, o=PR[:, ring * T:(ring + 1) * T], i=pst[:, 0:1024]:
             s.activation(o, i, AF.Exp, bias=NegC[:, :], scale=1.0), inc=c_ac)
        ex_ac[j] = c_ac.n

    def att_denav(j):
        pos, kcc = divmod(j, SC)
        par = pos % 2
        ring = j % NRING
        b.wait("tensor", c_ac, ex_ac[j])
        if kcc == 0:
            vsem, vn = vsem_at[pos]
            b.wait("tensor", vsem, vn)
        if kcc == 0 and pos >= 1:
            b.wait("tensor", c_ve, epi1_ve[pos - 1])
        for hh in range(2):
            b.op("tensor", lambda t, o=PSc[:, hh * 512:hh * 512 + 512],
                 i=PR[:, ring * T + hh * 512:ring * T + hh * 512 + 512],
                 stt=(kcc == 0), spp=(kcc == SC - 1):
                 t.matmul(o, OnesSqB[:, :], i, start=stt, stop=spp),
                 inc=(c_pe if (kcc == SC - 1 and hh == 1) else None))
        if kcc == SC - 1:
            den_pe[pos] = c_pe.n
        if kcc == 0 and pos >= 1:
            b.wait("tensor", c_ve, epi_ve[pos - 1])
        for hh in range(2):
            b.op("tensor", lambda t, o=PSd[:, hh * 512:hh * 512 + 512],
                 lw=VH[:, par * SC * 128 + kcc * 128:par * SC * 128 + kcc * 128 + 128],
                 i=PR[:, ring * T + hh * 512:ring * T + hh * 512 + 512],
                 stt=(kcc == 0), spp=(kcc == SC - 1):
                 t.matmul(o, lw, i, start=stt, stop=spp),
                 inc=(c_pe if hh == 1 else None))
        av_pe[j] = c_pe.n

    def att_epilogue(pos):
        h = ORD[pos]
        b.wait("vector", c_pe, den_pe[pos])
        b.op("vector", lambda v: v.tensor_scalar(Sq2[:, 0:1024], PSc[:, 0:1024], PADC, None, ALU.add),
             inc=c_ve)
        epi1_ve[pos] = c_ve.n
        b.op("vector", lambda v: v.reciprocal(SqB[:, 0:1024], Sq2[:, 0:1024]))
        b.wait("vector", c_pe, av_pe[pos * SC + SC - 1])
        b.op("vector", lambda v, o=A[:, h * T:(h + 1) * T]:
             v.tensor_tensor(o, PSd[:, 0:1024], SqB[:, 0:1024], ALU.mult), inc=c_ve)
        epi_ve[pos] = c_ve.n

    att_load(0)
    att_load(1)
    for j in range(NJ + LAG):
        if j < NJ:
            pos, kcc = divmod(j, SC)
            if kcc == LAG and pos >= 1 and pos + 1 < NH:
                att_load(pos + 1)
            att_scores(j)
            att_exp(j)
        jj = j - LAG
        if jj >= 0:
            att_denav(jj)
            if jj % SC == SC - 1:
                att_epilogue(jj // SC)
    att_scope.close()

    XC = XS
    wo_pe = {}
    wo_ve = {}
    q_sq_pe = {}
    x1_written = {}
    ln2_ac0 = None
    for oc in range(16):
        par = oc % 2
        buf = WsA[:, par * 2048:(par + 1) * 2048]
        wsem = s_wa if par == 0 else s_wb
        xsem = s_x1a if par == 0 else s_x1b
        sps = s_spa if par == 0 else s_spb
        if oc >= 2:
            b.wait("gpsimd", c_pe, wo_pe[oc - 2])
            b.wait("sync", c_pe, q_sq_pe[oc - 2])
            b.wait("sync", sps, sps.n)
        b.dma("gpsimd", wsem, buf, wo_h[oc])
        b.dma("sync", xsem, XC[:, par * T:(par + 1) * T], xT_h[oc * 128:(oc + 1) * 128, :])
        b.wait("tensor", wsem, wsem.n)
        if oc == 0:
            b.wait("tensor", c_ve, epi_ve[NH - 1])
        pst = PS[par]
        for kcc in range(16):
            for hh in range(2):
                b.op("tensor", lambda t, o=pst[:, hh * 512:hh * 512 + 512],
                     lw=buf[:, kcc * 128:kcc * 128 + 128],
                     i=A[:, kcc * T + hh * 512:kcc * T + hh * 512 + 512],
                     stt=(kcc == 0), spp=(kcc == 15):
                     t.matmul(o, lw, i, start=stt, stop=spp),
                     inc=(c_pe if (kcc == 15 and hh == 1) else None))
        wo_pe[oc] = c_pe.n
        b.wait("vector", c_pe)
        b.wait("vector", xsem, xsem.n)
        xcv = XC[:, par * T:(par + 1) * T]
        b.op("vector", lambda v, o=xcv, i=pst[:, 0:1024], bi=BiasO[:, oc:oc + 1]:
             v.scalar_tensor_tensor(o, i, bi, o, ALU.add, ALU.add), inc=c_ve)
        wo_ve[oc] = c_ve.n
        b.wait("scalar", c_ve, wo_ve[oc])
        b.dma("scalar", sps, x1_d.ap()[oc * 128:(oc + 1) * 128, :], xcv)
        b.op("scalar", lambda s, o=Sq2[:, 0:1024], i=xcv: s.activation(o, i, AF.Square), inc=c_ac)
        if ln2_ac0 is None:
            ln2_ac0 = c_ac.n - 1
        b.wait("tensor", c_ve, wo_ve[oc])
        for hh in range(2):
            b.op("tensor", lambda t, o=PSc[:, hh * 512:hh * 512 + 512],
                 i=xcv[:, hh * 512:hh * 512 + 512], stt=(oc == 0), spp=(oc == 15):
                 t.matmul(o, OnesSq[:, :], i, start=stt, stop=spp))
        b.wait("tensor", c_ac, ln2_ac0 + oc + 1)
        for hh in range(2):
            b.op("tensor", lambda t, o=PSd[:, hh * 512:hh * 512 + 512],
                 i=Sq2[:, hh * 512:hh * 512 + 512], stt=(oc == 0), spp=(oc == 15):
                 t.matmul(o, OnesSq[:, :], i, start=stt, stop=spp),
                 inc=c_pe)
        q_sq_pe[oc] = c_pe.n
    b.wait("vector", c_pe)
    ln2_fin_ve = emit_ln_finalize(PSc[:, 0:1024], PSd[:, 0:1024])
    XL = XS
    ln2_ve = {}
    for q in ("sync", "scalar"):
        b.wait(q, c_pe)
        b.wait(q, s_spa)
        b.wait(q, s_spb)
    for cch in range(KC):
        par = cch % 2
        xsem = s_x1a if par == 0 else s_x1b
        xq = "sync" if par == 0 else "scalar"
        if cch >= 2:
            b.wait(xq, c_ve, ln2_ve[cch - 2])
        b.dma(xq, xsem, XL[:, par * T:(par + 1) * T], x1_d.ap()[cch * 128:(cch + 1) * 128, :])
        b.wait("vector", xsem, xsem.n)
        xc = XL[:, par * T:(par + 1) * T]
        tmp = SqB[:, par * 1024:par * 1024 + 1024]
        b.op("vector", lambda v, a=tmp, x=xc: v.tensor_tensor(a, x, RstdB[:, :], ALU.mult))
        b.op("vector", lambda v, a=tmp, d=A[:, cch * T:(cch + 1) * T]:
             v.tensor_tensor(d, a, MrB[:, :], ALU.add), inc=c_ve)
        ln2_ve[cch] = c_ve.n
    ln2_done_ve = c_ve.n

    mid_scope.close()
    ffn_scope = ExitStack()
    Hid = ffn_scope.enter_context(nc.sbuf_tensor("Hid", [128, HC * T], BF16, side="right"))
    hid_ve = {}
    ffn_pe = {}
    silu_ac = {}
    b.wait("tensor", c_ve, ln2_done_ve)
    for hc in range(HC):
        par = hc % 2
        b1 = WsA[:, par * 2048:(par + 1) * 2048]
        b3 = WsB[:, par * 2048:(par + 1) * 2048]
        s1 = s_wa if par == 0 else s_wb
        s3 = s_wc if par == 0 else s_wd
        if hc >= 2:
            b.wait("gpsimd", c_pe, ffn_pe[hc - 2])
        b.dma("gpsimd", s1, b1, w1_h[hc])
        b.dma("gpsimd", s3, b3, w3_h[hc])
        b.wait("tensor", s1, s1.n)
        b.wait("tensor", s3, s3.n)
        psg = PS[2 * par]
        psl = PS[2 * par + 1]
        if hc >= 2:
            b.wait("tensor", c_ve, hid_ve[hc - 2])
            b.wait("tensor", c_ac, silu_ac[hc - 2])
        for kcc in range(16):
            for hh in range(2):
                b.op("tensor", lambda t, o=psg[:, hh * 512:hh * 512 + 512],
                     lw=b1[:, kcc * 128:kcc * 128 + 128],
                     i=A[:, kcc * T + hh * 512:kcc * T + hh * 512 + 512],
                     stt=(kcc == 0), spp=(kcc == 15):
                     t.matmul(o, lw, i, start=stt, stop=spp),
                     inc=(c_pe if (kcc == 15 and hh == 1) else None))
        gate_pe = c_pe.n
        for kcc in range(16):
            for hh in range(2):
                b.op("tensor", lambda t, o=psl[:, hh * 512:hh * 512 + 512],
                     lw=b3[:, kcc * 128:kcc * 128 + 128],
                     i=A[:, kcc * T + hh * 512:kcc * T + hh * 512 + 512],
                     stt=(kcc == 0), spp=(kcc == 15):
                     t.matmul(o, lw, i, start=stt, stop=spp),
                     inc=(c_pe if (kcc == 15 and hh == 1) else None))
        ffn_pe[hc] = c_pe.n
        b.wait("scalar", c_pe, gate_pe)
        b.op("scalar", lambda s, o=QSt[:, par * T:(par + 1) * T], i=psg[:, 0:1024]:
             s.activation(o, i, AF.Silu), inc=c_ac)
        silu_ac[hc] = c_ac.n
        b.wait("vector", c_ac, silu_ac[hc])
        b.wait("vector", c_pe, ffn_pe[hc])
        b.op("vector", lambda v, o=Hid[:, hc * T:(hc + 1) * T], g=QSt[:, par * T:(par + 1) * T],
             l=psl[:, 0:1024]: v.tensor_tensor(o, g, l, ALU.mult), inc=c_ve)
        hid_ve[hc] = c_ve.n

    ws_scope.close()
    w2s_scope = ExitStack()
    W2S = w2s_scope.enter_context(nc.sbuf_tensor("W2S", [128, 2 * 8 * 256], BF16))
    X1R = XS
    YSt = w2s_scope.enter_context(nc.sbuf_tensor("YSt", [128, 1024], F32))
    b.wait("sync", s_cc, cc_w2)
    b.wait("sync", s_spa)
    b.wait("sync", s_spb)
    b.wait("sync", c_ve, ln2_done_ve)
    ld_i = 0
    w2ld_pe = {}
    ev_ve = {}
    pair_free_ve = {}
    ytr_ve = {}
    ytr_i = 0
    for r in range(NCORE):
        pair = (PS[0], PS[1])
        w2src = w2t_all_d.ap()[r * 128 * HC:(r + 1) * 128 * HC, :].rearrange(
            "(p hc) d -> p (hc d)", p=128)
        for oc2 in range(2):
            xsem = s_x1a if oc2 == 0 else s_x1b
            if r >= 1:
                b.wait("scalar", c_ve, ev_ve[(r - 1, oc2)])
            b.dma("scalar", xsem, X1R[:, oc2 * 1024:(oc2 + 1) * 1024],
                  x1_d.ap()[(r * 2 + oc2) * 128:(r * 2 + oc2 + 1) * 128, :])
        for lg in range(8):
            parw = ld_i % 2
            wsem = s_w2sA if parw == 0 else s_w2sB
            wq_ = "sync" if parw == 0 else "gpsimd"
            if ld_i >= 2:
                b.wait(wq_, c_pe, w2ld_pe[ld_i - 2])
            elif ld_i == 0:
                b.wait("sync", c_pe, ffn_pe[HC - 1])
                b.wait("gpsimd", c_pe, ffn_pe[HC - 1])
            b.dma(wq_, wsem, W2S[:, parw * 2048:(parw + 1) * 2048],
                  w2src[:, lg * 2048:(lg + 1) * 2048])
            b.wait("tensor", wsem, wsem.n)
            if r == 0 and lg == 0:
                b.wait("tensor", c_ve, hid_ve[HC - 1])
                b.wait("tensor", c_ac, silu_ac[HC - 1])
            if lg == 0 and r >= 1:
                b.wait("tensor", c_ve, ev_ve[(r - 1, 1)])
            for hcl in range(8):
                hcg = lg * 8 + hcl
                for oc2 in range(2):
                    for th in range(2):
                        b.op("tensor", lambda t, o=pair[oc2][:, th * 512:(th + 1) * 512],
                             lw=W2S[:, parw * 2048 + hcl * 256 + oc2 * 128:
                                    parw * 2048 + hcl * 256 + oc2 * 128 + 128],
                             i=Hid[:, hcg * T + th * 512:hcg * T + th * 512 + 512],
                             stt=(hcg == 0), spp=(hcg == HC - 1):
                             t.matmul(o, lw, i, start=stt, stop=spp),
                             inc=(c_pe if (hcl == 7 and oc2 == 1 and th == 1) else None))
            w2ld_pe[ld_i] = c_pe.n
            ld_i += 1
        for oc2 in range(2):
            d2c = r * 2 + oc2
            xsem = s_x1a if oc2 == 0 else s_x1b
            ysem = s_sta
            b.wait("vector", c_pe, w2ld_pe[ld_i - 1])
            b.wait("vector", xsem, xsem.n)
            yf = SqB[:, oc2 * 1024:(oc2 + 1) * 1024]
            b.op("vector", lambda v, o=yf, i=pair[oc2][:, 0:1024],
                 x=X1R[:, oc2 * 1024:(oc2 + 1) * 1024]:
                 v.tensor_tensor(o, i, x, ALU.add), inc=c_ve)
            ev_ve[(r, oc2)] = c_ve.n
            b.wait("tensor", c_ve, ev_ve[(r, oc2)])
            if r * 2 + oc2 >= 1:
                b.wait("vector", ysem, ysem.n)
                b.wait("tensor", c_ve, ytr_ve[ytr_i - 1])
            for j in range(8):
                pstr = PS[2 + j // 4][:, (j % 4) * 128:(j % 4) * 128 + 128]
                b.op("tensor", lambda t, o=pstr, i=yf[:, j * 128:j * 128 + 128]:
                     t.transpose(o, i, Ident[:, :]), inc=(c_pe if j == 7 else None))
            b.wait("vector", c_pe)
            for j in range(8):
                pstr = PS[2 + j // 4][:, (j % 4) * 128:(j % 4) * 128 + 128]
                b.op("vector", lambda v, o=YSt[:, j * 128:(j + 1) * 128],
                     i=pstr: v.tensor_copy(o, i), inc=(c_ve if j == 7 else None))
            ytr_ve[ytr_i] = c_ve.n
            ytr_i += 1
            b.wait("scalar", c_ve, ytr_ve[ytr_i - 1])
            for j in range(8):
                b.dma("scalar", ysem,
                      y_h[j * 128:(j + 1) * 128, d2c * 128:(d2c + 1) * 128],
                      YSt[:, j * 128:(j + 1) * 128])
        pair_free_ve[r] = ytr_ve[ytr_i - 1]
    w2s_scope.close()
    ffn_scope.close()
    b.wait("scalar", s_sta, s_sta.n)
    b.wait("scalar", s_stb, s_stb.n)

    if verbose:
        print("sbuf bytes remaining:", nc.sbuf_bytes_remaining)
        print("instr counts:", {e: len(b.q[e]) for e in ENGS})

    with nc.Block() as block:
        b.replay(block)
    st.close()
    return nc


def _hadamard(n):
    h = np.array([[1.0]], dtype=np.float64)
    while h.shape[0] < n:
        h = np.block([[h, h], [h, -h]])
    return h


def _prep_inputs(inputs):
    f32 = np.float32
    x = np.asarray(inputs["L0_attn_x"], f32).reshape(S, D)
    Wq = np.asarray(inputs["L0_attn_Wq"], f32)
    bq = np.asarray(inputs["L0_attn_bq"], f32)
    Wk = np.asarray(inputs["L0_attn_Wk"], f32)
    bk = np.asarray(inputs["L0_attn_bk"], f32)
    Wv = np.asarray(inputs["L0_attn_Wv"], f32)
    bv = np.asarray(inputs["L0_attn_bv"], f32)
    Wo = np.asarray(inputs["L0_attn_Wo"], f32)
    bo = np.asarray(inputs["L0_attn_bo"], f32)
    Wi = np.asarray(inputs["L0_attn_Wi"], f32)
    bi = np.asarray(inputs["L0_attn_bi"], f32)
    W1 = np.asarray(inputs["L0_ffn_W1"], f32)
    W2 = np.asarray(inputs["L0_ffn_W2"], f32)
    W3 = np.asarray(inputs["L0_ffn_W3"], f32)

    def tile_oc(W):
        return np.ascontiguousarray(
            W.reshape(16, 128, 16, 128).transpose(2, 1, 0, 3).reshape(16, 128, 16 * 128))

    def tile_hc(Wh):
        return np.ascontiguousarray(
            Wh.reshape(64, 128, 16, 128).transpose(0, 3, 2, 1).reshape(64, 128, 16 * 128))

    wq_t = tile_oc(Wq)
    wo_t = tile_oc(Wo)
    w1_t = tile_hc(W1)
    w3_t = tile_hc(W3)
    bq_t = np.ascontiguousarray(bq.reshape(16, 128).T)
    bo_t = np.ascontiguousarray(bo.reshape(16, 128).T)
    bk_t = np.ascontiguousarray(bk.reshape(16, 128).T)
    wi_t = np.ascontiguousarray(Wi.reshape(16, 128).T)
    h128 = _hadamard(128).astype(ml_dtypes.bfloat16)
    ident = np.eye(128, dtype=f32)
    ones_c = np.ones((128, 1), f32)
    ones_cb = np.ones((128, 1), ml_dtypes.bfloat16)
    ones_r = np.ones((1, 128), f32)
    ones_rb = np.ones((1, 128), ml_dtypes.bfloat16)
    ones_sq = np.ones((128, 128), f32)
    ones_sqb = np.ones((128, 128), ml_dtypes.bfloat16)
    rind = np.zeros((128, 8), f32)
    for p in range(128):
        rind[p, p // 16] = 1.0
    iota384 = np.arange(REG, dtype=f32).reshape(1, REG)

    xT = np.ascontiguousarray(x.T)
    maps = []
    for c in range(NCORE):
        sub0 = 2 * c
        m = dict(
            xT=np.ascontiguousarray(xT[:, c * T:(c + 1) * T]),
            xrows=np.ascontiguousarray(x[c * T:(c + 1) * T, :]),
            wq_t=wq_t, wo_t=wo_t, w1_t=w1_t, w3_t=w3_t,
            wk_bh=np.ascontiguousarray(
                Wk[:, sub0 * 128:(sub0 + 2) * 128].reshape(16, 128, 256)
                .transpose(1, 0, 2).reshape(128, 16 * 256)),
            wv_bh=np.ascontiguousarray(
                Wv[:, sub0 * 128:(sub0 + 2) * 128].reshape(16, 128, 256)
                .transpose(1, 0, 2).reshape(128, 16 * 256)),
            w2_slt=np.ascontiguousarray(
                W2[c * 256:(c + 1) * 256, :].reshape(256, 64, 128)
                .transpose(2, 1, 0).reshape(128, 64 * 256)),
            wi_t=wi_t, bq_t=bq_t, bo_t=bo_t,
            bk_bh=np.ascontiguousarray(bk_t[:, sub0:sub0 + 2]),
            bv_row=np.ascontiguousarray(bv[sub0 * 128:(sub0 + 2) * 128].reshape(1, 256)),
            bi=np.asarray(bi, f32).reshape(1, 1),
            ones_c=ones_c, ones_cb=ones_cb, ones_r=ones_r, ones_rb=ones_rb, ones_sq=ones_sq,
            ones_sqb=ones_sqb,
            ident=ident, h128=h128, rind=rind, iota384=iota384,
        )
        maps.append(m)
    return maps


_NC_CACHE = {}


def kernel(**inputs):
    maps = _prep_inputs(inputs)
    if "nc" not in _NC_CACHE:
        _NC_CACHE["nc"] = build()
    nc = _NC_CACHE["nc"]
    res = run_bass_kernel_spmd(nc, maps, core_ids=list(range(NCORE)))
    y = np.concatenate([np.asarray(res.results[c]["y"]) for c in range(NCORE)], axis=0)
    return y.reshape(1, S, D)


if __name__ == "__main__":
    nc = build(verbose=True)
    print("built ok")
